# version 13
# baseline (speedup 1.0000x reference)
import numpy as np
from contextlib import ExitStack

import concourse.bass as bass
import concourse.mybir as mybir
from concourse.bass_utils import run_bass_kernel_spmd

F32 = mybir.dt.float32
BF16 = mybir.dt.bfloat16
AF = mybir.ActivationFunctionType
ALU = mybir.AluOpType
AX = mybir.AxisListType

D = 1024
NCH = 8
TOWN = 2048
HEADS = 4
DH = 256
FF = 4096
NE = 8
EPS = 1e-5
SC = 0.25

ENGINES = ("pe", "dve", "act", "pool", "sp")
DMA_SEM = {"sp": "dsp", "act": "dact", "pool": "dpool"}

_W_ORDER = ["ident", "vec", "wk", "wv", "wq", "wo", "wd", "ff1cm", "ff2",
            "router", "w1cm", "w2cm", "sgu_gate", "sgu_up", "sdown"]
_W_SIZES = {
    "ident": 128 * 128, "vec": 128 * 96,
    "wk": D * D, "wv": D * D, "wq": D * D, "wo": D * D, "wd": D * D,
    "ff1cm": D * FF, "ff2": FF * D, "router": D * 8,
    "w1cm": NE * D * D, "w2cm": NE * D * D,
    "sgu_gate": D * D, "sgu_up": D * D, "sdown": D * D,
}
_W_OFF = {}
_o = 0
for _k in _W_ORDER:
    _W_OFF[_k] = _o
    _o += _W_SIZES[_k]
NBLOB = _o

VI = {"ln1_g": 0, "ln1_b": 8, "ln2_g": 16, "ln2_b": 24,
      "bo": 32, "b_d": 40, "b_ff2": 48, "sgw": 56, "b_ff1": 64}


class BD:

    def __init__(self):
        self.ev = {e: [] for e in ENGINES}
        self.buf = {}
        self.required = set()
        self.floor = []
        self.dma_count = {}
        self.dma_tok = {}
        self.dma_last = {}

    def op(self, eng, fn, deps=(), reads=(), writes=(), writes_ext=(), dma=None):
        dl = [d for d in deps if d is not None]
        dl.extend(self.floor)
        for k in reads:
            s = self.buf.get(k)
            if s and s["w"] is not None:
                dl.append(s["w"])
        for k in writes:
            s = self.buf.get(k)
            if s:
                if s["w"] is not None:
                    dl.append(s["w"])
                dl.extend(s["r"])
        seq = len(self.ev[eng])
        h = (eng, seq, dma is not None)
        if dma is not None:
            prev = self.dma_last.get(dma)
            if prev is not None:
                dl.append(prev)
            self.dma_count[dma] = self.dma_count.get(dma, 0) + 16
            self.dma_tok[(eng, seq)] = (dma, self.dma_count[dma])
            self.dma_last[dma] = h
        for d in set(dl):
            if (d[0], d[1]) != (eng, seq):
                self.required.add(d)
        self.ev[eng].append((fn, tuple(set(dl)), dma))
        for k in reads:
            self.buf.setdefault(k, {"w": None, "r": []})["r"].append(h)
        for k in writes:
            self.buf[k] = {"w": h, "r": []}
        for k in writes_ext:
            self.buf.setdefault(k, {"w": None, "r": []})["w"] = h
            self.buf[k]["r"] = []
        return h

    def pe(self, fn, **kw):
        return self.op("pe", fn, **kw)

    def dve(self, fn, **kw):
        return self.op("dve", fn, **kw)

    def act(self, fn, **kw):
        return self.op("act", fn, **kw)

    def pool(self, fn, **kw):
        return self.op("pool", fn, **kw)

    def sp(self, fn, **kw):
        return self.op("sp", fn, **kw)

    def drain(self, keys):
        deps = []
        for k in keys:
            s = self.buf.get(k)
            if s:
                if s["w"] is not None:
                    deps.append(s["w"])
                deps.extend(s["r"])
        return deps

    def add_floor(self, keys):
        deps = self.drain(keys)
        h = self.op("dve", lambda e: e.engine_nop(), deps=deps)
        self.floor = [h]

    def finalize(self):
        self.tokval = {}
        self.incflag = {}
        for eng in ENGINES:
            c_eng = 0
            for seq, (fn, deps, dma) in enumerate(self.ev[eng]):
                if dma is not None:
                    self.tokval[(eng, seq, True)] = self.dma_tok[(eng, seq)]
                    self.incflag[(eng, seq)] = dma
                elif (eng, seq, False) in self.required:
                    c_eng += 1
                    self.tokval[(eng, seq, False)] = (eng, c_eng)
                    self.incflag[(eng, seq)] = "eng"
                else:
                    self.incflag[(eng, seq)] = None

    def replay(self, eng, engine_obj, sems):
        wm = {}
        for seq, (fn, deps, dma) in enumerate(self.ev[eng]):
            for d in sorted(deps):
                if (d[0], d[1]) == (eng, seq):
                    continue
                sem_name, val = self.tokval.get(d, (None, None))
                if sem_name is None:
                    continue
                if wm.get(sem_name, 0) < val:
                    engine_obj.wait_ge(sems[sem_name], val)
                    wm[sem_name] = val
            ins = fn(engine_obj)
            kind = self.incflag[(eng, seq)]
            if kind == "eng":
                ins.then_inc(sems[eng], 1)
            elif kind is not None:
                ins.then_inc(sems[kind], 16)


def _pack_chunk_cols(v):
    return np.ascontiguousarray(np.asarray(v, dtype=np.float32).reshape(-1, 128).T)


def make_wblob(inp):
    vec_cols = [
        _pack_chunk_cols(inp["L0_attn_ln1_g"]), _pack_chunk_cols(inp["L0_attn_ln1_b"]),
        _pack_chunk_cols(inp["L0_attn_ln2_g"]), _pack_chunk_cols(inp["L0_attn_ln2_b"]),
        _pack_chunk_cols(inp["L0_attn_bo"]), _pack_chunk_cols(inp["L0_attn_b_d"]),
        _pack_chunk_cols(inp["L0_attn_b_ff2"]),
        _pack_chunk_cols(np.asarray(inp["L0_ffn_shared_gate_w"]).ravel()),
        _pack_chunk_cols(inp["L0_attn_b_ff1"]),
    ]
    ff1 = np.asarray(inp["L0_attn_w_ff1"], dtype=np.float32)
    w1 = np.asarray(inp["L0_ffn_w1"], dtype=np.float32)
    w2 = np.asarray(inp["L0_ffn_w2"], dtype=np.float32)
    sgu = np.asarray(inp["L0_ffn_shared_gate_up"], dtype=np.float32)
    pieces = {
        "ident": np.eye(128, dtype=np.float32),
        "vec": np.concatenate(vec_cols, axis=1),
        "wk": inp["L0_attn_wk"], "wv": inp["L0_attn_wv"],
        "wq": inp["L0_attn_wq"],
        "wo": np.asarray(inp["L0_attn_wo"], dtype=np.float32).reshape(8, 128, 8, 128).transpose(2, 1, 0, 3),
        "wd": inp["L0_attn_w_d"],
        "ff1cm": ff1.reshape(8, 128, 32, 128).transpose(2, 1, 0, 3),
        "ff2": inp["L0_attn_w_ff2"],
        "router": np.asarray(inp["L0_ffn_router_w"], dtype=np.float32).reshape(8, 128, 8).transpose(1, 0, 2),
        "w1cm": w1.reshape(NE, 8, 128, 8, 128).transpose(0, 3, 2, 1, 4),
        "w2cm": w2.reshape(NE, 8, 128, 8, 128).transpose(0, 3, 2, 1, 4),
        "sgu_gate": sgu[:, :D], "sgu_up": sgu[:, D:],
        "sdown": inp["L0_ffn_shared_down"],
    }
    blob = np.empty(NBLOB, dtype=np.float32)
    for k in _W_ORDER:
        a = np.ascontiguousarray(pieces[k], dtype=np.float32).ravel()
        assert a.size == _W_SIZES[k], (k, a.size, _W_SIZES[k])
        blob[_W_OFF[k]:_W_OFF[k] + a.size] = a
    return blob


def build_program(taps=()):
    nc = bass.Bass("TRN2", target_bir_lowering=False, debug=False)
    wblob = nc.dram_tensor("wblob", [NBLOB], F32, kind="ExternalInput").ap()
    xhbm = nc.dram_tensor("xb", [D, 2 * TOWN], F32, kind="ExternalInput").ap()
    out_h = nc.dram_tensor("out", [D, TOWN], F32, kind="ExternalOutput").ap()
    tap_aps = {}
    for tname, tshape in taps:
        tap_aps[tname] = nc.dram_tensor("tap_" + tname, list(tshape), F32,
                                        kind="ExternalOutput").ap()

    def wsl(name, elem_off, p, c):
        base = _W_OFF[name] + elem_off
        return wblob[base:base + p * c].rearrange("(p c) -> p c", c=c)

    GRAN = []
    GIDX = {}
    def _reg_gran(tag, wname, src_off):
        GIDX[tag] = len(GRAN)
        GRAN.append((wname, src_off))
    for _o in range(8):
        _reg_gran(f"wo:{_o}", "wo", _o * 8 * 128 * 128)
    for _o1 in range(32):
        _reg_gran(f"ff1:{_o1}", "ff1cm", _o1 * 8 * 128 * 128)
        _reg_gran(f"ff2:{_o1}", "ff2", _o1 * 128 * 1024)
    for _c in range(8):
        _reg_gran(f"wd:{_c}", "wd", _c * 128 * 1024)
    for _e in range(8):
        for _o1 in range(8):
            _reg_gran(f"w1:{_e}:{_o1}", "w1cm", _e * D * D + _o1 * 8 * 128 * 128)
    for _e in range(8):
        for _o1 in range(8):
            _reg_gran(f"w2:{_e}:{_o1}", "w2cm", _e * D * D + _o1 * 8 * 128 * 128)
    for _c in range(8):
        _reg_gran(f"sgug:{_c}", "sgu_gate", _c * 128 * 1024)
    for _c in range(8):
        _reg_gran(f"sguu:{_c}", "sgu_up", _c * 128 * 1024)
    for _c in range(8):
        _reg_gran(f"sdown:{_c}", "sdown", _c * 128 * 1024)
    NGRAN = len(GRAN)
    wpair = nc.dram_tensor("wpair", [NGRAN * 128 * 2048], BF16, kind="Internal").ap()

    bd = BD()
    eso = ExitStack()

    with eso:
        sems = {}
        for s in ("pe", "dve", "act", "pool", "sp", "dxp",
                  "dldc", "dldx", "dldk", "dldv", "dldq", "dldo",
                  "dg10", "dg11", "df20", "df21", "dwd0", "dwd1", "dwo0", "dwo1",
                  "dw10", "dw11", "dw20", "dw21", "dgu0", "dgu1",
                  "dsd0", "dsd1", "dsd2", "dsd3", "dpwi", "dpwo", "dout"):
            sems[s] = eso.enter_context(nc.semaphore(s))
        pb = [eso.enter_context(nc.psum_tensor(f"pb{i}", [128, 512], F32))
              for i in range(8)]

        def sbo(es, name, p, c, dt=F32):
            return es.enter_context(nc.sbuf_tensor(name, [p, c], dt))

        ident = sbo(eso, "ident", 128, 128)
        vec = sbo(eso, "vec", 128, 96)
        onescol = sbo(eso, "onescol", 128, 1)
        onesrow = sbo(eso, "onesrow", 1, 128)
        X = sbo(eso, "X", 128, NCH * TOWN)
        stat0 = sbo(eso, "stat0", 1, 1536)
        srow = sbo(eso, "srow", 1, 2048)
        router_sb = sbo(eso, "router_sb", 128, 64)
        ksum = sbo(eso, "ksum", 128, 8)
        rks = sbo(eso, "rks", 128, 8)
        ctx = sbo(eso, "ctx", 128, 2048)

        MU, RHO, R0 = 0, 512, 1024

        def vcol(name, c):
            return vec[:, VI[name] + c: VI[name] + c + 1]

        bd.op("sp", lambda e: e.dma_start(out=ident[:], in_=wsl("ident", 0, 128, 128)),
              writes=("ident",), dma="dldc")
        bd.op("sp", lambda e: e.dma_start(out=vec[:], in_=wsl("vec", 0, 128, 96)),
              writes=("vec",), dma="dldc")
        bd.op("sp", lambda e: e.dma_start(out=router_sb[:], in_=wsl("router", 0, 128, 64)),
              writes=("router_sb",), dma="dldc")
        for c in range(NCH):
            bd.op("sp", lambda e, c=c: e.dma_start(
                out=X[:, c * TOWN:(c + 1) * TOWN], in_=xhbm[c * 128:(c + 1) * 128, 0:TOWN]),
                writes=(f"X:{c}",), dma="dldx")
        bd.dve(lambda e: e.memset(onescol[:], 1.0), writes=("onescol",))
        bd.dve(lambda e: e.memset(onesrow[:], 1.0), writes=("onesrow",))
        bd.dve(lambda e: e.memset(ksum[:], 0.0), writes=("ksum",))

        def ln_stats(src_fn, src_keys, xsq, xsq_key, tsz, want_r0):
            for c in range(NCH):
                bd.act(lambda e, c=c, src_fn=src_fn, xsq=xsq, tsz=tsz: e.activation(
                    out=xsq[:, c * tsz:(c + 1) * tsz], in_=src_fn(c), func=AF.Square),
                    reads=(src_keys[c],), writes=(f"{xsq_key}:{c}",))
            for c in range(NCH):
                bd.pe(lambda e, c=c, src_fn=src_fn, tsz=tsz: e.matmul(
                    pb[0][0:1, 0:tsz], onescol[:], src_fn(c),
                    start=(c == 0), stop=(c == NCH - 1)),
                    reads=("onescol", src_keys[c]),
                    writes=(("pb0",) if c == 0 else ()),
                    writes_ext=(() if c == 0 else ("pb0",)))
            for c in range(NCH):
                bd.pe(lambda e, c=c, xsq=xsq, tsz=tsz: e.matmul(
                    pb[0][32:33, 0:tsz], onescol[:], xsq[:, c * tsz:(c + 1) * tsz],
                    start=(c == 0), stop=(c == NCH - 1)),
                    reads=("onescol", f"{xsq_key}:{c}"), writes_ext=("pb0",))
            mu = stat0[0:1, MU:MU + tsz]
            rho = stat0[0:1, RHO:RHO + tsz]
            sq = srow[0:1, 0:tsz]
            esq = srow[0:1, 256:256 + tsz]
            var = srow[0:1, 512:512 + tsz]
            av = srow[0:1, 768:768 + tsz]
            ra = srow[0:1, 1024:1024 + tsz]
            vr = srow[0:1, 1280:1280 + tsz]
            avr = srow[0:1, 1536:1536 + tsz]
            rar = srow[0:1, 1792:1792 + tsz]
            r1 = av
            r0ap = stat0[0:1, R0:R0 + tsz] if want_r0 else sq
            bd.dve(lambda e, mu=mu, tsz=tsz: e.tensor_scalar_mul(mu, pb[0][0:1, 0:tsz], 1.0 / D),
                   reads=("pb0",), writes=("s_mu",))
            bd.dve(lambda e, sq=sq, mu=mu: e.tensor_mul(sq, mu, mu),
                   reads=("s_mu",), writes=("s_sq",))
            bd.dve(lambda e, esq=esq, tsz=tsz: e.tensor_scalar_mul(
                esq, pb[0][32:33, 0:tsz], 1.0 / D),
                reads=("pb0",), writes=("s_esq",))
            bd.dve(lambda e, var=var, esq=esq, sq=sq: e.tensor_sub(var, esq, sq),
                   reads=("s_esq", "s_sq"), writes=("s_var",))
            if want_r0:
                bd.dve(lambda e, av=av, var=var: e.tensor_scalar_add(av, var, float(EPS)),
                       reads=("s_var",), writes=("s_a",))
                bd.dve(lambda e, ra=ra, av=av: e.reciprocal(ra, av),
                       reads=("s_a",), writes=("s_ra",))
                bd.act(lambda e, r0ap=r0ap, ra=ra: e.activation(
                    out=r0ap, in_=ra, func=AF.Sqrt),
                    reads=("s_ra", "s_sq"), writes=("s_r0",))
            bd.dve(lambda e, avr=avr, var=var: e.tensor_scalar(
                avr, var, float(1.0 + EPS), float(EPS * EPS), ALU.mult, ALU.add),
                reads=("s_var",), writes=("s_avr",))
            bd.dve(lambda e, rar=rar, avr=avr: e.reciprocal(rar, avr),
                   reads=("s_avr",), writes=("s_rar",))
            bd.act(lambda e, rho=rho, rar=rar: e.activation(
                out=rho, in_=rar, func=AF.Sqrt),
                reads=("s_rar",), writes=("s_rho",))

        def bcast(bank, colo, width, row_ap, row_key, start, stop):
            bd.pe(lambda e, bank=bank, colo=colo, width=width, row_ap=row_ap: e.matmul(
                pb[bank][:, colo:colo + width], onesrow[:], row_ap, start=start, stop=stop),
                reads=("onesrow", row_key),
                writes=((f"pb{bank}",) if start else ()),
                writes_ext=(() if start else (f"pb{bank}",)))

        def mm3(bank, cs, ce_, lh, ll, rh, rl, start, stop, reads, prow=128):
            key = f"pb{bank}"
            bd.pe(lambda e, lh=lh, rh=rh, bank=bank, cs=cs, ce_=ce_, start=start,
                  prow=prow: e.matmul(
                pb[bank][0:prow, cs:ce_], lh, rh, start=start, stop=False),
                reads=reads, writes=((key,) if start else ()),
                writes_ext=(() if start else (key,)))
            bd.pe(lambda e, lh=lh, rl=rl, bank=bank, cs=cs, ce_=ce_, prow=prow: e.matmul(
                pb[bank][0:prow, cs:ce_], lh, rl, start=False, stop=False),
                reads=reads, writes_ext=(key,))
            bd.pe(lambda e, ll=ll, rh=rh, bank=bank, cs=cs, ce_=ce_, stop=stop,
                  prow=prow: e.matmul(
                pb[bank][0:prow, cs:ce_], ll, rh, start=False, stop=stop),
                reads=reads, writes_ext=(key,))

        def split(src_ap, hi_ap, lo_ap, reads, wkey):
            bd.act(lambda e, src_ap=src_ap, hi_ap=hi_ap: e.activation(
                out=hi_ap, in_=src_ap, func=AF.Copy),
                reads=reads, writes=(wkey + "h",))
            bd.dve(lambda e, src_ap=src_ap, hi_ap=hi_ap, lo_ap=lo_ap: e.tensor_sub(
                lo_ap, src_ap, hi_ap),
                reads=reads + (wkey + "h",), writes=(wkey + "l",))

        def load_split(stream, wname, granule_off, cols, wtmp, wtmp_key, hi_t, lo_t,
                       dst_off, wkey, eng="sp", woff=0):
            wkeys = (wtmp_key,) if isinstance(wtmp_key, str) else tuple(wtmp_key)
            bd.op(eng, lambda e, granule_off=granule_off, cols=cols, woff=woff: e.dma_start(
                out=wtmp[:, woff:woff + cols], in_=wsl(wname, granule_off, 128, cols)),
                writes=wkeys, dma=stream)
            bd.act(lambda e, cols=cols, dst_off=dst_off, woff=woff: e.activation(
                out=hi_t[:, dst_off:dst_off + cols], in_=wtmp[:, woff:woff + cols],
                func=AF.Copy),
                reads=wkeys, writes=(wkey + "h",))
            bd.dve(lambda e, cols=cols, dst_off=dst_off, woff=woff: e.tensor_sub(
                lo_t[:, dst_off:dst_off + cols], wtmp[:, woff:woff + cols],
                hi_t[:, dst_off:dst_off + cols]),
                reads=wkeys + (wkey + "h",), writes=(wkey + "l",))

        def load_pair(stream, tag, slot, slot_key, eng="sp"):
            gi = GIDX[tag]
            base = gi * 128 * 2048
            src2 = wpair[base:base + 128 * 2048].rearrange("(h p c) -> p h c", h=2, c=1024)
            bd.op(eng, lambda e, src2=src2: e.dma_start(out=slot[:], in_=src2),
                  reads=(f"pw:{gi}",), writes=(slot_key,), dma=stream)

        def emit_pair_build(lo_g, hi_g, wtmps, pslots):
            for gi in range(lo_g, hi_g):
                wtmp, wtmp_key = wtmps[gi % 2]
                pslot, pslot_key = pslots[gi % 2]
                wname, src_off = GRAN[gi]
                base = gi * 128 * 2048
                hi_hbm = wpair[base:base + 128 * 1024].rearrange("(p c) -> p c", c=1024)
                lo_hbm = wpair[base + 128 * 1024:base + 128 * 2048].rearrange(
                    "(p c) -> p c", c=1024)
                bd.op("sp", lambda e, wname=wname, src_off=src_off, wtmp=wtmp: e.dma_start(
                    out=wtmp[:], in_=wsl(wname, src_off, 128, 1024)),
                    writes=(wtmp_key,), dma="dpwi")
                bd.act(lambda e, wtmp=wtmp, pslot=pslot: e.activation(
                    out=pslot[:, 0:1024], in_=wtmp[:], func=AF.Copy),
                    reads=(wtmp_key,), writes=(pslot_key + "h",))
                bd.dve(lambda e, wtmp=wtmp, pslot=pslot: e.tensor_sub(
                    pslot[:, 1024:2048], wtmp[:], pslot[:, 0:1024]),
                    reads=(wtmp_key, pslot_key + "h"), writes=(pslot_key + "l",))
                bd.op("pool", lambda e, hi_hbm=hi_hbm, pslot=pslot: e.dma_start(
                    out=hi_hbm, in_=pslot[:, 0:1024]),
                    reads=(pslot_key + "h",), writes=(f"pwh:{gi}",), dma="dpwo")
                bd.op("pool", lambda e, lo_hbm=lo_hbm, pslot=pslot: e.dma_start(
                    out=lo_hbm, in_=pslot[:, 1024:2048]),
                    reads=(pslot_key + "l",), writes=(f"pw:{gi}",), dma="dpwo")

        esa = ExitStack()
        with esa:
            wkh = sbo(esa, "wkh", 128, NCH * 1024, BF16)
            wkl = sbo(esa, "wkl", 128, NCH * 1024, BF16)
            wvh = sbo(esa, "wvh", 128, NCH * 1024, BF16)
            wvl = sbo(esa, "wvl", 128, NCH * 1024, BF16)
            wtmpA = sbo(esa, "wtmpA", 128, 1024)
            wtmpA2 = sbo(esa, "wtmpA2", 128, 1024)
            xp = sbo(esa, "xp", 128, NCH * 512)
            hAh = sbo(esa, "hAh", 128, NCH * 128, BF16)
            hAl = sbo(esa, "hAl", 128, NCH * 128, BF16)
            xsqA = sbo(esa, "xsqA", 128, NCH * 128)
            diffA = sbo(esa, "diffA", 128, NCH * 128)
            kexp = sbo(esa, "kexp", 128, 1024)
            keh = sbo(esa, "keh", 128, 1024, BF16)
            kel = sbo(esa, "kel", 128, 1024, BF16)
            vth = sbo(esa, "vth", 128, 1024, BF16)
            vtl = sbo(esa, "vtl", 128, 1024, BF16)
            onescol_bf = sbo(esa, "onescol_bf", 128, 1, BF16)
            pslotA = sbo(esa, "pslotA", 128, 2048, BF16)
            pslotB = sbo(esa, "pslotB", 128, 2048, BF16)
            bd.dve(lambda e: e.memset(onescol_bf[:], 1.0), writes=("onescol_bf",))

            for c in range(NCH):
                load_split("dldk", "wk", c * 128 * 1024, 1024, wtmpA, "wtmpA",
                           wkh, wkl, c * 1024, f"wk:{c}")
                load_split("dldv", "wv", c * 128 * 1024, 1024, wtmpA2, "wtmpA2",
                           wvh, wvl, c * 1024, f"wv:{c}")

            for g in range(8):
                own = g < 4
                if not own:
                    for c in range(NCH):
                        bd.op("sp", lambda e, c=c, g=g: e.dma_start(
                            out=xp[:, c * 512:(c + 1) * 512],
                            in_=xhbm[c * 128:(c + 1) * 128, g * 512:(g + 1) * 512]),
                            writes=(f"xp:{c}",), dma="dxp")
                for s4 in range(4):
                    si_ = g * 4 + s4

                    def src(c, s4=s4, g=g, own=own):
                        if own:
                            o = g * 512 + s4 * 128
                            return X[:, c * TOWN + o: c * TOWN + o + 128]
                        return xp[:, c * 512 + s4 * 128: c * 512 + s4 * 128 + 128]

                    skeys = tuple((f"X:{c}" if own else f"xp:{c}") for c in range(NCH))
                    ln_stats(src, skeys, xsqA, "xsqA", 128, want_r0=False)
                    bcast(1, 0, 128, stat0[0:1, MU:MU + 128], "s_mu", True, False)
                    bcast(1, 128, 128, stat0[0:1, RHO:RHO + 128], "s_rho", False, True)
                    for c in range(NCH):
                        bd.dve(lambda e, c=c, src=src: e.tensor_sub(
                            diffA[:, c * 128:(c + 1) * 128], src(c), pb[1][:, 0:128]),
                            reads=(skeys[c], "pb1"), writes=(f"diffA:{c}",))
                    for c in range(NCH):
                        bd.dve(lambda e, c=c: e.tensor_mul(
                            diffA[:, c * 128:(c + 1) * 128], diffA[:, c * 128:(c + 1) * 128],
                            pb[1][:, 128:256]),
                            reads=("pb1",), writes=(f"diffA:{c}",))
                    for c in range(NCH):
                        bd.dve(lambda e, c=c: e.tensor_scalar(
                            diffA[:, c * 128:(c + 1) * 128], diffA[:, c * 128:(c + 1) * 128],
                            vcol("ln1_g", c), vcol("ln1_b", c), ALU.mult, ALU.add),
                            reads=("vec",), writes=(f"diffA:{c}",))
                    for c in range(NCH):
                        split(diffA[:, c * 128:(c + 1) * 128], hAh[:, c * 128:(c + 1) * 128],
                              hAl[:, c * 128:(c + 1) * 128], (f"diffA:{c}",), f"hA:{c}")
                    for hc in range(2):
                        for c in range(NCH):
                            mm3(2 + hc, 0, 512,
                                hAh[:, c * 128:(c + 1) * 128], hAl[:, c * 128:(c + 1) * 128],
                                wkh[:, c * 1024 + hc * 512: c * 1024 + hc * 512 + 512],
                                wkl[:, c * 1024 + hc * 512: c * 1024 + hc * 512 + 512],
                                start=(c == 0), stop=(c == NCH - 1),
                                reads=(f"hA:{c}h", f"hA:{c}l", f"wk:{c}h", f"wk:{c}l"))
                    emit_pair_build(si_ * 8, min(si_ * 8 + 2, NGRAN),
                                    ((wtmpA, "wtmpA"), (wtmpA2, "wtmpA2")),
                                    ((pslotA, "psA"), (pslotB, "psB")))
                    for hc in range(2):
                        bd.act(lambda e, hc=hc: e.activation(
                            out=kexp[:, hc * 512:(hc + 1) * 512], in_=pb[2 + hc][:, 0:512],
                            func=AF.Exp, scale=float(SC)),
                            reads=(f"pb{2 + hc}",), writes=(f"kexp:{hc}",))
                    for hc in range(2):
                        split(kexp[:, hc * 512:(hc + 1) * 512],
                              keh[:, hc * 512:(hc + 1) * 512],
                              kel[:, hc * 512:(hc + 1) * 512],
                              (f"kexp:{hc}",), f"kexp:{hc}")
                    for hc in range(2):
                        for c in range(NCH):
                            mm3(2 + hc, 0, 512,
                                hAh[:, c * 128:(c + 1) * 128], hAl[:, c * 128:(c + 1) * 128],
                                wvh[:, c * 1024 + hc * 512: c * 1024 + hc * 512 + 512],
                                wvl[:, c * 1024 + hc * 512: c * 1024 + hc * 512 + 512],
                                start=(c == 0), stop=(c == NCH - 1),
                                reads=(f"hA:{c}h", f"hA:{c}l", f"wv:{c}h", f"wv:{c}l"))
                    emit_pair_build(si_ * 8 + 2, min(si_ * 8 + 4, NGRAN),
                                    ((wtmpA, "wtmpA"), (wtmpA2, "wtmpA2")),
                                    ((pslotA, "psA"), (pslotB, "psB")))
                    for hc in range(2):
                        split(pb[2 + hc][:, 0:512], vth[:, hc * 512:(hc + 1) * 512],
                              vtl[:, hc * 512:(hc + 1) * 512],
                              (f"pb{2 + hc}",), f"vtm:{hc}")
                    emit_pair_build(si_ * 8 + 4, min(si_ * 8 + 6, NGRAN),
                                    ((wtmpA, "wtmpA"), (wtmpA2, "wtmpA2")),
                                    ((pslotA, "psA"), (pslotB, "psB")))
                    for j in range(8):
                        head, ec = j // 2, j % 2
                        bank, colo = 4 + j // 2, (j % 2) * 256
                        first = (s4 == 0 and colo == 0)
                        last = (s4 == 3 and colo == 256)
                        ks = head * 256 + ec * 128
                        mm3(bank, colo, colo + 256,
                            keh[:, ks:ks + 128], kel[:, ks:ks + 128],
                            vth[:, head * 256: head * 256 + 256],
                            vtl[:, head * 256: head * 256 + 256],
                            start=first, stop=last,
                            reads=(f"kexp:{ks // 512}h", f"kexp:{ks // 512}l",
                                   f"vtm:{(head * 256) // 512}h",
                                   f"vtm:{(head * 256) // 512}l"))
                    for j in range(8):
                        head, ec = j // 2, j % 2
                        ks = head * 256 + ec * 128
                        bd.pe(lambda e, j=j, ks=ks: e.matmul(
                            pb[1][:, j:j + 1], keh[:, ks:ks + 128], onescol_bf[:],
                            start=(j == 0), stop=False),
                            reads=(f"kexp:{ks // 512}h", "onescol_bf"),
                            writes=(("pb1",) if j == 0 else ()),
                            writes_ext=(() if j == 0 else ("pb1",)))
                        bd.pe(lambda e, j=j, ks=ks: e.matmul(
                            pb[1][:, j:j + 1], kel[:, ks:ks + 128], onescol_bf[:],
                            start=False, stop=(j == 7)),
                            reads=(f"kexp:{ks // 512}l", "onescol_bf"),
                            writes_ext=("pb1",))
                    bd.dve(lambda e: e.tensor_add(ksum[:, 0:8], ksum[:, 0:8], pb[1][:, 0:8]),
                           reads=("pb1",), writes=("ksum",))
                    si = g * 4 + s4
                    emit_pair_build(si * 8 + 6, min(si * 8 + 8, NGRAN),
                                    ((wtmpA, "wtmpA"), (wtmpA2, "wtmpA2")),
                                    ((pslotA, "psA"), (pslotB, "psB")))
                for j in range(8):
                    bank, colo = 4 + j // 2, (j % 2) * 256
                    if g == 0:
                        bd.dve(lambda e, j=j, bank=bank, colo=colo: e.tensor_copy(
                            ctx[:, j * 256:(j + 1) * 256], pb[bank][:, colo:colo + 256]),
                            reads=(f"pb{bank}",), writes=(f"ctx:{j}",))
                    else:
                        bd.dve(lambda e, j=j, bank=bank, colo=colo: e.tensor_add(
                            ctx[:, j * 256:(j + 1) * 256], ctx[:, j * 256:(j + 1) * 256],
                            pb[bank][:, colo:colo + 256]),
                            reads=(f"pb{bank}",), writes=(f"ctx:{j}",))

            bd.dve(lambda e: e.reciprocal(rks[:], ksum[:]), reads=("ksum",), writes=("rks",))
            for j in range(8):
                bd.dve(lambda e, j=j: e.tensor_scalar_mul(
                    ctx[:, j * 256:(j + 1) * 256], ctx[:, j * 256:(j + 1) * 256],
                    rks[:, j:j + 1]),
                    reads=("rks",), writes=(f"ctx:{j}",))
            if "ctx" in tap_aps:
                bd.pool(lambda e: e.dma_start(out=tap_aps["ctx"], in_=ctx[:]),
                        reads=tuple(f"ctx:{j}" for j in range(8)), dma="dout")

            bd.add_floor(
                [f"wk:{c}{s}" for c in range(NCH) for s in ("h", "l")]
                + [f"wv:{c}{s}" for c in range(NCH) for s in ("h", "l")]
                + [f"xp:{c}" for c in range(NCH)]
                + [f"hA:{c}{s}" for c in range(NCH) for s in ("h", "l")]
                + [f"xsqA:{c}" for c in range(NCH)] + [f"diffA:{c}" for c in range(NCH)]
                + ["wtmpA", "wtmpA2", "onescol_bf", "psAh", "psAl", "psBh", "psBl"]
                + [f"kexp:{h}{s}" for h in range(2) for s in ("", "h", "l")]
                + [f"vtm:{h}{s}" for h in range(2) for s in ("h", "l")])

        esb = ExitStack()
        with esb:
            wqh = sbo(esb, "wqh", 128, NCH * 1024, BF16)
            wql = sbo(esb, "wql", 128, NCH * 1024, BF16)
            B1 = sbo(esb, "B1", 128, 2048)
            B2 = sbo(esb, "B2", 128, 2048)
            B3 = sbo(esb, "B3", 128, 2048)
            B4 = sbo(esb, "B4", 128, 2048)
            qt = sbo(esb, "qt", 128, 16)
            P1h = sbo(esb, "P1h", 128, 2048, BF16)
            P1l = sbo(esb, "P1l", 128, 2048, BF16)
            wop = [sbo(esb, f"wop{i}", 128, 2048, BF16) for i in range(2)]
            g1p = [sbo(esb, f"g1p{i}", 128, 2048, BF16) for i in range(2)]
            f2p = [sbo(esb, f"f2p{i}", 128, 2048, BF16) for i in range(2)]
            wdp = [sbo(esb, f"wdp{i}", 128, 2048, BF16) for i in range(2)]
            gch = sbo(esb, "gch", 128, 512, BF16)
            gcl = sbo(esb, "gcl", 128, 512, BF16)

            for c in range(NCH):
                load_split("dldq", "wq", c * 128 * 1024, 1024, B1,
                           ("B1:0", "B1:1", "B1:2", "B1:3"), wqh, wql,
                           c * 1024, f"wq:{c}")

            TB = 256

            for t in range(TOWN // TB):
                T0 = t * TB

                def xsrc(c, T0=T0):
                    return X[:, c * TOWN + T0: c * TOWN + T0 + TB]

                xkeys = tuple(f"X:{c}" for c in range(NCH))
                ln_stats(xsrc, xkeys, B4, "B4", TB, want_r0=True)
                bcast(1, 0, TB, stat0[0:1, MU:MU + TB], "s_mu", True, False)
                bcast(1, TB, TB, stat0[0:1, RHO:RHO + TB], "s_rho", False, True)
                bcast(0, 0, TB, stat0[0:1, R0:R0 + TB], "s_r0", True, True)
                for c in range(NCH):
                    bd.dve(lambda e, c=c, xsrc=xsrc: e.tensor_sub(
                        B1[:, c * TB:(c + 1) * TB], xsrc(c), pb[1][:, 0:TB]),
                        reads=(xkeys[c], "pb1"), writes=(f"B1:{c}",))
                for c in range(NCH):
                    bd.dve(lambda e, c=c: e.tensor_mul(
                        B2[:, c * TB:(c + 1) * TB], B1[:, c * TB:(c + 1) * TB],
                        pb[1][:, TB:2 * TB]),
                        reads=("pb1", f"B1:{c}"), writes=(f"B2:{c}",))
                for c in range(NCH):
                    bd.dve(lambda e, c=c: e.tensor_scalar(
                        B2[:, c * TB:(c + 1) * TB], B2[:, c * TB:(c + 1) * TB],
                        vcol("ln1_g", c), vcol("ln1_b", c), ALU.mult, ALU.add),
                        reads=("vec",), writes=(f"B2:{c}",))
                for c in range(NCH):
                    bd.dve(lambda e, c=c: e.tensor_mul(
                        B3[:, c * TB:(c + 1) * TB], B1[:, c * TB:(c + 1) * TB],
                        pb[0][:, 0:TB]),
                        reads=("pb0", f"B1:{c}"), writes=(f"B3:{c}",))
                b2keys = tuple(f"B2:{c}" for c in range(NCH))
                bd.act(lambda e: e.activation(out=P1h[:], in_=B2[:], func=AF.Copy),
                       reads=b2keys, writes=tuple(f"P1:{c}h" for c in range(NCH)))
                bd.dve(lambda e: e.tensor_sub(P1l[:], B2[:], P1h[:]),
                       reads=b2keys + tuple(f"P1:{c}h" for c in range(NCH)),
                       writes=tuple(f"P1:{c}l" for c in range(NCH)))
                for s2 in range(2):
                    for hc in range(2):
                        for c in range(NCH):
                            mm3(2 + hc, 0, 512,
                                P1h[:, c * TB + s2 * 128: c * TB + s2 * 128 + 128],
                                P1l[:, c * TB + s2 * 128: c * TB + s2 * 128 + 128],
                                wqh[:, c * 1024 + hc * 512: c * 1024 + hc * 512 + 512],
                                wql[:, c * 1024 + hc * 512: c * 1024 + hc * 512 + 512],
                                start=(c == 0), stop=(c == NCH - 1),
                                reads=(f"P1:{c}h", f"P1:{c}l", f"wq:{c}h", f"wq:{c}l"))
                    for head in range(HEADS):
                        bank = 2 + head // 2
                        colo = (head % 2) * 256
                        qcell = qt[:, head * 4: head * 4 + 1]
                        qbias = qt[:, head * 4 + 1: head * 4 + 2]
                        qsum = qt[:, head * 4 + 2: head * 4 + 3]
                        qrec = qt[:, head * 4 + 3: head * 4 + 4]
                        qekey = f"B1:{s2 * 4 + head}"
                        qe_ap = B1[:, s2 * 1024 + head * 256: s2 * 1024 + head * 256 + 256]
                        bd.act(lambda e, bank=bank, colo=colo, qe_ap=qe_ap:
                               e.activation(out=qe_ap, in_=pb[bank][:, colo:colo + 256],
                                            func=AF.Exp, scale=float(SC)),
                               reads=(f"pb{bank}",), writes=(qekey,))
                        bd.dve(lambda e, qsum=qsum, qe_ap=qe_ap: e.reduce_sum(
                            qsum, qe_ap, axis=AX.X),
                            reads=(qekey,), writes=(f"qt:{head}c",))
                        bd.dve(lambda e, qrec=qrec, qsum=qsum: e.reciprocal(qrec, qsum),
                               reads=(f"qt:{head}c",), writes=(f"qt:{head}d",))
                        bd.dve(lambda e, qe_ap=qe_ap, qrec=qrec: e.tensor_scalar_mul(
                            qe_ap, qe_ap, qrec),
                            reads=(f"qt:{head}d",), writes=(qekey,))
                    for c in range(NCH):
                        bank = 4 + c // 4
                        colo = (c % 4) * 128
                        bd.pe(lambda e, c=c, s2=s2, bank=bank, colo=colo: e.matmul(
                            pb[bank][:, colo:colo + 128],
                            B1[:, s2 * 1024 + c * 128: s2 * 1024 + c * 128 + 128],
                            ident[:], is_transpose=True,
                            start=(colo == 0), stop=(colo == 384)),
                            reads=(f"B1:{s2 * 4 + c // 2}", "ident"),
                            writes=((f"pb{bank}",) if colo == 0 else ()),
                            writes_ext=(() if colo == 0 else (f"pb{bank}",)))
                    for c in range(NCH):
                        bank = 4 + c // 4
                        colo = (c % 4) * 128
                        bd.dve(lambda e, c=c, s2=s2, bank=bank, colo=colo: e.tensor_copy(
                            B4[:, c * TB + s2 * 128: c * TB + s2 * 128 + 128],
                            pb[bank][:, colo:colo + 128]),
                            reads=(f"pb{bank}",), writes=(f"B4:{c}",))
                if t == 0 and "qsm" in tap_aps:
                    bd.pool(lambda e: e.dma_start(out=tap_aps["qsm"], in_=B4[:]),
                            reads=tuple(f"B4:{c}" for c in range(NCH)), dma="dout")
                if t == 0 and "hB" in tap_aps:
                    bd.pool(lambda e: e.dma_start(out=tap_aps["hB"], in_=B2[:]),
                            reads=tuple(f"B2:{c}" for c in range(NCH)), dma="dout")
                attn_banks = [6, 6, 7, 7, 2, 2, 3, 3]
                for fj in range(8):
                    head, fc = fj // 2, fj % 2
                    bank = attn_banks[fj]
                    colo = (fj % 2) * 256
                    for ei in range(2):
                        j2 = head * 2 + ei
                        first = (ei == 0 and colo == 0)
                        last = (ei == 1 and colo == 256)
                        bd.pe(lambda e, j2=j2, fc=fc, bank=bank, colo=colo,
                              first=first, last=last: e.matmul(
                                  pb[bank][:, colo:colo + 256],
                                  ctx[:, j2 * 256 + fc * 128: j2 * 256 + fc * 128 + 128],
                                  B4[:, j2 * 256: j2 * 256 + 256],
                                  start=first, stop=last),
                              reads=(f"ctx:{j2}", f"B4:{j2}"),
                              writes=((f"pb{bank}",) if first else ()),
                              writes_ext=(() if first else (f"pb{bank}",)))
                for fj in range(8):
                    bank = attn_banks[fj]
                    colo = (fj % 2) * 256
                    split(pb[bank][0:128, colo:colo + 256],
                          P1h[:, fj * 256:(fj + 1) * 256],
                          P1l[:, fj * 256:(fj + 1) * 256],
                          (f"pb{bank}",), f"P1:{fj}")
                for o in range(NCH):
                    par = o % 2
                    bank = 4 + par
                    load_pair(f"dwo{par}", f"wo:{o}", wop[par], f"wop:{par}")
                    for f in range(NCH):
                        mm3(bank, 0, 256,
                            wop[par][:, f * 128:(f + 1) * 128],
                            wop[par][:, 1024 + f * 128: 1024 + f * 128 + 128],
                            P1h[:, f * 256:(f + 1) * 256], P1l[:, f * 256:(f + 1) * 256],
                            start=(f == 0), stop=(f == NCH - 1),
                            reads=(f"wop:{par}", f"P1:{f}h", f"P1:{f}l"))
                    bd.dve(lambda e, o=o, bank=bank: e.scalar_tensor_tensor(
                        B2[:, o * TB:(o + 1) * TB], pb[bank][:, 0:256], vcol("bo", o),
                        B3[:, o * TB:(o + 1) * TB], ALU.add, ALU.add),
                        reads=(f"pb{bank}", "vec", f"B3:{o}"), writes=(f"B2:{o}",))
                if t == 0 and "attn" in tap_aps:
                    bd.pool(lambda e: e.dma_start(out=tap_aps["attn"], in_=B1[:]),
                            reads=tuple(f"B1:{c}" for c in range(NCH)), dma="dout")
                if t == 0 and "X1" in tap_aps:
                    bd.pool(lambda e: e.dma_start(out=tap_aps["X1"], in_=B2[:]),
                            reads=tuple(f"B2:{c}" for c in range(NCH)), dma="dout")
                def x1src(c):
                    return B2[:, c * TB:(c + 1) * TB]

                x1keys = tuple(f"B2:{c}" for c in range(NCH))
                ln_stats(x1src, x1keys, B4, "B4", TB, want_r0=True)
                bcast(1, 0, TB, stat0[0:1, MU:MU + TB], "s_mu", True, False)
                bcast(1, TB, TB, stat0[0:1, R0:R0 + TB], "s_r0", False, True)
                for c in range(NCH):
                    bd.dve(lambda e, c=c, x1src=x1src: e.tensor_sub(
                        B3[:, c * TB:(c + 1) * TB], x1src(c), pb[1][:, 0:TB]),
                        reads=(x1keys[c], "pb1"), writes=(f"B3:{c}",))
                for c in range(NCH):
                    bd.dve(lambda e, c=c: e.tensor_mul(
                        B3[:, c * TB:(c + 1) * TB], B3[:, c * TB:(c + 1) * TB],
                        pb[1][:, TB:2 * TB]),
                        reads=("pb1",), writes=(f"B3:{c}",))
                for c in range(NCH):
                    bd.dve(lambda e, c=c: e.tensor_scalar(
                        B3[:, c * TB:(c + 1) * TB], B3[:, c * TB:(c + 1) * TB],
                        vcol("ln2_g", c), vcol("ln2_b", c), ALU.mult, ALU.add),
                        reads=("vec",), writes=(f"B3:{c}",))
                if t == 0 and "h2" in tap_aps:
                    bd.pool(lambda e: e.dma_start(out=tap_aps["h2"], in_=B3[:]),
                            reads=tuple(f"B3:{c}" for c in range(NCH)), dma="dout")
                for c in range(NCH):
                    split(B3[:, c * TB:(c + 1) * TB], P1h[:, c * TB:(c + 1) * TB],
                          P1l[:, c * TB:(c + 1) * TB], (f"B3:{c}",), f"P1:{c}")
                ff2_banks = [2, 2, 3, 3, 6, 6, 7, 7]
                for o1 in range(33):
                    par = o1 % 2
                    bank = 4 + par
                    if o1 < 32:
                        load_pair(f"dg1{par}", f"ff1:{o1}", g1p[par], f"g1:{par}")
                        for c in range(NCH):
                            mm3(bank, 0, 256,
                                g1p[par][:, c * 128:(c + 1) * 128],
                                g1p[par][:, 1024 + c * 128: 1024 + c * 128 + 128],
                                P1h[:, c * TB:(c + 1) * TB], P1l[:, c * TB:(c + 1) * TB],
                                start=(c == 0), stop=(c == NCH - 1),
                                reads=(f"g1:{par}", f"P1:{c}h", f"P1:{c}l"))
                        bd.act(lambda e, o1=o1, par=par, bank=bank: e.activation(
                            out=B1[:, par * 256: par * 256 + 256], in_=pb[bank][:, 0:256],
                            func=AF.Gelu_apprx_tanh,
                            bias=vec[:, VI["b_ff1"] + o1: VI["b_ff1"] + o1 + 1]),
                            reads=(f"pb{bank}", "vec"), writes=(f"B1:{par}",))
                        split(B1[:, par * 256: par * 256 + 256],
                              gch[:, par * 256: par * 256 + 256],
                              gcl[:, par * 256: par * 256 + 256],
                              (f"B1:{par}",), f"gc:{par}")
                        load_pair(f"df2{par}", f"ff2:{o1}", f2p[par], f"f2:{par}")
                    if o1 > 0:
                        o1p = o1 - 1
                        parp = o1p % 2
                        for o2 in range(NCH):
                            bank2 = ff2_banks[o2]
                            colo2 = (o2 % 2) * 256
                            first = (o1p == 0 and colo2 == 0)
                            last = (o1p == 31 and colo2 == 256)
                            mm3(bank2, colo2, colo2 + 256,
                                f2p[parp][:, o2 * 128:(o2 + 1) * 128],
                                f2p[parp][:, 1024 + o2 * 128: 1024 + o2 * 128 + 128],
                                gch[:, parp * 256: parp * 256 + 256],
                                gcl[:, parp * 256: parp * 256 + 256],
                                start=first, stop=last,
                                reads=(f"f2:{parp}", f"gc:{parp}h", f"gc:{parp}l"))
                for o2 in range(NCH):
                    bank2 = ff2_banks[o2]
                    colo2 = (o2 % 2) * 256
                    bd.dve(lambda e, o2=o2, bank2=bank2, colo2=colo2: e.scalar_tensor_tensor(
                        B4[:, o2 * TB:(o2 + 1) * TB], pb[bank2][:, colo2:colo2 + 256],
                        vcol("b_ff2", o2), B2[:, o2 * TB:(o2 + 1) * TB], ALU.add, ALU.add),
                        reads=(f"pb{bank2}", "vec", f"B2:{o2}"), writes=(f"B4:{o2}",))
                if t == 0 and "X2" in tap_aps:
                    bd.pool(lambda e: e.dma_start(out=tap_aps["X2"], in_=B4[:]),
                            reads=tuple(f"B4:{c}" for c in range(NCH)), dma="dout")
                for c in range(NCH):
                    split(B4[:, c * TB:(c + 1) * TB], P1h[:, c * TB:(c + 1) * TB],
                          P1l[:, c * TB:(c + 1) * TB], (f"B4:{c}",), f"P1:{c}")
                wd_banks = [0, 0, 1, 1, 4, 4, 5, 5]
                for c in range(NCH):
                    par = c % 2
                    load_pair(f"dwd{par}", f"wd:{c}", wdp[par], f"wdp:{par}")
                    for o in range(NCH):
                        bank = wd_banks[o]
                        colo = (o % 2) * 256
                        first = (c == 0 and colo == 0)
                        last = (c == NCH - 1 and colo == 256)
                        mm3(bank, colo, colo + 256,
                            wdp[par][:, o * 128:(o + 1) * 128],
                            wdp[par][:, 1024 + o * 128: 1024 + o * 128 + 128],
                            P1h[:, c * TB:(c + 1) * TB], P1l[:, c * TB:(c + 1) * TB],
                            start=first, stop=last,
                            reads=(f"wdp:{par}", f"P1:{c}h", f"P1:{c}l"))
                for o in range(NCH):
                    bank = wd_banks[o]
                    colo = (o % 2) * 256
                    bd.dve(lambda e, o=o, bank=bank, colo=colo, T0=T0: e.scalar_tensor_tensor(
                        X[:, o * TOWN + T0: o * TOWN + T0 + TB],
                        pb[bank][:, colo:colo + 256], vcol("b_d", o),
                        X[:, o * TOWN + T0: o * TOWN + T0 + TB], ALU.add, ALU.add),
                        reads=(f"pb{bank}", "vec"), writes=(f"X:{o}",))

            if "Y" in tap_aps:
                for c in range(NCH):
                    bd.pool(lambda e, c=c: e.dma_start(
                        out=tap_aps["Y"][c * 128:(c + 1) * 128, :],
                        in_=X[:, c * TOWN:(c + 1) * TOWN]),
                        reads=(f"X:{c}",), dma="dout")

            bd.add_floor(
                [f"wq:{c}{s}" for c in range(NCH) for s in ("h", "l")] + [f"wo:{c}" for c in range(NCH)]
                + [f"B1:{c}" for c in range(NCH)] + [f"B2:{c}" for c in range(NCH)]
                + [f"B3:{c}" for c in range(NCH)] + [f"B4:{c}" for c in range(NCH)]
                + [f"P1:{c}{s}" for c in range(NCH) for s in ("h", "l")]
                + [f"g1:{p}" for p in range(2)] + [f"f2:{p}" for p in range(2)]
                + ["wdp:0", "wdp:1", "wop:0", "wop:1"]
                + [f"gc:{p}{s}" for p in range(2) for s in ("h", "l")]
                + [f"qt:{h}{x}" for h in range(HEADS) for x in "abcd"])

        esd = ExitStack()
        with esd:
            h3h = sbo(esd, "h3h", 128, NCH * 512, BF16)
            h3l = sbo(esd, "h3l", 128, NCH * 512, BF16)
            routed = sbo(esd, "routed", 128, NCH * 512)
            comb = sbo(esd, "comb", 128, 32)
            prb = sbo(esd, "prb", 128, 8)
            mtmp = sbo(esd, "mtmp", 128, 8)
            rtmp = sbo(esd, "rtmp", 128, 8)
            routerh = sbo(esd, "routerh", 128, 64, BF16)
            routerl = sbo(esd, "routerl", 128, 64, BF16)
            sgwh = sbo(esd, "sgwh", 128, 8, BF16)
            sgwl = sbo(esd, "sgwl", 128, 8, BF16)
            ce_row = sbo(esd, "ce_row", 1, 512)
            s2row = ce_row
            s2b = sbo(esd, "s2b", 128, 512)
            ftmpD = sbo(esd, "ftmpD", 128, 512)
            ftmpD2 = sbo(esd, "ftmpD2", 128, 512)
            ftmps = [ftmpD, ftmpD2]
            fin = ftmpD
            heh = sbo(esd, "heh", 128, NCH * 512, BF16)
            hel = sbo(esd, "hel", 128, NCH * 512, BF16)
            w1p = [sbo(esd, f"w1p{i}", 128, 2048, BF16) for i in range(2)]
            w2p = [sbo(esd, f"w2p{i}", 128, 2048, BF16) for i in range(2)]
            sg = sbo(esd, "sg", 128, NCH * 512)
            mdh = sbo(esd, "mdh", 128, NCH * 512, BF16)
            mdl = sbo(esd, "mdl", 128, NCH * 512, BF16)
            gup = [sbo(esd, f"gup{i}", 128, 2048, BF16) for i in range(2)]
            split(router_sb[:], routerh[:], routerl[:], ("router_sb",), "router")
            split(vec[:, VI["sgw"]:VI["sgw"] + 8], sgwh[:], sgwl[:], ("vec",), "sgw")

            for hf in range(4):
                H0 = hf * 512
                bd.add_floor([f"routed:{o}" for o in range(NCH)]
                             + [f"xsqD:{c}" for c in range(NCH)]
                             + [f"d3:{c}" for c in range(NCH)])
                for tt in range(2):
                    T0 = H0 + tt * 256

                    def ysrc(c, T0=T0):
                        return X[:, c * TOWN + T0: c * TOWN + T0 + 256]

                    ykeys = tuple(f"X:{c}" for c in range(NCH))
                    ln_stats(ysrc, ykeys, routed, "xsqD", 256, want_r0=True)
                    bcast(1, 0, 256, stat0[0:1, MU:MU + 256], "s_mu", True, False)
                    bcast(1, 256, 256, stat0[0:1, R0:R0 + 256], "s_r0", False, True)
                    for c in range(NCH):
                        bd.dve(lambda e, c=c, tt=tt, ysrc=ysrc: e.tensor_sub(
                            routed[:, 2048 + c * 256: 2048 + c * 256 + 256],
                            ysrc(c), pb[1][:, 0:256]),
                            reads=(ykeys[c], "pb1"), writes=(f"d3:{c}",))
                    for c in range(NCH):
                        bd.dve(lambda e, c=c: e.tensor_mul(
                            routed[:, 2048 + c * 256: 2048 + c * 256 + 256],
                            routed[:, 2048 + c * 256: 2048 + c * 256 + 256],
                            pb[1][:, 256:512]),
                            reads=("pb1",), writes=(f"d3:{c}",))
                    for c in range(NCH):
                        split(routed[:, 2048 + c * 256: 2048 + c * 256 + 256],
                              h3h[:, c * 512 + tt * 256: c * 512 + tt * 256 + 256],
                              h3l[:, c * 512 + tt * 256: c * 512 + tt * 256 + 256],
                              (f"d3:{c}",), f"h3:{c}")
                for sub in range(4):
                    for c in range(NCH):
                        mm3(2, 0, 8,
                            h3h[:, c * 512 + sub * 128: c * 512 + sub * 128 + 128],
                            h3l[:, c * 512 + sub * 128: c * 512 + sub * 128 + 128],
                            routerh[:, c * 8:(c + 1) * 8], routerl[:, c * 8:(c + 1) * 8],
                            start=(c == 0), stop=(c == NCH - 1),
                            reads=(f"h3:{c}h", f"h3:{c}l", "routerh", "routerl"))
                    bd.dve(lambda e: e.reduce_max(rtmp[:, 0:1], pb[2][:, 0:8], axis=AX.X),
                           reads=("pb2",), writes=("rt0",))
                    bd.dve(lambda e: e.tensor_scalar_mul(rtmp[:, 1:2], rtmp[:, 0:1], -1.0),
                           reads=("rt0",), writes=("rt1",))
                    bd.act(lambda e: e.activation(out=prb[:], in_=pb[2][:, 0:8],
                                                  func=AF.Exp, bias=rtmp[:, 1:2]),
                           reads=("pb2", "rt1"), writes=("prb",))
                    bd.dve(lambda e: e.reduce_sum(rtmp[:, 2:3], prb[:], axis=AX.X),
                           reads=("prb",), writes=("rt2",))
                    bd.dve(lambda e: e.reciprocal(rtmp[:, 3:4], rtmp[:, 2:3]),
                           reads=("rt2",), writes=("rt3",))
                    bd.dve(lambda e: e.tensor_scalar_mul(prb[:], prb[:], rtmp[:, 3:4]),
                           reads=("rt3",), writes=("prb",))
                    bd.dve(lambda e: e.reduce_max(rtmp[:, 4:5], prb[:], axis=AX.X),
                           reads=("prb",), writes=("rt4",))
                    bd.dve(lambda e: e.tensor_single_scalar(
                        mtmp[:], prb[:], rtmp[:, 4:5], ALU.is_ge),
                        reads=("prb", "rt4"), writes=("mtmp",))
                    bd.dve(lambda e: e.scalar_tensor_tensor(
                        mtmp[:], mtmp[:], -2.0, prb[:], ALU.mult, ALU.add),
                        reads=("prb",), writes=("mtmp",))
                    bd.dve(lambda e: e.reduce_max(rtmp[:, 5:6], mtmp[:], axis=AX.X),
                           reads=("mtmp",), writes=("rt5",))
                    bd.dve(lambda e, sub=sub: e.tensor_single_scalar(
                        comb[:, sub * 8:(sub + 1) * 8], prb[:], rtmp[:, 5:6], ALU.is_ge),
                        reads=("prb", "rt5"), writes=(f"comb:{sub}",))
                    bd.dve(lambda e, sub=sub: e.tensor_mul(
                        comb[:, sub * 8:(sub + 1) * 8], comb[:, sub * 8:(sub + 1) * 8],
                        prb[:]),
                        reads=("prb",), writes=(f"comb:{sub}",))
                xsq_drain = bd.drain([f"xsqD:{c}" for c in range(NCH)]
                                     + [f"d3:{c}" for c in range(NCH)])
                if True:
                    for en in range(NE):
                        for sub in range(4):
                            colo = sub * 128
                            bd.pe(lambda e, en=en, sub=sub, colo=colo: e.matmul(
                                pb[3][0:1, colo:colo + 128],
                                comb[:, sub * 8 + en: sub * 8 + en + 1], ident[:],
                                is_transpose=True,
                                start=(colo == 0), stop=(colo == 384)),
                                reads=(f"comb:{sub}", "ident"),
                                writes=(("pb3",) if colo == 0 else ()),
                                writes_ext=(() if colo == 0 else ("pb3",)))
                        bd.act(lambda e: e.activation(
                            out=ce_row[0:1, 0:512], in_=pb[3][0:1, 0:512], func=AF.Copy),
                            reads=("pb3",), writes=("ce_row",))
                        bd.pe(lambda e: e.matmul(
                            pb[6][:, 0:512], onesrow[:], ce_row[0:1, 0:512],
                            start=True, stop=True),
                            reads=("onesrow", "ce_row"), writes=("pb6",))
                        for o1 in range(NCH):
                            par = o1 % 2
                            load_pair(f"dw1{par}", f"w1:{en}:{o1}", w1p[par], f"w1p:{par}")
                            for c in range(NCH):
                                mm3(4 + par, 0, 512,
                                    w1p[par][:, c * 128:(c + 1) * 128],
                                    w1p[par][:, 1024 + c * 128: 1024 + c * 128 + 128],
                                    h3h[:, c * 512:(c + 1) * 512],
                                    h3l[:, c * 512:(c + 1) * 512],
                                    start=(c == 0), stop=(c == NCH - 1),
                                    reads=(f"w1p:{par}", f"h3:{c}h", f"h3:{c}l"))
                            ft = ftmps[par]
                            ftk = "ftmpD" if par == 0 else "ftmpD2"
                            bd.act(lambda e, par=par, ft=ft: e.activation(
                                out=ft[:], in_=pb[4 + par][:, 0:512], func=AF.Silu),
                                reads=(f"pb{4 + par}",), writes=(ftk,))
                            bd.dve(lambda e, ft=ft: e.tensor_mul(ft[:], ft[:],
                                                                 pb[6][:, 0:512]),
                                   reads=("pb6",), writes=(ftk,))
                            split(ft[:], heh[:, o1 * 512:(o1 + 1) * 512],
                                  hel[:, o1 * 512:(o1 + 1) * 512], (ftk,), f"he:{o1}")
                        for o2 in range(NCH):
                            par = o2 % 2
                            load_pair(f"dw2{par}", f"w2:{en}:{o2}", w2p[par], f"w2p:{par}")
                            for c in range(NCH):
                                mm3(2 + par, 0, 512,
                                    w2p[par][:, c * 128:(c + 1) * 128],
                                    w2p[par][:, 1024 + c * 128: 1024 + c * 128 + 128],
                                    heh[:, c * 512:(c + 1) * 512],
                                    hel[:, c * 512:(c + 1) * 512],
                                    start=(c == 0), stop=(c == NCH - 1),
                                    reads=(f"w2p:{par}", f"he:{c}h", f"he:{c}l"))
                            if en == 0:
                                bd.dve(lambda e, o2=o2, par=par: e.tensor_copy(
                                    routed[:, o2 * 512:(o2 + 1) * 512],
                                    pb[2 + par][:, 0:512]),
                                    deps=xsq_drain,
                                    reads=(f"pb{2 + par}",), writes=(f"routed:{o2}",))
                            else:
                                bd.dve(lambda e, o2=o2, par=par: e.tensor_add(
                                    routed[:, o2 * 512:(o2 + 1) * 512],
                                    routed[:, o2 * 512:(o2 + 1) * 512],
                                    pb[2 + par][:, 0:512]),
                                    reads=(f"pb{2 + par}",), writes=(f"routed:{o2}",))
                if True:
                    for c in range(NCH):
                        par = c % 2
                        load_pair(f"dgu{par}", f"sgug:{c}", gup[par], f"gup:{par}")
                        for og in range(NCH):
                            mm3(og, 0, 512,
                                gup[par][:, og * 128:(og + 1) * 128],
                                gup[par][:, 1024 + og * 128: 1024 + og * 128 + 128],
                                h3h[:, c * 512:(c + 1) * 512],
                                h3l[:, c * 512:(c + 1) * 512],
                                start=(c == 0), stop=(c == NCH - 1),
                                reads=(f"gup:{par}", f"h3:{c}h", f"h3:{c}l"))
                    for og in range(NCH):
                        bd.act(lambda e, og=og: e.activation(
                            out=sg[:, og * 512:(og + 1) * 512], in_=pb[og][:, 0:512],
                            func=AF.Silu),
                            reads=(f"pb{og}",), writes=(f"sg:{og}",))
                    for c in range(NCH):
                        par = c % 2
                        load_pair(f"dgu{par}", f"sguu:{c}", gup[par], f"gup:{par}")
                        for og in range(NCH):
                            mm3(og, 0, 512,
                                gup[par][:, og * 128:(og + 1) * 128],
                                gup[par][:, 1024 + og * 128: 1024 + og * 128 + 128],
                                h3h[:, c * 512:(c + 1) * 512],
                                h3l[:, c * 512:(c + 1) * 512],
                                start=(c == 0), stop=(c == NCH - 1),
                                reads=(f"gup:{par}", f"h3:{c}h", f"h3:{c}l"))
                    for og in range(NCH):
                        bd.dve(lambda e, og=og: e.tensor_mul(
                            sg[:, og * 512:(og + 1) * 512], sg[:, og * 512:(og + 1) * 512],
                            pb[og][:, 0:512]),
                            reads=(f"pb{og}",), writes=(f"sg:{og}",))
                    for og in range(NCH):
                        split(sg[:, og * 512:(og + 1) * 512],
                              mdh[:, og * 512:(og + 1) * 512],
                              mdl[:, og * 512:(og + 1) * 512], (f"sg:{og}",), f"md:{og}")
                    for c in range(NCH):
                        mm3(0, 0, 512,
                            sgwh[:, c:c + 1], sgwl[:, c:c + 1],
                            h3h[:, c * 512:(c + 1) * 512], h3l[:, c * 512:(c + 1) * 512],
                            start=(c == 0), stop=(c == NCH - 1),
                            reads=("sgwh", "sgwl", f"h3:{c}h", f"h3:{c}l"), prow=1)
                    bd.act(lambda e: e.activation(out=s2row[0:1, 0:512],
                                                  in_=pb[0][0:1, 0:512], func=AF.Sigmoid),
                           reads=("pb0",), writes=("ce_row",))
                    bd.pe(lambda e: e.matmul(pb[1][:, 0:512], onesrow[:],
                                             s2row[0:1, 0:512], start=True, stop=True),
                          reads=("onesrow", "ce_row"), writes=("pb1",))
                    bd.dve(lambda e: e.tensor_copy(s2b[:], pb[1][:, 0:512]),
                           reads=("pb1",), writes=("s2b",))
                    for og in range(NCH):
                        par = og % 2
                        load_pair(f"dsd{par}", f"sdown:{og}", gup[par], f"gup:{par}")
                        for o in range(NCH):
                            mm3(o, 0, 512,
                                gup[par][:, o * 128:(o + 1) * 128],
                                gup[par][:, 1024 + o * 128: 1024 + o * 128 + 128],
                                mdh[:, og * 512:(og + 1) * 512],
                                mdl[:, og * 512:(og + 1) * 512],
                                start=(og == 0), stop=(og == NCH - 1),
                                reads=(f"gup:{par}", f"md:{og}h", f"md:{og}l"))
                    for o in range(NCH):
                        bd.dve(lambda e, o=o: e.tensor_mul(
                            fin[:], pb[o][:, 0:512], s2b[:]),
                            reads=(f"pb{o}", "s2b"), writes=("ftmpD",))
                        bd.dve(lambda e, o=o: e.tensor_add(
                            fin[:], fin[:], routed[:, o * 512:(o + 1) * 512]),
                            reads=(f"routed:{o}",), writes=("ftmpD",))
                        bd.dve(lambda e, o=o, H0=H0: e.tensor_add(
                            fin[:], fin[:], X[:, o * TOWN + H0: o * TOWN + H0 + 512]),
                            reads=(f"X:{o}",), writes=("ftmpD",))
                        bd.pool(lambda e, o=o, H0=H0: e.dma_start(
                            out=out_h[o * 128:(o + 1) * 128, H0: H0 + 512],
                            in_=fin[:]),
                            reads=("ftmpD",), dma="dout")

        bd.op("pool", lambda e: e.wait_ge(sems["dout"], bd.dma_count.get("dout", 0)))

        bd.finalize()
        with nc.Block() as blk:
            @blk.tensor
            def _(eng):
                bd.replay("pe", eng, sems)

            @blk.vector
            def _(eng):
                bd.replay("dve", eng, sems)

            @blk.scalar
            def _(eng):
                bd.replay("act", eng, sems)

            @blk.gpsimd
            def _(eng):
                bd.replay("pool", eng, sems)

            @blk.sync
            def _(eng):
                bd.replay("sp", eng, sems)

    return nc


def kernel(**inputs):
    x = np.asarray(inputs["L0_attn_x"], dtype=np.float32)
    blob = make_wblob(inputs)
    nc = build_program()
    in_maps = []
    for core in range(8):
        b, half = core // 2, core % 2
        xt = np.ascontiguousarray(x[b].T)
        if half == 1:
            xt = np.ascontiguousarray(np.concatenate([xt[:, TOWN:], xt[:, :TOWN]], axis=1))
        in_maps.append({"wblob": blob, "xb": xt})
    res = run_bass_kernel_spmd(nc, in_maps, core_ids=list(range(8)))
    out = np.empty((4, 2 * TOWN, D), dtype=np.float32)
    for core in range(8):
        b, half = core // 2, core % 2
        out[b, half * TOWN:(half + 1) * TOWN, :] = res.results[core]["out"].T
    return out


# version 14
# speedup vs baseline: 1.0313x; 1.0313x over previous
import numpy as np
from contextlib import ExitStack

import concourse.bass as bass
import concourse.mybir as mybir
from concourse.bass_utils import run_bass_kernel_spmd

F32 = mybir.dt.float32
BF16 = mybir.dt.bfloat16
AF = mybir.ActivationFunctionType
ALU = mybir.AluOpType
AX = mybir.AxisListType

D = 1024
NCH = 8
TOWN = 2048
HEADS = 4
DH = 256
FF = 4096
NE = 8
EPS = 1e-5
SC = 0.25

ENGINES = ("pe", "dve", "act", "pool", "sp")
DMA_SEM = {"sp": "dsp", "act": "dact", "pool": "dpool"}

_W_ORDER = ["ident", "vec", "wk", "wv", "wq", "wo", "wd", "ff1cm", "ff2",
            "router", "w1cm", "w2cm", "sgu_gate", "sgu_up", "sdown"]
_W_SIZES = {
    "ident": 128 * 128, "vec": 128 * 96,
    "wk": D * D, "wv": D * D, "wq": D * D, "wo": D * D, "wd": D * D,
    "ff1cm": D * FF, "ff2": FF * D, "router": D * 8,
    "w1cm": NE * D * D, "w2cm": NE * D * D,
    "sgu_gate": D * D, "sgu_up": D * D, "sdown": D * D,
}
_W_OFF = {}
_o = 0
for _k in _W_ORDER:
    _W_OFF[_k] = _o
    _o += _W_SIZES[_k]
NBLOB = _o

VI = {"ln1_g": 0, "ln1_b": 8, "ln2_g": 16, "ln2_b": 24,
      "bo": 32, "b_d": 40, "b_ff2": 48, "sgw": 56, "b_ff1": 64}


class BD:

    def __init__(self):
        self.ev = {e: [] for e in ENGINES}
        self.buf = {}
        self.required = set()
        self.floor = []
        self.dma_count = {}
        self.dma_tok = {}
        self.dma_last = {}

    def op(self, eng, fn, deps=(), reads=(), writes=(), writes_ext=(), dma=None):
        dl = [d for d in deps if d is not None]
        dl.extend(self.floor)
        for k in reads:
            s = self.buf.get(k)
            if s and s["w"] is not None:
                dl.append(s["w"])
        for k in writes:
            s = self.buf.get(k)
            if s:
                if s["w"] is not None:
                    dl.append(s["w"])
                dl.extend(s["r"])
        seq = len(self.ev[eng])
        h = (eng, seq, dma is not None)
        if dma is not None:
            prev = self.dma_last.get(dma)
            if prev is not None:
                dl.append(prev)
            self.dma_count[dma] = self.dma_count.get(dma, 0) + 16
            self.dma_tok[(eng, seq)] = (dma, self.dma_count[dma])
            self.dma_last[dma] = h
        for d in set(dl):
            if (d[0], d[1]) != (eng, seq):
                self.required.add(d)
        self.ev[eng].append((fn, tuple(set(dl)), dma))
        for k in reads:
            self.buf.setdefault(k, {"w": None, "r": []})["r"].append(h)
        for k in writes:
            self.buf[k] = {"w": h, "r": []}
        for k in writes_ext:
            self.buf.setdefault(k, {"w": None, "r": []})["w"] = h
            self.buf[k]["r"] = []
        return h

    def pe(self, fn, **kw):
        return self.op("pe", fn, **kw)

    def dve(self, fn, **kw):
        return self.op("dve", fn, **kw)

    def act(self, fn, **kw):
        return self.op("act", fn, **kw)

    def pool(self, fn, **kw):
        return self.op("pool", fn, **kw)

    def sp(self, fn, **kw):
        return self.op("sp", fn, **kw)

    def drain(self, keys):
        deps = []
        for k in keys:
            s = self.buf.get(k)
            if s:
                if s["w"] is not None:
                    deps.append(s["w"])
                deps.extend(s["r"])
        return deps

    def add_floor(self, keys):
        deps = self.drain(keys)
        h = self.op("dve", lambda e: e.engine_nop(), deps=deps)
        self.floor = [h]

    def finalize(self):
        self.tokval = {}
        self.incflag = {}
        for eng in ENGINES:
            c_eng = 0
            for seq, (fn, deps, dma) in enumerate(self.ev[eng]):
                if dma is not None:
                    self.tokval[(eng, seq, True)] = self.dma_tok[(eng, seq)]
                    self.incflag[(eng, seq)] = dma
                elif (eng, seq, False) in self.required:
                    c_eng += 1
                    self.tokval[(eng, seq, False)] = (eng, c_eng)
                    self.incflag[(eng, seq)] = "eng"
                else:
                    self.incflag[(eng, seq)] = None

    def replay(self, eng, engine_obj, sems):
        wm = {}
        for seq, (fn, deps, dma) in enumerate(self.ev[eng]):
            for d in sorted(deps):
                if (d[0], d[1]) == (eng, seq):
                    continue
                sem_name, val = self.tokval.get(d, (None, None))
                if sem_name is None:
                    continue
                if wm.get(sem_name, 0) < val:
                    engine_obj.wait_ge(sems[sem_name], val)
                    wm[sem_name] = val
            ins = fn(engine_obj)
            kind = self.incflag[(eng, seq)]
            if kind == "eng":
                ins.then_inc(sems[eng], 1)
            elif kind is not None:
                ins.then_inc(sems[kind], 16)


def _pack_chunk_cols(v):
    return np.ascontiguousarray(np.asarray(v, dtype=np.float32).reshape(-1, 128).T)


def make_wblob(inp):
    vec_cols = [
        _pack_chunk_cols(inp["L0_attn_ln1_g"]), _pack_chunk_cols(inp["L0_attn_ln1_b"]),
        _pack_chunk_cols(inp["L0_attn_ln2_g"]), _pack_chunk_cols(inp["L0_attn_ln2_b"]),
        _pack_chunk_cols(inp["L0_attn_bo"]), _pack_chunk_cols(inp["L0_attn_b_d"]),
        _pack_chunk_cols(inp["L0_attn_b_ff2"]),
        _pack_chunk_cols(np.asarray(inp["L0_ffn_shared_gate_w"]).ravel()),
        _pack_chunk_cols(inp["L0_attn_b_ff1"]),
    ]
    ff1 = np.asarray(inp["L0_attn_w_ff1"], dtype=np.float32)
    w1 = np.asarray(inp["L0_ffn_w1"], dtype=np.float32)
    w2 = np.asarray(inp["L0_ffn_w2"], dtype=np.float32)
    sgu = np.asarray(inp["L0_ffn_shared_gate_up"], dtype=np.float32)
    pieces = {
        "ident": np.eye(128, dtype=np.float32),
        "vec": np.concatenate(vec_cols, axis=1),
        "wk": inp["L0_attn_wk"], "wv": inp["L0_attn_wv"],
        "wq": inp["L0_attn_wq"],
        "wo": np.asarray(inp["L0_attn_wo"], dtype=np.float32).reshape(8, 128, 8, 128).transpose(2, 1, 0, 3),
        "wd": inp["L0_attn_w_d"],
        "ff1cm": ff1.reshape(8, 128, 32, 128).transpose(2, 1, 0, 3),
        "ff2": inp["L0_attn_w_ff2"],
        "router": np.asarray(inp["L0_ffn_router_w"], dtype=np.float32).reshape(8, 128, 8).transpose(1, 0, 2),
        "w1cm": w1.reshape(NE, 8, 128, 8, 128).transpose(0, 3, 2, 1, 4),
        "w2cm": w2.reshape(NE, 8, 128, 8, 128).transpose(0, 3, 2, 1, 4),
        "sgu_gate": sgu[:, :D], "sgu_up": sgu[:, D:],
        "sdown": inp["L0_ffn_shared_down"],
    }
    blob = np.empty(NBLOB, dtype=np.float32)
    for k in _W_ORDER:
        a = np.ascontiguousarray(pieces[k], dtype=np.float32).ravel()
        assert a.size == _W_SIZES[k], (k, a.size, _W_SIZES[k])
        blob[_W_OFF[k]:_W_OFF[k] + a.size] = a
    return blob


def build_program(taps=()):
    nc = bass.Bass("TRN2", target_bir_lowering=False, debug=False)
    wblob = nc.dram_tensor("wblob", [NBLOB], F32, kind="ExternalInput").ap()
    xhbm = nc.dram_tensor("xb", [D, 2 * TOWN], F32, kind="ExternalInput").ap()
    out_h = nc.dram_tensor("out", [D, TOWN], F32, kind="ExternalOutput").ap()
    tap_aps = {}
    for tname, tshape in taps:
        tap_aps[tname] = nc.dram_tensor("tap_" + tname, list(tshape), F32,
                                        kind="ExternalOutput").ap()

    def wsl(name, elem_off, p, c):
        base = _W_OFF[name] + elem_off
        return wblob[base:base + p * c].rearrange("(p c) -> p c", c=c)

    GRAN = []
    GIDX = {}
    def _reg_gran(tag, wname, src_off):
        GIDX[tag] = len(GRAN)
        GRAN.append((wname, src_off))
    for _o in range(8):
        _reg_gran(f"wo:{_o}", "wo", _o * 8 * 128 * 128)
    for _o1 in range(32):
        _reg_gran(f"ff1:{_o1}", "ff1cm", _o1 * 8 * 128 * 128)
        _reg_gran(f"ff2:{_o1}", "ff2", _o1 * 128 * 1024)
    for _c in range(8):
        _reg_gran(f"wd:{_c}", "wd", _c * 128 * 1024)
    for _e in range(8):
        for _o1 in range(8):
            _reg_gran(f"w1:{_e}:{_o1}", "w1cm", _e * D * D + _o1 * 8 * 128 * 128)
    for _e in range(8):
        for _o1 in range(8):
            _reg_gran(f"w2:{_e}:{_o1}", "w2cm", _e * D * D + _o1 * 8 * 128 * 128)
    for _c in range(8):
        _reg_gran(f"sgug:{_c}", "sgu_gate", _c * 128 * 1024)
    for _c in range(8):
        _reg_gran(f"sguu:{_c}", "sgu_up", _c * 128 * 1024)
    for _c in range(8):
        _reg_gran(f"sdown:{_c}", "sdown", _c * 128 * 1024)
    NGRAN = len(GRAN)
    wpair = nc.dram_tensor("wpair", [NGRAN * 128 * 2048], BF16, kind="Internal").ap()

    bd = BD()
    eso = ExitStack()

    with eso:
        sems = {}
        for s in ("pe", "dve", "act", "pool", "sp", "dxp",
                  "dldc", "dldx", "dldk", "dldv", "dldq", "dldo",
                  "dg10", "dg11", "df20", "df21", "dwd0", "dwd1", "dwo0", "dwo1",
                  "dw10", "dw11", "dw20", "dw21", "dgu0", "dgu1",
                  "dsd0", "dsd1", "dsd2", "dsd3", "dpwi", "dpwo", "dout"):
            sems[s] = eso.enter_context(nc.semaphore(s))
        pb = [eso.enter_context(nc.psum_tensor(f"pb{i}", [128, 512], F32))
              for i in range(8)]

        def sbo(es, name, p, c, dt=F32):
            return es.enter_context(nc.sbuf_tensor(name, [p, c], dt))

        ident = sbo(eso, "ident", 128, 128)
        vec = sbo(eso, "vec", 128, 96)
        onescol = sbo(eso, "onescol", 128, 1)
        onesrow = sbo(eso, "onesrow", 1, 128)
        X = sbo(eso, "X", 128, NCH * TOWN)
        stat0 = sbo(eso, "stat0", 1, 1536)
        srow = sbo(eso, "srow", 1, 2048)
        router_sb = sbo(eso, "router_sb", 128, 64)
        ksum = sbo(eso, "ksum", 128, 8)
        rks = sbo(eso, "rks", 128, 8)
        ctx = sbo(eso, "ctx", 128, 2048)

        MU, RHO, R0 = 0, 512, 1024

        def vcol(name, c):
            return vec[:, VI[name] + c: VI[name] + c + 1]

        bd.op("sp", lambda e: e.dma_start(out=ident[:], in_=wsl("ident", 0, 128, 128)),
              writes=("ident",), dma="dldc")
        bd.op("sp", lambda e: e.dma_start(out=vec[:], in_=wsl("vec", 0, 128, 96)),
              writes=("vec",), dma="dldc")
        bd.op("sp", lambda e: e.dma_start(out=router_sb[:], in_=wsl("router", 0, 128, 64)),
              writes=("router_sb",), dma="dldc")
        for c in range(NCH):
            bd.op("sp", lambda e, c=c: e.dma_start(
                out=X[:, c * TOWN:(c + 1) * TOWN], in_=xhbm[c * 128:(c + 1) * 128, 0:TOWN]),
                writes=(f"X:{c}",), dma="dldx")
        bd.dve(lambda e: e.memset(onescol[:], 1.0), writes=("onescol",))
        bd.dve(lambda e: e.memset(onesrow[:], 1.0), writes=("onesrow",))
        bd.dve(lambda e: e.memset(ksum[:], 0.0), writes=("ksum",))

        def ln_stats(src_fn, src_keys, xsq, xsq_key, tsz, want_r0):
            for c in range(NCH):
                bd.act(lambda e, c=c, src_fn=src_fn, xsq=xsq, tsz=tsz: e.activation(
                    out=xsq[:, c * tsz:(c + 1) * tsz], in_=src_fn(c), func=AF.Square),
                    reads=(src_keys[c],), writes=(f"{xsq_key}:{c}",))
            for c in range(NCH):
                bd.pe(lambda e, c=c, src_fn=src_fn, tsz=tsz: e.matmul(
                    pb[0][0:1, 0:tsz], onescol[:], src_fn(c),
                    start=(c == 0), stop=(c == NCH - 1)),
                    reads=("onescol", src_keys[c]),
                    writes=(("pb0",) if c == 0 else ()),
                    writes_ext=(() if c == 0 else ("pb0",)))
            for c in range(NCH):
                bd.pe(lambda e, c=c, xsq=xsq, tsz=tsz: e.matmul(
                    pb[0][32:33, 0:tsz], onescol[:], xsq[:, c * tsz:(c + 1) * tsz],
                    start=(c == 0), stop=(c == NCH - 1)),
                    reads=("onescol", f"{xsq_key}:{c}"), writes_ext=("pb0",))
            mu = stat0[0:1, MU:MU + tsz]
            rho = stat0[0:1, RHO:RHO + tsz]
            sq = srow[0:1, 0:tsz]
            esq = srow[0:1, 256:256 + tsz]
            var = srow[0:1, 512:512 + tsz]
            av = srow[0:1, 768:768 + tsz]
            ra = srow[0:1, 1024:1024 + tsz]
            vr = srow[0:1, 1280:1280 + tsz]
            avr = srow[0:1, 1536:1536 + tsz]
            rar = srow[0:1, 1792:1792 + tsz]
            r1 = av
            r0ap = stat0[0:1, R0:R0 + tsz] if want_r0 else sq
            bd.dve(lambda e, mu=mu, tsz=tsz: e.tensor_scalar_mul(mu, pb[0][0:1, 0:tsz], 1.0 / D),
                   reads=("pb0",), writes=("s_mu",))
            bd.dve(lambda e, sq=sq, mu=mu: e.tensor_mul(sq, mu, mu),
                   reads=("s_mu",), writes=("s_sq",))
            bd.dve(lambda e, esq=esq, tsz=tsz: e.tensor_scalar_mul(
                esq, pb[0][32:33, 0:tsz], 1.0 / D),
                reads=("pb0",), writes=("s_esq",))
            bd.dve(lambda e, var=var, esq=esq, sq=sq: e.tensor_sub(var, esq, sq),
                   reads=("s_esq", "s_sq"), writes=("s_var",))
            if want_r0:
                bd.dve(lambda e, av=av, var=var: e.tensor_scalar_add(av, var, float(EPS)),
                       reads=("s_var",), writes=("s_a",))
                bd.dve(lambda e, ra=ra, av=av: e.reciprocal(ra, av),
                       reads=("s_a",), writes=("s_ra",))
                bd.act(lambda e, r0ap=r0ap, ra=ra: e.activation(
                    out=r0ap, in_=ra, func=AF.Sqrt),
                    reads=("s_ra", "s_sq"), writes=("s_r0",))
            bd.dve(lambda e, avr=avr, var=var: e.tensor_scalar(
                avr, var, float(1.0 + EPS), float(EPS * EPS), ALU.mult, ALU.add),
                reads=("s_var",), writes=("s_avr",))
            bd.dve(lambda e, rar=rar, avr=avr: e.reciprocal(rar, avr),
                   reads=("s_avr",), writes=("s_rar",))
            bd.act(lambda e, rho=rho, rar=rar: e.activation(
                out=rho, in_=rar, func=AF.Sqrt),
                reads=("s_rar",), writes=("s_rho",))

        def bcast(bank, colo, width, row_ap, row_key, start, stop):
            bd.pe(lambda e, bank=bank, colo=colo, width=width, row_ap=row_ap: e.matmul(
                pb[bank][:, colo:colo + width], onesrow[:], row_ap, start=start, stop=stop),
                reads=("onesrow", row_key),
                writes=((f"pb{bank}",) if start else ()),
                writes_ext=(() if start else (f"pb{bank}",)))

        def mm3(bank, cs, ce_, lh, ll, rh, rl, start, stop, reads, prow=128):
            key = f"pb{bank}"
            bd.pe(lambda e, lh=lh, rh=rh, bank=bank, cs=cs, ce_=ce_, start=start,
                  prow=prow: e.matmul(
                pb[bank][0:prow, cs:ce_], lh, rh, start=start, stop=False),
                reads=reads, writes=((key,) if start else ()),
                writes_ext=(() if start else (key,)))
            bd.pe(lambda e, lh=lh, rl=rl, bank=bank, cs=cs, ce_=ce_, prow=prow: e.matmul(
                pb[bank][0:prow, cs:ce_], lh, rl, start=False, stop=False),
                reads=reads, writes_ext=(key,))
            bd.pe(lambda e, ll=ll, rh=rh, bank=bank, cs=cs, ce_=ce_, stop=stop,
                  prow=prow: e.matmul(
                pb[bank][0:prow, cs:ce_], ll, rh, start=False, stop=stop),
                reads=reads, writes_ext=(key,))

        def split(src_ap, hi_ap, lo_ap, reads, wkey):
            bd.act(lambda e, src_ap=src_ap, hi_ap=hi_ap: e.activation(
                out=hi_ap, in_=src_ap, func=AF.Copy),
                reads=reads, writes=(wkey + "h",))
            bd.dve(lambda e, src_ap=src_ap, hi_ap=hi_ap, lo_ap=lo_ap: e.tensor_sub(
                lo_ap, src_ap, hi_ap),
                reads=reads + (wkey + "h",), writes=(wkey + "l",))

        def load_split(stream, wname, granule_off, cols, wtmp, wtmp_key, hi_t, lo_t,
                       dst_off, wkey, eng="sp", woff=0):
            wkeys = (wtmp_key,) if isinstance(wtmp_key, str) else tuple(wtmp_key)
            bd.op(eng, lambda e, granule_off=granule_off, cols=cols, woff=woff: e.dma_start(
                out=wtmp[:, woff:woff + cols], in_=wsl(wname, granule_off, 128, cols)),
                writes=wkeys, dma=stream)
            bd.act(lambda e, cols=cols, dst_off=dst_off, woff=woff: e.activation(
                out=hi_t[:, dst_off:dst_off + cols], in_=wtmp[:, woff:woff + cols],
                func=AF.Copy),
                reads=wkeys, writes=(wkey + "h",))
            bd.dve(lambda e, cols=cols, dst_off=dst_off, woff=woff: e.tensor_sub(
                lo_t[:, dst_off:dst_off + cols], wtmp[:, woff:woff + cols],
                hi_t[:, dst_off:dst_off + cols]),
                reads=wkeys + (wkey + "h",), writes=(wkey + "l",))

        def load_pair(stream, tag, slot, slot_key, eng="sp"):
            gi = GIDX[tag]
            base = gi * 128 * 2048
            src2 = wpair[base:base + 128 * 2048].rearrange("(h p c) -> p h c", h=2, c=1024)
            bd.op(eng, lambda e, src2=src2: e.dma_start(out=slot[:], in_=src2),
                  reads=(f"pw:{gi}",), writes=(slot_key,), dma=stream)

        def emit_pair_build(lo_g, hi_g, wtmps, pslots):
            for gi in range(lo_g, hi_g):
                wtmp, wtmp_key = wtmps[gi % 2]
                pslot, pslot_key = pslots[gi % 2]
                wname, src_off = GRAN[gi]
                base = gi * 128 * 2048
                hi_hbm = wpair[base:base + 128 * 1024].rearrange("(p c) -> p c", c=1024)
                lo_hbm = wpair[base + 128 * 1024:base + 128 * 2048].rearrange(
                    "(p c) -> p c", c=1024)
                bd.op("sp", lambda e, wname=wname, src_off=src_off, wtmp=wtmp: e.dma_start(
                    out=wtmp[:], in_=wsl(wname, src_off, 128, 1024)),
                    writes=(wtmp_key,), dma="dpwi")
                bd.act(lambda e, wtmp=wtmp, pslot=pslot: e.activation(
                    out=pslot[:, 0:1024], in_=wtmp[:], func=AF.Copy),
                    reads=(wtmp_key,), writes=(pslot_key + "h",))
                bd.dve(lambda e, wtmp=wtmp, pslot=pslot: e.tensor_sub(
                    pslot[:, 1024:2048], wtmp[:], pslot[:, 0:1024]),
                    reads=(wtmp_key, pslot_key + "h"), writes=(pslot_key + "l",))
                bd.op("pool", lambda e, hi_hbm=hi_hbm, pslot=pslot: e.dma_start(
                    out=hi_hbm, in_=pslot[:, 0:1024]),
                    reads=(pslot_key + "h",), writes=(f"pwh:{gi}",), dma="dpwo")
                bd.op("pool", lambda e, lo_hbm=lo_hbm, pslot=pslot: e.dma_start(
                    out=lo_hbm, in_=pslot[:, 1024:2048]),
                    reads=(pslot_key + "l",), writes=(f"pw:{gi}",), dma="dpwo")

        esa = ExitStack()
        with esa:
            wkh = sbo(esa, "wkh", 128, NCH * 1024, BF16)
            wkl = sbo(esa, "wkl", 128, NCH * 1024, BF16)
            wvh = sbo(esa, "wvh", 128, NCH * 1024, BF16)
            wvl = sbo(esa, "wvl", 128, NCH * 1024, BF16)
            wtmpA = sbo(esa, "wtmpA", 128, 1024)
            wtmpA2 = sbo(esa, "wtmpA2", 128, 1024)
            xp = sbo(esa, "xp", 128, NCH * 512)
            hAh = sbo(esa, "hAh", 128, NCH * 128, BF16)
            hAl = sbo(esa, "hAl", 128, NCH * 128, BF16)
            xsqA = sbo(esa, "xsqA", 128, NCH * 128)
            diffA = sbo(esa, "diffA", 128, NCH * 128)
            kexp = sbo(esa, "kexp", 128, 1024)
            keh = sbo(esa, "keh", 128, 1024, BF16)
            kel = sbo(esa, "kel", 128, 1024, BF16)
            vth = sbo(esa, "vth", 128, 1024, BF16)
            vtl = sbo(esa, "vtl", 128, 1024, BF16)
            onescol_bf = sbo(esa, "onescol_bf", 128, 1, BF16)
            pslotA = sbo(esa, "pslotA", 128, 2048, BF16)
            pslotB = sbo(esa, "pslotB", 128, 2048, BF16)
            bd.dve(lambda e: e.memset(onescol_bf[:], 1.0), writes=("onescol_bf",))

            for c in range(NCH):
                load_split("dldk", "wk", c * 128 * 1024, 1024, wtmpA, "wtmpA",
                           wkh, wkl, c * 1024, f"wk:{c}")
                load_split("dldv", "wv", c * 128 * 1024, 1024, wtmpA2, "wtmpA2",
                           wvh, wvl, c * 1024, f"wv:{c}")

            for g in range(8):
                own = g < 4
                if not own:
                    for c in range(NCH):
                        bd.op("sp", lambda e, c=c, g=g: e.dma_start(
                            out=xp[:, c * 512:(c + 1) * 512],
                            in_=xhbm[c * 128:(c + 1) * 128, g * 512:(g + 1) * 512]),
                            writes=(f"xp:{c}",), dma="dxp")
                for s4 in range(4):
                    si_ = g * 4 + s4

                    def src(c, s4=s4, g=g, own=own):
                        if own:
                            o = g * 512 + s4 * 128
                            return X[:, c * TOWN + o: c * TOWN + o + 128]
                        return xp[:, c * 512 + s4 * 128: c * 512 + s4 * 128 + 128]

                    skeys = tuple((f"X:{c}" if own else f"xp:{c}") for c in range(NCH))
                    ln_stats(src, skeys, xsqA, "xsqA", 128, want_r0=False)
                    bcast(1, 0, 128, stat0[0:1, MU:MU + 128], "s_mu", True, False)
                    bcast(1, 128, 128, stat0[0:1, RHO:RHO + 128], "s_rho", False, True)
                    for c in range(NCH):
                        bd.dve(lambda e, c=c, src=src: e.tensor_sub(
                            diffA[:, c * 128:(c + 1) * 128], src(c), pb[1][:, 0:128]),
                            reads=(skeys[c], "pb1"), writes=(f"diffA:{c}",))
                    for c in range(NCH):
                        bd.dve(lambda e, c=c: e.tensor_mul(
                            diffA[:, c * 128:(c + 1) * 128], diffA[:, c * 128:(c + 1) * 128],
                            pb[1][:, 128:256]),
                            reads=("pb1",), writes=(f"diffA:{c}",))
                    for c in range(NCH):
                        bd.dve(lambda e, c=c: e.tensor_scalar(
                            diffA[:, c * 128:(c + 1) * 128], diffA[:, c * 128:(c + 1) * 128],
                            vcol("ln1_g", c), vcol("ln1_b", c), ALU.mult, ALU.add),
                            reads=("vec",), writes=(f"diffA:{c}",))
                    for c in range(NCH):
                        split(diffA[:, c * 128:(c + 1) * 128], hAh[:, c * 128:(c + 1) * 128],
                              hAl[:, c * 128:(c + 1) * 128], (f"diffA:{c}",), f"hA:{c}")
                    for c in range(NCH):
                        for hc in range(2):
                            mm3(2 + hc, 0, 512,
                                hAh[:, c * 128:(c + 1) * 128], hAl[:, c * 128:(c + 1) * 128],
                                wkh[:, c * 1024 + hc * 512: c * 1024 + hc * 512 + 512],
                                wkl[:, c * 1024 + hc * 512: c * 1024 + hc * 512 + 512],
                                start=(c == 0), stop=(c == NCH - 1),
                                reads=(f"hA:{c}h", f"hA:{c}l", f"wk:{c}h", f"wk:{c}l"))
                    emit_pair_build(si_ * 8, min(si_ * 8 + 2, NGRAN),
                                    ((wtmpA, "wtmpA"), (wtmpA2, "wtmpA2")),
                                    ((pslotA, "psA"), (pslotB, "psB")))
                    for hc in range(2):
                        bd.act(lambda e, hc=hc: e.activation(
                            out=kexp[:, hc * 512:(hc + 1) * 512], in_=pb[2 + hc][:, 0:512],
                            func=AF.Exp, scale=float(SC)),
                            reads=(f"pb{2 + hc}",), writes=(f"kexp:{hc}",))
                    for hc in range(2):
                        split(kexp[:, hc * 512:(hc + 1) * 512],
                              keh[:, hc * 512:(hc + 1) * 512],
                              kel[:, hc * 512:(hc + 1) * 512],
                              (f"kexp:{hc}",), f"kexp:{hc}")
                    for c in range(NCH):
                        for hc in range(2):
                            mm3(2 + hc, 0, 512,
                                hAh[:, c * 128:(c + 1) * 128], hAl[:, c * 128:(c + 1) * 128],
                                wvh[:, c * 1024 + hc * 512: c * 1024 + hc * 512 + 512],
                                wvl[:, c * 1024 + hc * 512: c * 1024 + hc * 512 + 512],
                                start=(c == 0), stop=(c == NCH - 1),
                                reads=(f"hA:{c}h", f"hA:{c}l", f"wv:{c}h", f"wv:{c}l"))
                    emit_pair_build(si_ * 8 + 2, min(si_ * 8 + 4, NGRAN),
                                    ((wtmpA, "wtmpA"), (wtmpA2, "wtmpA2")),
                                    ((pslotA, "psA"), (pslotB, "psB")))
                    for hc in range(2):
                        split(pb[2 + hc][:, 0:512], vth[:, hc * 512:(hc + 1) * 512],
                              vtl[:, hc * 512:(hc + 1) * 512],
                              (f"pb{2 + hc}",), f"vtm:{hc}")
                    emit_pair_build(si_ * 8 + 4, min(si_ * 8 + 6, NGRAN),
                                    ((wtmpA, "wtmpA"), (wtmpA2, "wtmpA2")),
                                    ((pslotA, "psA"), (pslotB, "psB")))
                    for j in range(8):
                        head, ec = j // 2, j % 2
                        bank, colo = 4 + j // 2, (j % 2) * 256
                        first = (s4 == 0 and colo == 0)
                        last = (s4 == 3 and colo == 256)
                        ks = head * 256 + ec * 128
                        mm3(bank, colo, colo + 256,
                            keh[:, ks:ks + 128], kel[:, ks:ks + 128],
                            vth[:, head * 256: head * 256 + 256],
                            vtl[:, head * 256: head * 256 + 256],
                            start=first, stop=last,
                            reads=(f"kexp:{ks // 512}h", f"kexp:{ks // 512}l",
                                   f"vtm:{(head * 256) // 512}h",
                                   f"vtm:{(head * 256) // 512}l"))
                    for j in range(8):
                        head, ec = j // 2, j % 2
                        ks = head * 256 + ec * 128
                        bd.pe(lambda e, j=j, ks=ks: e.matmul(
                            pb[1][:, j:j + 1], keh[:, ks:ks + 128], onescol_bf[:],
                            start=(j == 0), stop=False),
                            reads=(f"kexp:{ks // 512}h", "onescol_bf"),
                            writes=(("pb1",) if j == 0 else ()),
                            writes_ext=(() if j == 0 else ("pb1",)))
                        bd.pe(lambda e, j=j, ks=ks: e.matmul(
                            pb[1][:, j:j + 1], kel[:, ks:ks + 128], onescol_bf[:],
                            start=False, stop=(j == 7)),
                            reads=(f"kexp:{ks // 512}l", "onescol_bf"),
                            writes_ext=("pb1",))
                    bd.dve(lambda e: e.tensor_add(ksum[:, 0:8], ksum[:, 0:8], pb[1][:, 0:8]),
                           reads=("pb1",), writes=("ksum",))
                    si = g * 4 + s4
                    emit_pair_build(si * 8 + 6, min(si * 8 + 8, NGRAN),
                                    ((wtmpA, "wtmpA"), (wtmpA2, "wtmpA2")),
                                    ((pslotA, "psA"), (pslotB, "psB")))
                for j in range(8):
                    bank, colo = 4 + j // 2, (j % 2) * 256
                    if g == 0:
                        bd.dve(lambda e, j=j, bank=bank, colo=colo: e.tensor_copy(
                            ctx[:, j * 256:(j + 1) * 256], pb[bank][:, colo:colo + 256]),
                            reads=(f"pb{bank}",), writes=(f"ctx:{j}",))
                    else:
                        bd.dve(lambda e, j=j, bank=bank, colo=colo: e.tensor_add(
                            ctx[:, j * 256:(j + 1) * 256], ctx[:, j * 256:(j + 1) * 256],
                            pb[bank][:, colo:colo + 256]),
                            reads=(f"pb{bank}",), writes=(f"ctx:{j}",))

            bd.dve(lambda e: e.reciprocal(rks[:], ksum[:]), reads=("ksum",), writes=("rks",))
            for j in range(8):
                bd.dve(lambda e, j=j: e.tensor_scalar_mul(
                    ctx[:, j * 256:(j + 1) * 256], ctx[:, j * 256:(j + 1) * 256],
                    rks[:, j:j + 1]),
                    reads=("rks",), writes=(f"ctx:{j}",))
            if "ctx" in tap_aps:
                bd.pool(lambda e: e.dma_start(out=tap_aps["ctx"], in_=ctx[:]),
                        reads=tuple(f"ctx:{j}" for j in range(8)), dma="dout")

            bd.add_floor(
                [f"wk:{c}{s}" for c in range(NCH) for s in ("h", "l")]
                + [f"wv:{c}{s}" for c in range(NCH) for s in ("h", "l")]
                + [f"xp:{c}" for c in range(NCH)]
                + [f"hA:{c}{s}" for c in range(NCH) for s in ("h", "l")]
                + [f"xsqA:{c}" for c in range(NCH)] + [f"diffA:{c}" for c in range(NCH)]
                + ["wtmpA", "wtmpA2", "onescol_bf", "psAh", "psAl", "psBh", "psBl"]
                + [f"kexp:{h}{s}" for h in range(2) for s in ("", "h", "l")]
                + [f"vtm:{h}{s}" for h in range(2) for s in ("h", "l")])

        esb = ExitStack()
        with esb:
            wq_sb = sbo(esb, "wq_sb", 128, NCH * 1024)
            B1 = sbo(esb, "B1", 128, 2048)
            B2 = sbo(esb, "B2", 128, 2048)
            B3 = sbo(esb, "B3", 128, 2048)
            B4 = sbo(esb, "B4", 128, 2048)
            qt = sbo(esb, "qt", 128, 16)
            P1h = sbo(esb, "P1h", 128, 2048, BF16)
            P1l = sbo(esb, "P1l", 128, 2048, BF16)
            wop = [sbo(esb, f"wop{i}", 128, 2048, BF16) for i in range(2)]
            g1p = [sbo(esb, f"g1p{i}", 128, 2048, BF16) for i in range(2)]
            f2p = [sbo(esb, f"f2p{i}", 128, 2048, BF16) for i in range(2)]
            wdp = [sbo(esb, f"wdp{i}", 128, 2048, BF16) for i in range(2)]
            gch = sbo(esb, "gch", 128, 512, BF16)
            gcl = sbo(esb, "gcl", 128, 512, BF16)

            for c in range(NCH):
                bd.op("sp", lambda e, c=c: e.dma_start(
                    out=wq_sb[:, c * 1024:(c + 1) * 1024],
                    in_=wsl("wq", c * 128 * 1024, 128, 1024)),
                    writes=(f"wq:{c}",), dma="dldq")

            TB = 256

            for t in range(TOWN // TB):
                T0 = t * TB

                def xsrc(c, T0=T0):
                    return X[:, c * TOWN + T0: c * TOWN + T0 + TB]

                xkeys = tuple(f"X:{c}" for c in range(NCH))
                ln_stats(xsrc, xkeys, B4, "B4", TB, want_r0=True)
                bcast(1, 0, TB, stat0[0:1, MU:MU + TB], "s_mu", True, False)
                bcast(1, TB, TB, stat0[0:1, RHO:RHO + TB], "s_rho", False, True)
                bcast(0, 0, TB, stat0[0:1, R0:R0 + TB], "s_r0", True, True)
                for c in range(NCH):
                    bd.dve(lambda e, c=c, xsrc=xsrc: e.tensor_sub(
                        B1[:, c * TB:(c + 1) * TB], xsrc(c), pb[1][:, 0:TB]),
                        reads=(xkeys[c], "pb1"), writes=(f"B1:{c}",))
                for c in range(NCH):
                    bd.dve(lambda e, c=c: e.tensor_mul(
                        B2[:, c * TB:(c + 1) * TB], B1[:, c * TB:(c + 1) * TB],
                        pb[1][:, TB:2 * TB]),
                        reads=("pb1", f"B1:{c}"), writes=(f"B2:{c}",))
                for c in range(NCH):
                    bd.dve(lambda e, c=c: e.tensor_scalar(
                        B2[:, c * TB:(c + 1) * TB], B2[:, c * TB:(c + 1) * TB],
                        vcol("ln1_g", c), vcol("ln1_b", c), ALU.mult, ALU.add),
                        reads=("vec",), writes=(f"B2:{c}",))
                for c in range(NCH):
                    bd.dve(lambda e, c=c: e.tensor_mul(
                        B3[:, c * TB:(c + 1) * TB], B1[:, c * TB:(c + 1) * TB],
                        pb[0][:, 0:TB]),
                        reads=("pb0", f"B1:{c}"), writes=(f"B3:{c}",))
                for s2 in range(2):
                    for c in range(NCH):
                        for hc in range(2):
                            bd.pe(lambda e, c=c, hc=hc, s2=s2: e.matmul(
                                pb[2 + hc][:, 0:512],
                                B2[:, c * TB + s2 * 128: c * TB + s2 * 128 + 128],
                                wq_sb[:, c * 1024 + hc * 512: c * 1024 + hc * 512 + 512],
                                start=(c == 0), stop=(c == NCH - 1)),
                                reads=(f"B2:{c}", f"wq:{c}"),
                                writes=((f"pb{2 + hc}",) if c == 0 else ()),
                                writes_ext=(() if c == 0 else (f"pb{2 + hc}",)))
                    for head in range(HEADS):
                        bank = 2 + head // 2
                        colo = (head % 2) * 256
                        qcell = qt[:, head * 4: head * 4 + 1]
                        qbias = qt[:, head * 4 + 1: head * 4 + 2]
                        qsum = qt[:, head * 4 + 2: head * 4 + 3]
                        qrec = qt[:, head * 4 + 3: head * 4 + 4]
                        qekey = f"B1:{s2 * 4 + head}"
                        qe_ap = B1[:, s2 * 1024 + head * 256: s2 * 1024 + head * 256 + 256]
                        bd.act(lambda e, bank=bank, colo=colo, qe_ap=qe_ap:
                               e.activation(out=qe_ap, in_=pb[bank][:, colo:colo + 256],
                                            func=AF.Exp, scale=float(SC)),
                               reads=(f"pb{bank}",), writes=(qekey,))
                        bd.dve(lambda e, qsum=qsum, qe_ap=qe_ap: e.reduce_sum(
                            qsum, qe_ap, axis=AX.X),
                            reads=(qekey,), writes=(f"qt:{head}c",))
                        bd.dve(lambda e, qrec=qrec, qsum=qsum: e.reciprocal(qrec, qsum),
                               reads=(f"qt:{head}c",), writes=(f"qt:{head}d",))
                        bd.dve(lambda e, qe_ap=qe_ap, qrec=qrec: e.tensor_scalar_mul(
                            qe_ap, qe_ap, qrec),
                            reads=(f"qt:{head}d",), writes=(qekey,))
                    for c in range(NCH):
                        bank = 4 + c // 4
                        colo = (c % 4) * 128
                        bd.pe(lambda e, c=c, s2=s2, bank=bank, colo=colo: e.matmul(
                            pb[bank][:, colo:colo + 128],
                            B1[:, s2 * 1024 + c * 128: s2 * 1024 + c * 128 + 128],
                            ident[:], is_transpose=True,
                            start=(colo == 0), stop=(colo == 384)),
                            reads=(f"B1:{s2 * 4 + c // 2}", "ident"),
                            writes=((f"pb{bank}",) if colo == 0 else ()),
                            writes_ext=(() if colo == 0 else (f"pb{bank}",)))
                    for c in range(NCH):
                        bank = 4 + c // 4
                        colo = (c % 4) * 128
                        bd.dve(lambda e, c=c, s2=s2, bank=bank, colo=colo: e.tensor_copy(
                            B4[:, c * TB + s2 * 128: c * TB + s2 * 128 + 128],
                            pb[bank][:, colo:colo + 128]),
                            reads=(f"pb{bank}",), writes=(f"B4:{c}",))
                if t == 0 and "qsm" in tap_aps:
                    bd.pool(lambda e: e.dma_start(out=tap_aps["qsm"], in_=B4[:]),
                            reads=tuple(f"B4:{c}" for c in range(NCH)), dma="dout")
                if t == 0 and "hB" in tap_aps:
                    bd.pool(lambda e: e.dma_start(out=tap_aps["hB"], in_=B2[:]),
                            reads=tuple(f"B2:{c}" for c in range(NCH)), dma="dout")
                attn_banks = [6, 6, 7, 7, 2, 2, 3, 3]
                for fj in range(8):
                    head, fc = fj // 2, fj % 2
                    bank = attn_banks[fj]
                    colo = (fj % 2) * 256
                    for ei in range(2):
                        j2 = head * 2 + ei
                        first = (ei == 0 and colo == 0)
                        last = (ei == 1 and colo == 256)
                        bd.pe(lambda e, j2=j2, fc=fc, bank=bank, colo=colo,
                              first=first, last=last: e.matmul(
                                  pb[bank][:, colo:colo + 256],
                                  ctx[:, j2 * 256 + fc * 128: j2 * 256 + fc * 128 + 128],
                                  B4[:, j2 * 256: j2 * 256 + 256],
                                  start=first, stop=last),
                              reads=(f"ctx:{j2}", f"B4:{j2}"),
                              writes=((f"pb{bank}",) if first else ()),
                              writes_ext=(() if first else (f"pb{bank}",)))
                for fj in range(8):
                    bank = attn_banks[fj]
                    colo = (fj % 2) * 256
                    split(pb[bank][0:128, colo:colo + 256],
                          P1h[:, fj * 256:(fj + 1) * 256],
                          P1l[:, fj * 256:(fj + 1) * 256],
                          (f"pb{bank}",), f"P1:{fj}")
                for o in range(NCH):
                    par = o % 2
                    bank = 4 + par
                    load_pair(f"dwo{par}", f"wo:{o}", wop[par], f"wop:{par}")
                    for f in range(NCH):
                        mm3(bank, 0, 256,
                            wop[par][:, f * 128:(f + 1) * 128],
                            wop[par][:, 1024 + f * 128: 1024 + f * 128 + 128],
                            P1h[:, f * 256:(f + 1) * 256], P1l[:, f * 256:(f + 1) * 256],
                            start=(f == 0), stop=(f == NCH - 1),
                            reads=(f"wop:{par}", f"P1:{f}h", f"P1:{f}l"))
                    bd.dve(lambda e, o=o, bank=bank: e.scalar_tensor_tensor(
                        B2[:, o * TB:(o + 1) * TB], pb[bank][:, 0:256], vcol("bo", o),
                        B3[:, o * TB:(o + 1) * TB], ALU.add, ALU.add),
                        reads=(f"pb{bank}", "vec", f"B3:{o}"), writes=(f"B2:{o}",))
                if t == 0 and "attn" in tap_aps:
                    bd.pool(lambda e: e.dma_start(out=tap_aps["attn"], in_=B1[:]),
                            reads=tuple(f"B1:{c}" for c in range(NCH)), dma="dout")
                if t == 0 and "X1" in tap_aps:
                    bd.pool(lambda e: e.dma_start(out=tap_aps["X1"], in_=B2[:]),
                            reads=tuple(f"B2:{c}" for c in range(NCH)), dma="dout")
                def x1src(c):
                    return B2[:, c * TB:(c + 1) * TB]

                x1keys = tuple(f"B2:{c}" for c in range(NCH))
                ln_stats(x1src, x1keys, B4, "B4", TB, want_r0=True)
                bcast(1, 0, TB, stat0[0:1, MU:MU + TB], "s_mu", True, False)
                bcast(1, TB, TB, stat0[0:1, R0:R0 + TB], "s_r0", False, True)
                for c in range(NCH):
                    bd.dve(lambda e, c=c, x1src=x1src: e.tensor_sub(
                        B3[:, c * TB:(c + 1) * TB], x1src(c), pb[1][:, 0:TB]),
                        reads=(x1keys[c], "pb1"), writes=(f"B3:{c}",))
                for c in range(NCH):
                    bd.dve(lambda e, c=c: e.tensor_mul(
                        B3[:, c * TB:(c + 1) * TB], B3[:, c * TB:(c + 1) * TB],
                        pb[1][:, TB:2 * TB]),
                        reads=("pb1",), writes=(f"B3:{c}",))
                for c in range(NCH):
                    bd.dve(lambda e, c=c: e.tensor_scalar(
                        B3[:, c * TB:(c + 1) * TB], B3[:, c * TB:(c + 1) * TB],
                        vcol("ln2_g", c), vcol("ln2_b", c), ALU.mult, ALU.add),
                        reads=("vec",), writes=(f"B3:{c}",))
                if t == 0 and "h2" in tap_aps:
                    bd.pool(lambda e: e.dma_start(out=tap_aps["h2"], in_=B3[:]),
                            reads=tuple(f"B3:{c}" for c in range(NCH)), dma="dout")
                for c in range(NCH):
                    split(B3[:, c * TB:(c + 1) * TB], P1h[:, c * TB:(c + 1) * TB],
                          P1l[:, c * TB:(c + 1) * TB], (f"B3:{c}",), f"P1:{c}")
                ff2_banks = [2, 2, 3, 3, 6, 6, 7, 7]
                for o1 in range(33):
                    par = o1 % 2
                    bank = 4 + par
                    if o1 < 32:
                        load_pair(f"dg1{par}", f"ff1:{o1}", g1p[par], f"g1:{par}")
                        for c in range(NCH):
                            mm3(bank, 0, 256,
                                g1p[par][:, c * 128:(c + 1) * 128],
                                g1p[par][:, 1024 + c * 128: 1024 + c * 128 + 128],
                                P1h[:, c * TB:(c + 1) * TB], P1l[:, c * TB:(c + 1) * TB],
                                start=(c == 0), stop=(c == NCH - 1),
                                reads=(f"g1:{par}", f"P1:{c}h", f"P1:{c}l"))
                        bd.act(lambda e, o1=o1, par=par, bank=bank: e.activation(
                            out=B1[:, par * 256: par * 256 + 256], in_=pb[bank][:, 0:256],
                            func=AF.Gelu_apprx_tanh,
                            bias=vec[:, VI["b_ff1"] + o1: VI["b_ff1"] + o1 + 1]),
                            reads=(f"pb{bank}", "vec"), writes=(f"B1:{par}",))
                        split(B1[:, par * 256: par * 256 + 256],
                              gch[:, par * 256: par * 256 + 256],
                              gcl[:, par * 256: par * 256 + 256],
                              (f"B1:{par}",), f"gc:{par}")
                        load_pair(f"df2{par}", f"ff2:{o1}", f2p[par], f"f2:{par}")
                    if o1 > 0:
                        o1p = o1 - 1
                        parp = o1p % 2
                        for o2 in range(NCH):
                            bank2 = ff2_banks[o2]
                            colo2 = (o2 % 2) * 256
                            first = (o1p == 0 and colo2 == 0)
                            last = (o1p == 31 and colo2 == 256)
                            mm3(bank2, colo2, colo2 + 256,
                                f2p[parp][:, o2 * 128:(o2 + 1) * 128],
                                f2p[parp][:, 1024 + o2 * 128: 1024 + o2 * 128 + 128],
                                gch[:, parp * 256: parp * 256 + 256],
                                gcl[:, parp * 256: parp * 256 + 256],
                                start=first, stop=last,
                                reads=(f"f2:{parp}", f"gc:{parp}h", f"gc:{parp}l"))
                for o2 in range(NCH):
                    bank2 = ff2_banks[o2]
                    colo2 = (o2 % 2) * 256
                    bd.dve(lambda e, o2=o2, bank2=bank2, colo2=colo2: e.scalar_tensor_tensor(
                        B4[:, o2 * TB:(o2 + 1) * TB], pb[bank2][:, colo2:colo2 + 256],
                        vcol("b_ff2", o2), B2[:, o2 * TB:(o2 + 1) * TB], ALU.add, ALU.add),
                        reads=(f"pb{bank2}", "vec", f"B2:{o2}"), writes=(f"B4:{o2}",))
                if t == 0 and "X2" in tap_aps:
                    bd.pool(lambda e: e.dma_start(out=tap_aps["X2"], in_=B4[:]),
                            reads=tuple(f"B4:{c}" for c in range(NCH)), dma="dout")
                for c in range(NCH):
                    split(B4[:, c * TB:(c + 1) * TB], P1h[:, c * TB:(c + 1) * TB],
                          P1l[:, c * TB:(c + 1) * TB], (f"B4:{c}",), f"P1:{c}")
                wd_banks = [0, 0, 1, 1, 4, 4, 5, 5]
                for c in range(NCH):
                    par = c % 2
                    load_pair(f"dwd{par}", f"wd:{c}", wdp[par], f"wdp:{par}")
                    for o in range(NCH):
                        bank = wd_banks[o]
                        colo = (o % 2) * 256
                        first = (c == 0 and colo == 0)
                        last = (c == NCH - 1 and colo == 256)
                        mm3(bank, colo, colo + 256,
                            wdp[par][:, o * 128:(o + 1) * 128],
                            wdp[par][:, 1024 + o * 128: 1024 + o * 128 + 128],
                            P1h[:, c * TB:(c + 1) * TB], P1l[:, c * TB:(c + 1) * TB],
                            start=first, stop=last,
                            reads=(f"wdp:{par}", f"P1:{c}h", f"P1:{c}l"))
                for o in range(NCH):
                    bank = wd_banks[o]
                    colo = (o % 2) * 256
                    bd.dve(lambda e, o=o, bank=bank, colo=colo, T0=T0: e.scalar_tensor_tensor(
                        X[:, o * TOWN + T0: o * TOWN + T0 + TB],
                        pb[bank][:, colo:colo + 256], vcol("b_d", o),
                        X[:, o * TOWN + T0: o * TOWN + T0 + TB], ALU.add, ALU.add),
                        reads=(f"pb{bank}", "vec"), writes=(f"X:{o}",))

            if "Y" in tap_aps:
                for c in range(NCH):
                    bd.pool(lambda e, c=c: e.dma_start(
                        out=tap_aps["Y"][c * 128:(c + 1) * 128, :],
                        in_=X[:, c * TOWN:(c + 1) * TOWN]),
                        reads=(f"X:{c}",), dma="dout")

            bd.add_floor(
                [f"wq:{c}" for c in range(NCH)] + [f"wo:{c}" for c in range(NCH)]
                + [f"B1:{c}" for c in range(NCH)] + [f"B2:{c}" for c in range(NCH)]
                + [f"B3:{c}" for c in range(NCH)] + [f"B4:{c}" for c in range(NCH)]
                + [f"P1:{c}{s}" for c in range(NCH) for s in ("h", "l")]
                + [f"g1:{p}" for p in range(2)] + [f"f2:{p}" for p in range(2)]
                + ["wdp:0", "wdp:1", "wop:0", "wop:1"]
                + [f"gc:{p}{s}" for p in range(2) for s in ("h", "l")]
                + [f"qt:{h}{x}" for h in range(HEADS) for x in "abcd"])

        esd = ExitStack()
        with esd:
            h3h = sbo(esd, "h3h", 128, NCH * 512, BF16)
            h3l = sbo(esd, "h3l", 128, NCH * 512, BF16)
            routed = sbo(esd, "routed", 128, NCH * 512)
            comb = sbo(esd, "comb", 128, 32)
            prb = sbo(esd, "prb", 128, 8)
            mtmp = sbo(esd, "mtmp", 128, 8)
            rtmp = sbo(esd, "rtmp", 128, 8)
            routerh = sbo(esd, "routerh", 128, 64, BF16)
            routerl = sbo(esd, "routerl", 128, 64, BF16)
            sgwh = sbo(esd, "sgwh", 128, 8, BF16)
            sgwl = sbo(esd, "sgwl", 128, 8, BF16)
            ce_row = sbo(esd, "ce_row", 1, 512)
            s2row = ce_row
            s2b = sbo(esd, "s2b", 128, 512)
            ftmpD = sbo(esd, "ftmpD", 128, 512)
            fin = ftmpD
            heh = sbo(esd, "heh", 128, NCH * 512, BF16)
            hel = sbo(esd, "hel", 128, NCH * 512, BF16)
            w1p = [sbo(esd, f"w1p{i}", 128, 2048, BF16) for i in range(2)]
            w2p = [sbo(esd, f"w2p{i}", 128, 2048, BF16) for i in range(2)]
            sg = sbo(esd, "sg", 128, NCH * 512)
            mdh = sbo(esd, "mdh", 128, NCH * 512, BF16)
            mdl = sbo(esd, "mdl", 128, NCH * 512, BF16)
            gup = [sbo(esd, f"gup{i}", 128, 2048, BF16) for i in range(2)]
            split(router_sb[:], routerh[:], routerl[:], ("router_sb",), "router")
            split(vec[:, VI["sgw"]:VI["sgw"] + 8], sgwh[:], sgwl[:], ("vec",), "sgw")

            for hf in range(4):
                H0 = hf * 512
                bd.add_floor([f"routed:{o}" for o in range(NCH)]
                             + [f"xsqD:{c}" for c in range(NCH)]
                             + [f"d3:{c}" for c in range(NCH)])
                for tt in range(2):
                    T0 = H0 + tt * 256

                    def ysrc(c, T0=T0):
                        return X[:, c * TOWN + T0: c * TOWN + T0 + 256]

                    ykeys = tuple(f"X:{c}" for c in range(NCH))
                    ln_stats(ysrc, ykeys, routed, "xsqD", 256, want_r0=True)
                    bcast(1, 0, 256, stat0[0:1, MU:MU + 256], "s_mu", True, False)
                    bcast(1, 256, 256, stat0[0:1, R0:R0 + 256], "s_r0", False, True)
                    for c in range(NCH):
                        bd.dve(lambda e, c=c, tt=tt, ysrc=ysrc: e.tensor_sub(
                            routed[:, 2048 + c * 256: 2048 + c * 256 + 256],
                            ysrc(c), pb[1][:, 0:256]),
                            reads=(ykeys[c], "pb1"), writes=(f"d3:{c}",))
                    for c in range(NCH):
                        bd.dve(lambda e, c=c: e.tensor_mul(
                            routed[:, 2048 + c * 256: 2048 + c * 256 + 256],
                            routed[:, 2048 + c * 256: 2048 + c * 256 + 256],
                            pb[1][:, 256:512]),
                            reads=("pb1",), writes=(f"d3:{c}",))
                    for c in range(NCH):
                        split(routed[:, 2048 + c * 256: 2048 + c * 256 + 256],
                              h3h[:, c * 512 + tt * 256: c * 512 + tt * 256 + 256],
                              h3l[:, c * 512 + tt * 256: c * 512 + tt * 256 + 256],
                              (f"d3:{c}",), f"h3:{c}")
                for sub in range(4):
                    for c in range(NCH):
                        mm3(2, 0, 8,
                            h3h[:, c * 512 + sub * 128: c * 512 + sub * 128 + 128],
                            h3l[:, c * 512 + sub * 128: c * 512 + sub * 128 + 128],
                            routerh[:, c * 8:(c + 1) * 8], routerl[:, c * 8:(c + 1) * 8],
                            start=(c == 0), stop=(c == NCH - 1),
                            reads=(f"h3:{c}h", f"h3:{c}l", "routerh", "routerl"))
                    bd.dve(lambda e: e.reduce_max(rtmp[:, 0:1], pb[2][:, 0:8], axis=AX.X),
                           reads=("pb2",), writes=("rt0",))
                    bd.dve(lambda e: e.tensor_scalar_mul(rtmp[:, 1:2], rtmp[:, 0:1], -1.0),
                           reads=("rt0",), writes=("rt1",))
                    bd.act(lambda e: e.activation(out=prb[:], in_=pb[2][:, 0:8],
                                                  func=AF.Exp, bias=rtmp[:, 1:2]),
                           reads=("pb2", "rt1"), writes=("prb",))
                    bd.dve(lambda e: e.reduce_sum(rtmp[:, 2:3], prb[:], axis=AX.X),
                           reads=("prb",), writes=("rt2",))
                    bd.dve(lambda e: e.reciprocal(rtmp[:, 3:4], rtmp[:, 2:3]),
                           reads=("rt2",), writes=("rt3",))
                    bd.dve(lambda e: e.tensor_scalar_mul(prb[:], prb[:], rtmp[:, 3:4]),
                           reads=("rt3",), writes=("prb",))
                    bd.dve(lambda e: e.reduce_max(rtmp[:, 4:5], prb[:], axis=AX.X),
                           reads=("prb",), writes=("rt4",))
                    bd.dve(lambda e: e.tensor_single_scalar(
                        mtmp[:], prb[:], rtmp[:, 4:5], ALU.is_ge),
                        reads=("prb", "rt4"), writes=("mtmp",))
                    bd.dve(lambda e: e.scalar_tensor_tensor(
                        mtmp[:], mtmp[:], -2.0, prb[:], ALU.mult, ALU.add),
                        reads=("prb",), writes=("mtmp",))
                    bd.dve(lambda e: e.reduce_max(rtmp[:, 5:6], mtmp[:], axis=AX.X),
                           reads=("mtmp",), writes=("rt5",))
                    bd.dve(lambda e, sub=sub: e.tensor_single_scalar(
                        comb[:, sub * 8:(sub + 1) * 8], prb[:], rtmp[:, 5:6], ALU.is_ge),
                        reads=("prb", "rt5"), writes=(f"comb:{sub}",))
                    bd.dve(lambda e, sub=sub: e.tensor_mul(
                        comb[:, sub * 8:(sub + 1) * 8], comb[:, sub * 8:(sub + 1) * 8],
                        prb[:]),
                        reads=("prb",), writes=(f"comb:{sub}",))
                xsq_drain = bd.drain([f"xsqD:{c}" for c in range(NCH)]
                                     + [f"d3:{c}" for c in range(NCH)])
                if True:
                    for en in range(NE):
                        for sub in range(4):
                            colo = sub * 128
                            bd.pe(lambda e, en=en, sub=sub, colo=colo: e.matmul(
                                pb[3][0:1, colo:colo + 128],
                                comb[:, sub * 8 + en: sub * 8 + en + 1], ident[:],
                                is_transpose=True,
                                start=(colo == 0), stop=(colo == 384)),
                                reads=(f"comb:{sub}", "ident"),
                                writes=(("pb3",) if colo == 0 else ()),
                                writes_ext=(() if colo == 0 else ("pb3",)))
                        bd.act(lambda e: e.activation(
                            out=ce_row[0:1, 0:512], in_=pb[3][0:1, 0:512], func=AF.Copy),
                            reads=("pb3",), writes=("ce_row",))
                        bd.pe(lambda e: e.matmul(
                            pb[6][:, 0:512], onesrow[:], ce_row[0:1, 0:512],
                            start=True, stop=True),
                            reads=("onesrow", "ce_row"), writes=("pb6",))
                        for o1 in range(NCH):
                            par = o1 % 2
                            load_pair(f"dw1{par}", f"w1:{en}:{o1}", w1p[par], f"w1p:{par}")
                            for c in range(NCH):
                                mm3(4 + par, 0, 512,
                                    w1p[par][:, c * 128:(c + 1) * 128],
                                    w1p[par][:, 1024 + c * 128: 1024 + c * 128 + 128],
                                    h3h[:, c * 512:(c + 1) * 512],
                                    h3l[:, c * 512:(c + 1) * 512],
                                    start=(c == 0), stop=(c == NCH - 1),
                                    reads=(f"w1p:{par}", f"h3:{c}h", f"h3:{c}l"))
                            bd.act(lambda e, par=par: e.activation(
                                out=ftmpD[:], in_=pb[4 + par][:, 0:512], func=AF.Silu),
                                reads=(f"pb{4 + par}",), writes=("ftmpD",))
                            bd.dve(lambda e: e.tensor_mul(ftmpD[:], ftmpD[:],
                                                          pb[6][:, 0:512]),
                                   reads=("pb6",), writes=("ftmpD",))
                            split(ftmpD[:], heh[:, o1 * 512:(o1 + 1) * 512],
                                  hel[:, o1 * 512:(o1 + 1) * 512], ("ftmpD",), f"he:{o1}")
                        for o2 in range(NCH):
                            par = o2 % 2
                            load_pair(f"dw2{par}", f"w2:{en}:{o2}", w2p[par], f"w2p:{par}")
                            for c in range(NCH):
                                mm3(2 + par, 0, 512,
                                    w2p[par][:, c * 128:(c + 1) * 128],
                                    w2p[par][:, 1024 + c * 128: 1024 + c * 128 + 128],
                                    heh[:, c * 512:(c + 1) * 512],
                                    hel[:, c * 512:(c + 1) * 512],
                                    start=(c == 0), stop=(c == NCH - 1),
                                    reads=(f"w2p:{par}", f"he:{c}h", f"he:{c}l"))
                            if en == 0:
                                bd.dve(lambda e, o2=o2, par=par: e.tensor_copy(
                                    routed[:, o2 * 512:(o2 + 1) * 512],
                                    pb[2 + par][:, 0:512]),
                                    deps=xsq_drain,
                                    reads=(f"pb{2 + par}",), writes=(f"routed:{o2}",))
                            else:
                                bd.dve(lambda e, o2=o2, par=par: e.tensor_add(
                                    routed[:, o2 * 512:(o2 + 1) * 512],
                                    routed[:, o2 * 512:(o2 + 1) * 512],
                                    pb[2 + par][:, 0:512]),
                                    reads=(f"pb{2 + par}",), writes=(f"routed:{o2}",))
                if True:
                    for c in range(NCH):
                        par = c % 2
                        load_pair(f"dgu{par}", f"sgug:{c}", gup[par], f"gup:{par}")
                        for og in range(NCH):
                            mm3(og, 0, 512,
                                gup[par][:, og * 128:(og + 1) * 128],
                                gup[par][:, 1024 + og * 128: 1024 + og * 128 + 128],
                                h3h[:, c * 512:(c + 1) * 512],
                                h3l[:, c * 512:(c + 1) * 512],
                                start=(c == 0), stop=(c == NCH - 1),
                                reads=(f"gup:{par}", f"h3:{c}h", f"h3:{c}l"))
                    for og in range(NCH):
                        bd.act(lambda e, og=og: e.activation(
                            out=sg[:, og * 512:(og + 1) * 512], in_=pb[og][:, 0:512],
                            func=AF.Silu),
                            reads=(f"pb{og}",), writes=(f"sg:{og}",))
                    for c in range(NCH):
                        par = c % 2
                        load_pair(f"dgu{par}", f"sguu:{c}", gup[par], f"gup:{par}")
                        for og in range(NCH):
                            mm3(og, 0, 512,
                                gup[par][:, og * 128:(og + 1) * 128],
                                gup[par][:, 1024 + og * 128: 1024 + og * 128 + 128],
                                h3h[:, c * 512:(c + 1) * 512],
                                h3l[:, c * 512:(c + 1) * 512],
                                start=(c == 0), stop=(c == NCH - 1),
                                reads=(f"gup:{par}", f"h3:{c}h", f"h3:{c}l"))
                    for og in range(NCH):
                        bd.dve(lambda e, og=og: e.tensor_mul(
                            sg[:, og * 512:(og + 1) * 512], sg[:, og * 512:(og + 1) * 512],
                            pb[og][:, 0:512]),
                            reads=(f"pb{og}",), writes=(f"sg:{og}",))
                    for og in range(NCH):
                        split(sg[:, og * 512:(og + 1) * 512],
                              mdh[:, og * 512:(og + 1) * 512],
                              mdl[:, og * 512:(og + 1) * 512], (f"sg:{og}",), f"md:{og}")
                    for c in range(NCH):
                        mm3(0, 0, 512,
                            sgwh[:, c:c + 1], sgwl[:, c:c + 1],
                            h3h[:, c * 512:(c + 1) * 512], h3l[:, c * 512:(c + 1) * 512],
                            start=(c == 0), stop=(c == NCH - 1),
                            reads=("sgwh", "sgwl", f"h3:{c}h", f"h3:{c}l"), prow=1)
                    bd.act(lambda e: e.activation(out=s2row[0:1, 0:512],
                                                  in_=pb[0][0:1, 0:512], func=AF.Sigmoid),
                           reads=("pb0",), writes=("ce_row",))
                    bd.pe(lambda e: e.matmul(pb[1][:, 0:512], onesrow[:],
                                             s2row[0:1, 0:512], start=True, stop=True),
                          reads=("onesrow", "ce_row"), writes=("pb1",))
                    bd.dve(lambda e: e.tensor_copy(s2b[:], pb[1][:, 0:512]),
                           reads=("pb1",), writes=("s2b",))
                    for og in range(NCH):
                        par = og % 2
                        load_pair(f"dsd{par}", f"sdown:{og}", gup[par], f"gup:{par}")
                        for o in range(NCH):
                            mm3(o, 0, 512,
                                gup[par][:, o * 128:(o + 1) * 128],
                                gup[par][:, 1024 + o * 128: 1024 + o * 128 + 128],
                                mdh[:, og * 512:(og + 1) * 512],
                                mdl[:, og * 512:(og + 1) * 512],
                                start=(og == 0), stop=(og == NCH - 1),
                                reads=(f"gup:{par}", f"md:{og}h", f"md:{og}l"))
                    for o in range(NCH):
                        bd.dve(lambda e, o=o: e.tensor_mul(
                            fin[:], pb[o][:, 0:512], s2b[:]),
                            reads=(f"pb{o}", "s2b"), writes=("ftmpD",))
                        bd.dve(lambda e, o=o: e.tensor_add(
                            fin[:], fin[:], routed[:, o * 512:(o + 1) * 512]),
                            reads=(f"routed:{o}",), writes=("ftmpD",))
                        bd.dve(lambda e, o=o, H0=H0: e.tensor_add(
                            fin[:], fin[:], X[:, o * TOWN + H0: o * TOWN + H0 + 512]),
                            reads=(f"X:{o}",), writes=("ftmpD",))
                        bd.pool(lambda e, o=o, H0=H0: e.dma_start(
                            out=out_h[o * 128:(o + 1) * 128, H0: H0 + 512],
                            in_=fin[:]),
                            reads=("ftmpD",), dma="dout")

        bd.op("pool", lambda e: e.wait_ge(sems["dout"], bd.dma_count.get("dout", 0)))

        bd.finalize()
        with nc.Block() as blk:
            @blk.tensor
            def _(eng):
                bd.replay("pe", eng, sems)

            @blk.vector
            def _(eng):
                bd.replay("dve", eng, sems)

            @blk.scalar
            def _(eng):
                bd.replay("act", eng, sems)

            @blk.gpsimd
            def _(eng):
                bd.replay("pool", eng, sems)

            @blk.sync
            def _(eng):
                bd.replay("sp", eng, sems)

    return nc


def kernel(**inputs):
    x = np.asarray(inputs["L0_attn_x"], dtype=np.float32)
    blob = make_wblob(inputs)
    nc = build_program()
    in_maps = []
    for core in range(8):
        b, half = core // 2, core % 2
        xt = np.ascontiguousarray(x[b].T)
        if half == 1:
            xt = np.ascontiguousarray(np.concatenate([xt[:, TOWN:], xt[:, :TOWN]], axis=1))
        in_maps.append({"wblob": blob, "xb": xt})
    res = run_bass_kernel_spmd(nc, in_maps, core_ids=list(range(8)))
    out = np.empty((4, 2 * TOWN, D), dtype=np.float32)
    for core in range(8):
        b, half = core // 2, core % 2
        out[b, half * TOWN:(half + 1) * TOWN, :] = res.results[core]["out"].T
    return out


# version 15
# speedup vs baseline: 1.0425x; 1.0109x over previous
import numpy as np
from contextlib import ExitStack

import concourse.bass as bass
import concourse.mybir as mybir
from concourse.bass_utils import run_bass_kernel_spmd

F32 = mybir.dt.float32
BF16 = mybir.dt.bfloat16
AF = mybir.ActivationFunctionType
ALU = mybir.AluOpType
AX = mybir.AxisListType

D = 1024
NCH = 8
TOWN = 2048
HEADS = 4
DH = 256
FF = 4096
NE = 8
EPS = 1e-5
SC = 0.25

ENGINES = ("pe", "dve", "act", "pool", "sp")
DMA_SEM = {"sp": "dsp", "act": "dact", "pool": "dpool"}

_W_ORDER = ["ident", "vec", "wk", "wv", "wq", "wo", "wd", "ff1cm", "ff2",
            "router", "w1cm", "w2cm", "sgu_gate", "sgu_up", "sdown"]
_W_SIZES = {
    "ident": 128 * 128, "vec": 128 * 96,
    "wk": D * D, "wv": D * D, "wq": D * D, "wo": D * D, "wd": D * D,
    "ff1cm": D * FF, "ff2": FF * D, "router": D * 8,
    "w1cm": NE * D * D, "w2cm": NE * D * D,
    "sgu_gate": D * D, "sgu_up": D * D, "sdown": D * D,
}
_W_OFF = {}
_o = 0
for _k in _W_ORDER:
    _W_OFF[_k] = _o
    _o += _W_SIZES[_k]
NBLOB = _o

VI = {"ln1_g": 0, "ln1_b": 8, "ln2_g": 16, "ln2_b": 24,
      "bo": 32, "b_d": 40, "b_ff2": 48, "sgw": 56, "b_ff1": 64}


class BD:

    def __init__(self):
        self.ev = {e: [] for e in ENGINES}
        self.buf = {}
        self.required = set()
        self.floor = []
        self.dma_count = {}
        self.dma_tok = {}
        self.dma_last = {}

    def op(self, eng, fn, deps=(), reads=(), writes=(), writes_ext=(), dma=None):
        dl = [d for d in deps if d is not None]
        dl.extend(self.floor)
        for k in reads:
            s = self.buf.get(k)
            if s and s["w"] is not None:
                dl.append(s["w"])
        for k in writes:
            s = self.buf.get(k)
            if s:
                if s["w"] is not None:
                    dl.append(s["w"])
                dl.extend(s["r"])
        seq = len(self.ev[eng])
        h = (eng, seq, dma is not None)
        if dma is not None:
            prev = self.dma_last.get(dma)
            if prev is not None:
                dl.append(prev)
            self.dma_count[dma] = self.dma_count.get(dma, 0) + 16
            self.dma_tok[(eng, seq)] = (dma, self.dma_count[dma])
            self.dma_last[dma] = h
        for d in set(dl):
            if (d[0], d[1]) != (eng, seq):
                self.required.add(d)
        self.ev[eng].append((fn, tuple(set(dl)), dma))
        for k in reads:
            self.buf.setdefault(k, {"w": None, "r": []})["r"].append(h)
        for k in writes:
            self.buf[k] = {"w": h, "r": []}
        for k in writes_ext:
            self.buf.setdefault(k, {"w": None, "r": []})["w"] = h
            self.buf[k]["r"] = []
        return h

    def pe(self, fn, **kw):
        return self.op("pe", fn, **kw)

    def dve(self, fn, **kw):
        return self.op("dve", fn, **kw)

    def act(self, fn, **kw):
        return self.op("act", fn, **kw)

    def pool(self, fn, **kw):
        return self.op("pool", fn, **kw)

    def sp(self, fn, **kw):
        return self.op("sp", fn, **kw)

    def drain(self, keys):
        deps = []
        for k in keys:
            s = self.buf.get(k)
            if s:
                if s["w"] is not None:
                    deps.append(s["w"])
                deps.extend(s["r"])
        return deps

    def add_floor(self, keys):
        deps = self.drain(keys)
        h = self.op("dve", lambda e: e.engine_nop(), deps=deps)
        self.floor = [h]

    def finalize(self):
        self.tokval = {}
        self.incflag = {}
        for eng in ENGINES:
            c_eng = 0
            for seq, (fn, deps, dma) in enumerate(self.ev[eng]):
                if dma is not None:
                    self.tokval[(eng, seq, True)] = self.dma_tok[(eng, seq)]
                    self.incflag[(eng, seq)] = dma
                elif (eng, seq, False) in self.required:
                    c_eng += 1
                    self.tokval[(eng, seq, False)] = (eng, c_eng)
                    self.incflag[(eng, seq)] = "eng"
                else:
                    self.incflag[(eng, seq)] = None

    def replay(self, eng, engine_obj, sems):
        wm = {}
        for seq, (fn, deps, dma) in enumerate(self.ev[eng]):
            for d in sorted(deps):
                if (d[0], d[1]) == (eng, seq):
                    continue
                sem_name, val = self.tokval.get(d, (None, None))
                if sem_name is None:
                    continue
                if wm.get(sem_name, 0) < val:
                    engine_obj.wait_ge(sems[sem_name], val)
                    wm[sem_name] = val
            ins = fn(engine_obj)
            kind = self.incflag[(eng, seq)]
            if kind == "eng":
                ins.then_inc(sems[eng], 1)
            elif kind is not None:
                ins.then_inc(sems[kind], 16)


def _pack_chunk_cols(v):
    return np.ascontiguousarray(np.asarray(v, dtype=np.float32).reshape(-1, 128).T)


def make_wblob(inp):
    vec_cols = [
        _pack_chunk_cols(inp["L0_attn_ln1_g"]), _pack_chunk_cols(inp["L0_attn_ln1_b"]),
        _pack_chunk_cols(inp["L0_attn_ln2_g"]), _pack_chunk_cols(inp["L0_attn_ln2_b"]),
        _pack_chunk_cols(inp["L0_attn_bo"]), _pack_chunk_cols(inp["L0_attn_b_d"]),
        _pack_chunk_cols(inp["L0_attn_b_ff2"]),
        _pack_chunk_cols(np.asarray(inp["L0_ffn_shared_gate_w"]).ravel()),
        _pack_chunk_cols(inp["L0_attn_b_ff1"]),
    ]
    ff1 = np.asarray(inp["L0_attn_w_ff1"], dtype=np.float32)
    w1 = np.asarray(inp["L0_ffn_w1"], dtype=np.float32)
    w2 = np.asarray(inp["L0_ffn_w2"], dtype=np.float32)
    sgu = np.asarray(inp["L0_ffn_shared_gate_up"], dtype=np.float32)
    pieces = {
        "ident": np.eye(128, dtype=np.float32),
        "vec": np.concatenate(vec_cols, axis=1),
        "wk": inp["L0_attn_wk"], "wv": inp["L0_attn_wv"],
        "wq": inp["L0_attn_wq"],
        "wo": np.asarray(inp["L0_attn_wo"], dtype=np.float32).reshape(8, 128, 8, 128).transpose(2, 1, 0, 3),
        "wd": inp["L0_attn_w_d"],
        "ff1cm": ff1.reshape(8, 128, 32, 128).transpose(2, 1, 0, 3),
        "ff2": inp["L0_attn_w_ff2"],
        "router": np.asarray(inp["L0_ffn_router_w"], dtype=np.float32).reshape(8, 128, 8).transpose(1, 0, 2),
        "w1cm": w1.reshape(NE, 8, 128, 8, 128).transpose(0, 3, 2, 1, 4),
        "w2cm": w2.reshape(NE, 8, 128, 8, 128).transpose(0, 3, 2, 1, 4),
        "sgu_gate": sgu[:, :D], "sgu_up": sgu[:, D:],
        "sdown": inp["L0_ffn_shared_down"],
    }
    blob = np.empty(NBLOB, dtype=np.float32)
    for k in _W_ORDER:
        a = np.ascontiguousarray(pieces[k], dtype=np.float32).ravel()
        assert a.size == _W_SIZES[k], (k, a.size, _W_SIZES[k])
        blob[_W_OFF[k]:_W_OFF[k] + a.size] = a
    return blob


def build_program(taps=()):
    nc = bass.Bass("TRN2", target_bir_lowering=False, debug=False)
    wblob = nc.dram_tensor("wblob", [NBLOB], F32, kind="ExternalInput").ap()
    xhbm = nc.dram_tensor("xb", [D, 2 * TOWN], F32, kind="ExternalInput").ap()
    out_h = nc.dram_tensor("out", [D, TOWN], F32, kind="ExternalOutput").ap()
    tap_aps = {}
    for tname, tshape in taps:
        tap_aps[tname] = nc.dram_tensor("tap_" + tname, list(tshape), F32,
                                        kind="ExternalOutput").ap()

    def wsl(name, elem_off, p, c):
        base = _W_OFF[name] + elem_off
        return wblob[base:base + p * c].rearrange("(p c) -> p c", c=c)

    GRAN = []
    GIDX = {}
    def _reg_gran(tag, wname, src_off):
        GIDX[tag] = len(GRAN)
        GRAN.append((wname, src_off))
    for _o in range(8):
        _reg_gran(f"wo:{_o}", "wo", _o * 8 * 128 * 128)
    for _o1 in range(32):
        _reg_gran(f"ff1:{_o1}", "ff1cm", _o1 * 8 * 128 * 128)
        _reg_gran(f"ff2:{_o1}", "ff2", _o1 * 128 * 1024)
    for _c in range(8):
        _reg_gran(f"wd:{_c}", "wd", _c * 128 * 1024)
    for _e in range(8):
        for _o1 in range(8):
            _reg_gran(f"w1:{_e}:{_o1}", "w1cm", _e * D * D + _o1 * 8 * 128 * 128)
    for _e in range(8):
        for _o1 in range(8):
            _reg_gran(f"w2:{_e}:{_o1}", "w2cm", _e * D * D + _o1 * 8 * 128 * 128)
    for _c in range(8):
        _reg_gran(f"sgug:{_c}", "sgu_gate", _c * 128 * 1024)
    for _c in range(8):
        _reg_gran(f"sguu:{_c}", "sgu_up", _c * 128 * 1024)
    for _c in range(8):
        _reg_gran(f"sdown:{_c}", "sdown", _c * 128 * 1024)
    NGRAN = len(GRAN)
    wpair = nc.dram_tensor("wpair", [NGRAN * 128 * 2048], BF16, kind="Internal").ap()

    bd = BD()
    eso = ExitStack()

    with eso:
        sems = {}
        for s in ("pe", "dve", "act", "pool", "sp", "dxp",
                  "dldc", "dldx", "dldk", "dldv", "dldq", "dldo",
                  "dg10", "dg11", "df20", "df21", "dwd0", "dwd1", "dwo0", "dwo1",
                  "dw10", "dw11", "dw20", "dw21", "dgu0", "dgu1",
                  "dsd0", "dsd1", "dsd2", "dsd3", "dpwi", "dpwo", "dout"):
            sems[s] = eso.enter_context(nc.semaphore(s))
        pb = [eso.enter_context(nc.psum_tensor(f"pb{i}", [128, 512], F32))
              for i in range(8)]

        def sbo(es, name, p, c, dt=F32):
            return es.enter_context(nc.sbuf_tensor(name, [p, c], dt))

        ident = sbo(eso, "ident", 128, 128)
        vec = sbo(eso, "vec", 128, 96)
        onescol = sbo(eso, "onescol", 128, 1)
        onesrow = sbo(eso, "onesrow", 1, 128)
        X = sbo(eso, "X", 128, NCH * TOWN)
        stat0 = sbo(eso, "stat0", 1, 1536)
        srow = sbo(eso, "srow", 1, 2048)
        router_sb = sbo(eso, "router_sb", 128, 64)
        ksum = sbo(eso, "ksum", 128, 8)
        rks = sbo(eso, "rks", 128, 8)
        ctx = sbo(eso, "ctx", 128, 2048)

        MU, RHO, R0 = 0, 512, 1024

        def vcol(name, c):
            return vec[:, VI[name] + c: VI[name] + c + 1]

        bd.op("sp", lambda e: e.dma_start(out=ident[:], in_=wsl("ident", 0, 128, 128)),
              writes=("ident",), dma="dldc")
        bd.op("sp", lambda e: e.dma_start(out=vec[:], in_=wsl("vec", 0, 128, 96)),
              writes=("vec",), dma="dldc")
        bd.op("sp", lambda e: e.dma_start(out=router_sb[:], in_=wsl("router", 0, 128, 64)),
              writes=("router_sb",), dma="dldc")
        for c in range(NCH):
            bd.op("sp", lambda e, c=c: e.dma_start(
                out=X[:, c * TOWN:(c + 1) * TOWN], in_=xhbm[c * 128:(c + 1) * 128, 0:TOWN]),
                writes=(f"X:{c}",), dma="dldx")
        bd.dve(lambda e: e.memset(onescol[:], 1.0), writes=("onescol",))
        bd.dve(lambda e: e.memset(onesrow[:], 1.0), writes=("onesrow",))
        bd.dve(lambda e: e.memset(ksum[:], 0.0), writes=("ksum",))

        def ln_stats(src_fn, src_keys, xsq, xsq_key, tsz, want_r0, bank=0, soff=0):
            for c in range(NCH):
                bd.act(lambda e, c=c, src_fn=src_fn, xsq=xsq, tsz=tsz: e.activation(
                    out=xsq[:, c * tsz:(c + 1) * tsz], in_=src_fn(c), func=AF.Square),
                    reads=(src_keys[c],), writes=(f"{xsq_key}:{c}",))
            for c in range(NCH):
                bd.pe(lambda e, c=c, src_fn=src_fn, tsz=tsz, bank=bank: e.matmul(
                    pb[bank][0:1, 0:tsz], onescol[:], src_fn(c),
                    start=(c == 0), stop=(c == NCH - 1)),
                    reads=("onescol", src_keys[c]),
                    writes=((f"pb{bank}",) if c == 0 else ()),
                    writes_ext=(() if c == 0 else (f"pb{bank}",)))
            for c in range(NCH):
                bd.pe(lambda e, c=c, xsq=xsq, tsz=tsz, bank=bank: e.matmul(
                    pb[bank][32:33, 0:tsz], onescol[:], xsq[:, c * tsz:(c + 1) * tsz],
                    start=(c == 0), stop=(c == NCH - 1)),
                    reads=("onescol", f"{xsq_key}:{c}"), writes_ext=(f"pb{bank}",))
            mu = stat0[0:1, MU + soff:MU + soff + tsz]
            rho = stat0[0:1, RHO + soff:RHO + soff + tsz]
            sq = srow[0:1, 0:tsz]
            esq = srow[0:1, 256:256 + tsz]
            var = srow[0:1, 512:512 + tsz]
            av = srow[0:1, 768:768 + tsz]
            ra = srow[0:1, 1024:1024 + tsz]
            vr = srow[0:1, 1280:1280 + tsz]
            avr = srow[0:1, 1536:1536 + tsz]
            rar = srow[0:1, 1792:1792 + tsz]
            r1 = av
            r0ap = stat0[0:1, R0 + soff:R0 + soff + tsz] if want_r0 else sq
            bd.dve(lambda e, mu=mu, tsz=tsz, bank=bank: e.tensor_scalar_mul(
                mu, pb[bank][0:1, 0:tsz], 1.0 / D),
                reads=(f"pb{bank}",), writes=("s_mu" + str(soff),))
            bd.dve(lambda e, sq=sq, mu=mu: e.tensor_mul(sq, mu, mu),
                   reads=("s_mu" + str(soff),), writes=("s_sq",))
            bd.dve(lambda e, esq=esq, tsz=tsz, bank=bank: e.tensor_scalar_mul(
                esq, pb[bank][32:33, 0:tsz], 1.0 / D),
                reads=(f"pb{bank}",), writes=("s_esq",))
            bd.dve(lambda e, var=var, esq=esq, sq=sq: e.tensor_sub(var, esq, sq),
                   reads=("s_esq", "s_sq"), writes=("s_var",))
            if want_r0:
                bd.dve(lambda e, av=av, var=var: e.tensor_scalar_add(av, var, float(EPS)),
                       reads=("s_var",), writes=("s_a",))
                bd.dve(lambda e, ra=ra, av=av: e.reciprocal(ra, av),
                       reads=("s_a",), writes=("s_ra",))
                bd.act(lambda e, r0ap=r0ap, ra=ra: e.activation(
                    out=r0ap, in_=ra, func=AF.Sqrt),
                    reads=("s_ra", "s_sq"), writes=("s_r0" + str(soff),))
            bd.dve(lambda e, avr=avr, var=var: e.tensor_scalar(
                avr, var, float(1.0 + EPS), float(EPS * EPS), ALU.mult, ALU.add),
                reads=("s_var",), writes=("s_avr",))
            bd.dve(lambda e, rar=rar, avr=avr: e.reciprocal(rar, avr),
                   reads=("s_avr",), writes=("s_rar",))
            bd.act(lambda e, rho=rho, rar=rar: e.activation(
                out=rho, in_=rar, func=AF.Sqrt),
                reads=("s_rar",), writes=("s_rho" + str(soff),))

        def bcast(bank, colo, width, row_ap, row_key, start, stop):
            bd.pe(lambda e, bank=bank, colo=colo, width=width, row_ap=row_ap: e.matmul(
                pb[bank][:, colo:colo + width], onesrow[:], row_ap, start=start, stop=stop),
                reads=("onesrow", row_key),
                writes=((f"pb{bank}",) if start else ()),
                writes_ext=(() if start else (f"pb{bank}",)))

        def mm3(bank, cs, ce_, lh, ll, rh, rl, start, stop, reads, prow=128):
            key = f"pb{bank}"
            bd.pe(lambda e, lh=lh, rh=rh, bank=bank, cs=cs, ce_=ce_, start=start,
                  prow=prow: e.matmul(
                pb[bank][0:prow, cs:ce_], lh, rh, start=start, stop=False),
                reads=reads, writes=((key,) if start else ()),
                writes_ext=(() if start else (key,)))
            bd.pe(lambda e, lh=lh, rl=rl, bank=bank, cs=cs, ce_=ce_, prow=prow: e.matmul(
                pb[bank][0:prow, cs:ce_], lh, rl, start=False, stop=False),
                reads=reads, writes_ext=(key,))
            bd.pe(lambda e, ll=ll, rh=rh, bank=bank, cs=cs, ce_=ce_, stop=stop,
                  prow=prow: e.matmul(
                pb[bank][0:prow, cs:ce_], ll, rh, start=False, stop=stop),
                reads=reads, writes_ext=(key,))

        def split(src_ap, hi_ap, lo_ap, reads, wkey):
            bd.act(lambda e, src_ap=src_ap, hi_ap=hi_ap: e.activation(
                out=hi_ap, in_=src_ap, func=AF.Copy),
                reads=reads, writes=(wkey + "h",))
            bd.dve(lambda e, src_ap=src_ap, hi_ap=hi_ap, lo_ap=lo_ap: e.tensor_sub(
                lo_ap, src_ap, hi_ap),
                reads=reads + (wkey + "h",), writes=(wkey + "l",))

        def load_split(stream, wname, granule_off, cols, wtmp, wtmp_key, hi_t, lo_t,
                       dst_off, wkey, eng="sp", woff=0):
            wkeys = (wtmp_key,) if isinstance(wtmp_key, str) else tuple(wtmp_key)
            bd.op(eng, lambda e, granule_off=granule_off, cols=cols, woff=woff: e.dma_start(
                out=wtmp[:, woff:woff + cols], in_=wsl(wname, granule_off, 128, cols)),
                writes=wkeys, dma=stream)
            bd.act(lambda e, cols=cols, dst_off=dst_off, woff=woff: e.activation(
                out=hi_t[:, dst_off:dst_off + cols], in_=wtmp[:, woff:woff + cols],
                func=AF.Copy),
                reads=wkeys, writes=(wkey + "h",))
            bd.dve(lambda e, cols=cols, dst_off=dst_off, woff=woff: e.tensor_sub(
                lo_t[:, dst_off:dst_off + cols], wtmp[:, woff:woff + cols],
                hi_t[:, dst_off:dst_off + cols]),
                reads=wkeys + (wkey + "h",), writes=(wkey + "l",))

        def load_pair(stream, tag, slot, slot_key, eng="sp"):
            gi = GIDX[tag]
            base = gi * 128 * 2048
            src2 = wpair[base:base + 128 * 2048].rearrange("(h p c) -> p h c", h=2, c=1024)
            bd.op(eng, lambda e, src2=src2: e.dma_start(out=slot[:], in_=src2),
                  reads=(f"pw:{gi}",), writes=(slot_key,), dma=stream)

        def emit_pair_build(lo_g, hi_g, wtmps, pslots):
            for gi in range(lo_g, hi_g):
                wtmp, wtmp_key = wtmps[gi % 2]
                pslot, pslot_key = pslots[gi % 2]
                wname, src_off = GRAN[gi]
                base = gi * 128 * 2048
                hi_hbm = wpair[base:base + 128 * 1024].rearrange("(p c) -> p c", c=1024)
                lo_hbm = wpair[base + 128 * 1024:base + 128 * 2048].rearrange(
                    "(p c) -> p c", c=1024)
                bd.op("sp", lambda e, wname=wname, src_off=src_off, wtmp=wtmp: e.dma_start(
                    out=wtmp[:], in_=wsl(wname, src_off, 128, 1024)),
                    writes=(wtmp_key,), dma="dpwi")
                bd.act(lambda e, wtmp=wtmp, pslot=pslot: e.activation(
                    out=pslot[:, 0:1024], in_=wtmp[:], func=AF.Copy),
                    reads=(wtmp_key,), writes=(pslot_key + "h",))
                bd.dve(lambda e, wtmp=wtmp, pslot=pslot: e.tensor_sub(
                    pslot[:, 1024:2048], wtmp[:], pslot[:, 0:1024]),
                    reads=(wtmp_key, pslot_key + "h"), writes=(pslot_key + "l",))
                bd.op("pool", lambda e, hi_hbm=hi_hbm, pslot=pslot: e.dma_start(
                    out=hi_hbm, in_=pslot[:, 0:1024]),
                    reads=(pslot_key + "h",), writes=(f"pwh:{gi}",), dma="dpwo")
                bd.op("pool", lambda e, lo_hbm=lo_hbm, pslot=pslot: e.dma_start(
                    out=lo_hbm, in_=pslot[:, 1024:2048]),
                    reads=(pslot_key + "l",), writes=(f"pw:{gi}",), dma="dpwo")

        esa = ExitStack()
        with esa:
            wkh = sbo(esa, "wkh", 128, NCH * 1024, BF16)
            wkl = sbo(esa, "wkl", 128, NCH * 1024, BF16)
            wvh = sbo(esa, "wvh", 128, NCH * 1024, BF16)
            wvl = sbo(esa, "wvl", 128, NCH * 1024, BF16)
            wtmpA = sbo(esa, "wtmpA", 128, 1024)
            wtmpA2 = sbo(esa, "wtmpA2", 128, 1024)
            xp = sbo(esa, "xp", 128, NCH * 512)
            hAh = sbo(esa, "hAh", 128, NCH * 128, BF16)
            hAl = sbo(esa, "hAl", 128, NCH * 128, BF16)
            xsqA = sbo(esa, "xsqA", 128, NCH * 128)
            diffA = sbo(esa, "diffA", 128, NCH * 128)
            kexp = sbo(esa, "kexp", 128, 1024)
            keh = sbo(esa, "keh", 128, 1024, BF16)
            kel = sbo(esa, "kel", 128, 1024, BF16)
            vth = sbo(esa, "vth", 128, 1024, BF16)
            vtl = sbo(esa, "vtl", 128, 1024, BF16)
            onescol_bf = sbo(esa, "onescol_bf", 128, 1, BF16)
            pslotA = sbo(esa, "pslotA", 128, 2048, BF16)
            pslotB = sbo(esa, "pslotB", 128, 2048, BF16)
            bd.dve(lambda e: e.memset(onescol_bf[:], 1.0), writes=("onescol_bf",))

            for c in range(NCH):
                load_split("dldk", "wk", c * 128 * 1024, 1024, wtmpA, "wtmpA",
                           wkh, wkl, c * 1024, f"wk:{c}")
                load_split("dldv", "wv", c * 128 * 1024, 1024, wtmpA2, "wtmpA2",
                           wvh, wvl, c * 1024, f"wv:{c}")

            for g in range(8):
                own = g < 4
                if not own:
                    for c in range(NCH):
                        bd.op("sp", lambda e, c=c, g=g: e.dma_start(
                            out=xp[:, c * 512:(c + 1) * 512],
                            in_=xhbm[c * 128:(c + 1) * 128, g * 512:(g + 1) * 512]),
                            writes=(f"xp:{c}",), dma="dxp")
                for s4 in range(4):
                    si_ = g * 4 + s4

                    def src(c, s4=s4, g=g, own=own):
                        if own:
                            o = g * 512 + s4 * 128
                            return X[:, c * TOWN + o: c * TOWN + o + 128]
                        return xp[:, c * 512 + s4 * 128: c * 512 + s4 * 128 + 128]

                    skeys = tuple((f"X:{c}" if own else f"xp:{c}") for c in range(NCH))
                    ln_stats(src, skeys, xsqA, "xsqA", 128, want_r0=False)
                    bcast(1, 0, 128, stat0[0:1, MU:MU + 128], "s_mu0", True, False)
                    bcast(1, 128, 128, stat0[0:1, RHO:RHO + 128], "s_rho0", False, True)
                    for c in range(NCH):
                        bd.dve(lambda e, c=c, src=src: e.tensor_sub(
                            diffA[:, c * 128:(c + 1) * 128], src(c), pb[1][:, 0:128]),
                            reads=(skeys[c], "pb1"), writes=(f"diffA:{c}",))
                    for c in range(NCH):
                        bd.dve(lambda e, c=c: e.tensor_mul(
                            diffA[:, c * 128:(c + 1) * 128], diffA[:, c * 128:(c + 1) * 128],
                            pb[1][:, 128:256]),
                            reads=("pb1",), writes=(f"diffA:{c}",))
                    for c in range(NCH):
                        bd.dve(lambda e, c=c: e.tensor_scalar(
                            diffA[:, c * 128:(c + 1) * 128], diffA[:, c * 128:(c + 1) * 128],
                            vcol("ln1_g", c), vcol("ln1_b", c), ALU.mult, ALU.add),
                            reads=("vec",), writes=(f"diffA:{c}",))
                    for c in range(NCH):
                        split(diffA[:, c * 128:(c + 1) * 128], hAh[:, c * 128:(c + 1) * 128],
                              hAl[:, c * 128:(c + 1) * 128], (f"diffA:{c}",), f"hA:{c}")
                    for c in range(NCH):
                        for hc in range(2):
                            mm3(2 + hc, 0, 512,
                                hAh[:, c * 128:(c + 1) * 128], hAl[:, c * 128:(c + 1) * 128],
                                wkh[:, c * 1024 + hc * 512: c * 1024 + hc * 512 + 512],
                                wkl[:, c * 1024 + hc * 512: c * 1024 + hc * 512 + 512],
                                start=(c == 0), stop=(c == NCH - 1),
                                reads=(f"hA:{c}h", f"hA:{c}l", f"wk:{c}h", f"wk:{c}l"))
                    emit_pair_build(si_ * 8, min(si_ * 8 + 2, NGRAN),
                                    ((wtmpA, "wtmpA"), (wtmpA2, "wtmpA2")),
                                    ((pslotA, "psA"), (pslotB, "psB")))
                    for hc in range(2):
                        bd.act(lambda e, hc=hc: e.activation(
                            out=kexp[:, hc * 512:(hc + 1) * 512], in_=pb[2 + hc][:, 0:512],
                            func=AF.Exp, scale=float(SC)),
                            reads=(f"pb{2 + hc}",), writes=(f"kexp:{hc}",))
                    for hc in range(2):
                        split(kexp[:, hc * 512:(hc + 1) * 512],
                              keh[:, hc * 512:(hc + 1) * 512],
                              kel[:, hc * 512:(hc + 1) * 512],
                              (f"kexp:{hc}",), f"kexp:{hc}")
                    for c in range(NCH):
                        for hc in range(2):
                            mm3(2 + hc, 0, 512,
                                hAh[:, c * 128:(c + 1) * 128], hAl[:, c * 128:(c + 1) * 128],
                                wvh[:, c * 1024 + hc * 512: c * 1024 + hc * 512 + 512],
                                wvl[:, c * 1024 + hc * 512: c * 1024 + hc * 512 + 512],
                                start=(c == 0), stop=(c == NCH - 1),
                                reads=(f"hA:{c}h", f"hA:{c}l", f"wv:{c}h", f"wv:{c}l"))
                    emit_pair_build(si_ * 8 + 2, min(si_ * 8 + 4, NGRAN),
                                    ((wtmpA, "wtmpA"), (wtmpA2, "wtmpA2")),
                                    ((pslotA, "psA"), (pslotB, "psB")))
                    for hc in range(2):
                        split(pb[2 + hc][:, 0:512], vth[:, hc * 512:(hc + 1) * 512],
                              vtl[:, hc * 512:(hc + 1) * 512],
                              (f"pb{2 + hc}",), f"vtm:{hc}")
                    emit_pair_build(si_ * 8 + 4, min(si_ * 8 + 6, NGRAN),
                                    ((wtmpA, "wtmpA"), (wtmpA2, "wtmpA2")),
                                    ((pslotA, "psA"), (pslotB, "psB")))
                    for j in range(8):
                        head, ec = j // 2, j % 2
                        bank, colo = 4 + j // 2, (j % 2) * 256
                        first = (s4 == 0 and colo == 0)
                        last = (s4 == 3 and colo == 256)
                        ks = head * 256 + ec * 128
                        mm3(bank, colo, colo + 256,
                            keh[:, ks:ks + 128], kel[:, ks:ks + 128],
                            vth[:, head * 256: head * 256 + 256],
                            vtl[:, head * 256: head * 256 + 256],
                            start=first, stop=last,
                            reads=(f"kexp:{ks // 512}h", f"kexp:{ks // 512}l",
                                   f"vtm:{(head * 256) // 512}h",
                                   f"vtm:{(head * 256) // 512}l"))
                    for j in range(8):
                        head, ec = j // 2, j % 2
                        ks = head * 256 + ec * 128
                        bd.pe(lambda e, j=j, ks=ks: e.matmul(
                            pb[1][:, j:j + 1], keh[:, ks:ks + 128], onescol_bf[:],
                            start=(j == 0), stop=False),
                            reads=(f"kexp:{ks // 512}h", "onescol_bf"),
                            writes=(("pb1",) if j == 0 else ()),
                            writes_ext=(() if j == 0 else ("pb1",)))
                        bd.pe(lambda e, j=j, ks=ks: e.matmul(
                            pb[1][:, j:j + 1], kel[:, ks:ks + 128], onescol_bf[:],
                            start=False, stop=(j == 7)),
                            reads=(f"kexp:{ks // 512}l", "onescol_bf"),
                            writes_ext=("pb1",))
                    bd.dve(lambda e: e.tensor_add(ksum[:, 0:8], ksum[:, 0:8], pb[1][:, 0:8]),
                           reads=("pb1",), writes=("ksum",))
                    si = g * 4 + s4
                    emit_pair_build(si * 8 + 6, min(si * 8 + 8, NGRAN),
                                    ((wtmpA, "wtmpA"), (wtmpA2, "wtmpA2")),
                                    ((pslotA, "psA"), (pslotB, "psB")))
                for j in range(8):
                    bank, colo = 4 + j // 2, (j % 2) * 256
                    if g == 0:
                        bd.dve(lambda e, j=j, bank=bank, colo=colo: e.tensor_copy(
                            ctx[:, j * 256:(j + 1) * 256], pb[bank][:, colo:colo + 256]),
                            reads=(f"pb{bank}",), writes=(f"ctx:{j}",))
                    else:
                        bd.dve(lambda e, j=j, bank=bank, colo=colo: e.tensor_add(
                            ctx[:, j * 256:(j + 1) * 256], ctx[:, j * 256:(j + 1) * 256],
                            pb[bank][:, colo:colo + 256]),
                            reads=(f"pb{bank}",), writes=(f"ctx:{j}",))

            bd.dve(lambda e: e.reciprocal(rks[:], ksum[:]), reads=("ksum",), writes=("rks",))
            for j in range(8):
                bd.dve(lambda e, j=j: e.tensor_scalar_mul(
                    ctx[:, j * 256:(j + 1) * 256], ctx[:, j * 256:(j + 1) * 256],
                    rks[:, j:j + 1]),
                    reads=("rks",), writes=(f"ctx:{j}",))
            if "ctx" in tap_aps:
                bd.pool(lambda e: e.dma_start(out=tap_aps["ctx"], in_=ctx[:]),
                        reads=tuple(f"ctx:{j}" for j in range(8)), dma="dout")

            bd.add_floor(
                [f"wk:{c}{s}" for c in range(NCH) for s in ("h", "l")]
                + [f"wv:{c}{s}" for c in range(NCH) for s in ("h", "l")]
                + [f"xp:{c}" for c in range(NCH)]
                + [f"hA:{c}{s}" for c in range(NCH) for s in ("h", "l")]
                + [f"xsqA:{c}" for c in range(NCH)] + [f"diffA:{c}" for c in range(NCH)]
                + ["wtmpA", "wtmpA2", "onescol_bf", "psAh", "psAl", "psBh", "psBl"]
                + [f"kexp:{h}{s}" for h in range(2) for s in ("", "h", "l")]
                + [f"vtm:{h}{s}" for h in range(2) for s in ("h", "l")])

        esb = ExitStack()
        with esb:
            wq_sb = sbo(esb, "wq_sb", 128, NCH * 1024)
            B1 = sbo(esb, "B1", 128, 2048)
            B2 = sbo(esb, "B2", 128, 2048)
            B3 = sbo(esb, "B3", 128, 2048)
            B4 = sbo(esb, "B4", 128, 2048)
            qt = sbo(esb, "qt", 128, 16)
            P1h = sbo(esb, "P1h", 128, 2048, BF16)
            P1l = sbo(esb, "P1l", 128, 2048, BF16)
            wop = [sbo(esb, f"wop{i}", 128, 2048, BF16) for i in range(2)]
            g1p = [sbo(esb, f"g1p{i}", 128, 2048, BF16) for i in range(2)]
            f2p = [sbo(esb, f"f2p{i}", 128, 2048, BF16) for i in range(2)]
            wdp = [sbo(esb, f"wdp{i}", 128, 2048, BF16) for i in range(2)]
            gch = sbo(esb, "gch", 128, 512, BF16)
            gcl = sbo(esb, "gcl", 128, 512, BF16)

            for c in range(NCH):
                bd.op("sp", lambda e, c=c: e.dma_start(
                    out=wq_sb[:, c * 1024:(c + 1) * 1024],
                    in_=wsl("wq", c * 128 * 1024, 128, 1024)),
                    writes=(f"wq:{c}",), dma="dldq")

            TB = 256

            for t in range(TOWN // TB):
                T0 = t * TB

                def xsrc(c, T0=T0):
                    return X[:, c * TOWN + T0: c * TOWN + T0 + TB]

                xkeys = tuple(f"X:{c}" for c in range(NCH))
                ln_stats(xsrc, xkeys, B4, "B4", TB, want_r0=True)
                bcast(1, 0, TB, stat0[0:1, MU:MU + TB], "s_mu0", True, False)
                bcast(1, TB, TB, stat0[0:1, RHO:RHO + TB], "s_rho0", False, True)
                bcast(0, 0, TB, stat0[0:1, R0:R0 + TB], "s_r00", True, True)
                for c in range(NCH):
                    bd.dve(lambda e, c=c, xsrc=xsrc: e.tensor_sub(
                        B1[:, c * TB:(c + 1) * TB], xsrc(c), pb[1][:, 0:TB]),
                        reads=(xkeys[c], "pb1"), writes=(f"B1:{c}",))
                for c in range(NCH):
                    bd.dve(lambda e, c=c: e.tensor_mul(
                        B2[:, c * TB:(c + 1) * TB], B1[:, c * TB:(c + 1) * TB],
                        pb[1][:, TB:2 * TB]),
                        reads=("pb1", f"B1:{c}"), writes=(f"B2:{c}",))
                for c in range(NCH):
                    bd.dve(lambda e, c=c: e.tensor_scalar(
                        B2[:, c * TB:(c + 1) * TB], B2[:, c * TB:(c + 1) * TB],
                        vcol("ln1_g", c), vcol("ln1_b", c), ALU.mult, ALU.add),
                        reads=("vec",), writes=(f"B2:{c}",))
                for c in range(NCH):
                    bd.dve(lambda e, c=c: e.tensor_mul(
                        B3[:, c * TB:(c + 1) * TB], B1[:, c * TB:(c + 1) * TB],
                        pb[0][:, 0:TB]),
                        reads=("pb0", f"B1:{c}"), writes=(f"B3:{c}",))
                for s2 in range(2):
                    for c in range(NCH):
                        for hc in range(2):
                            bd.pe(lambda e, c=c, hc=hc, s2=s2: e.matmul(
                                pb[2 + hc][:, 0:512],
                                B2[:, c * TB + s2 * 128: c * TB + s2 * 128 + 128],
                                wq_sb[:, c * 1024 + hc * 512: c * 1024 + hc * 512 + 512],
                                start=(c == 0), stop=(c == NCH - 1)),
                                reads=(f"B2:{c}", f"wq:{c}"),
                                writes=((f"pb{2 + hc}",) if c == 0 else ()),
                                writes_ext=(() if c == 0 else (f"pb{2 + hc}",)))
                    for head in range(HEADS):
                        bank = 2 + head // 2
                        colo = (head % 2) * 256
                        qcell = qt[:, head * 4: head * 4 + 1]
                        qbias = qt[:, head * 4 + 1: head * 4 + 2]
                        qsum = qt[:, head * 4 + 2: head * 4 + 3]
                        qrec = qt[:, head * 4 + 3: head * 4 + 4]
                        qekey = f"B1:{s2 * 4 + head}"
                        qe_ap = B1[:, s2 * 1024 + head * 256: s2 * 1024 + head * 256 + 256]
                        bd.act(lambda e, bank=bank, colo=colo, qe_ap=qe_ap:
                               e.activation(out=qe_ap, in_=pb[bank][:, colo:colo + 256],
                                            func=AF.Exp, scale=float(SC)),
                               reads=(f"pb{bank}",), writes=(qekey,))
                        bd.dve(lambda e, qsum=qsum, qe_ap=qe_ap: e.reduce_sum(
                            qsum, qe_ap, axis=AX.X),
                            reads=(qekey,), writes=(f"qt:{head}c",))
                        bd.dve(lambda e, qrec=qrec, qsum=qsum: e.reciprocal(qrec, qsum),
                               reads=(f"qt:{head}c",), writes=(f"qt:{head}d",))
                        bd.dve(lambda e, qe_ap=qe_ap, qrec=qrec: e.tensor_scalar_mul(
                            qe_ap, qe_ap, qrec),
                            reads=(f"qt:{head}d",), writes=(qekey,))
                    for c in range(NCH):
                        bank = 4 + c // 4
                        colo = (c % 4) * 128
                        bd.pe(lambda e, c=c, s2=s2, bank=bank, colo=colo: e.matmul(
                            pb[bank][:, colo:colo + 128],
                            B1[:, s2 * 1024 + c * 128: s2 * 1024 + c * 128 + 128],
                            ident[:], is_transpose=True,
                            start=(colo == 0), stop=(colo == 384)),
                            reads=(f"B1:{s2 * 4 + c // 2}", "ident"),
                            writes=((f"pb{bank}",) if colo == 0 else ()),
                            writes_ext=(() if colo == 0 else (f"pb{bank}",)))
                    for c in range(NCH):
                        bank = 4 + c // 4
                        colo = (c % 4) * 128
                        bd.dve(lambda e, c=c, s2=s2, bank=bank, colo=colo: e.tensor_copy(
                            B4[:, c * TB + s2 * 128: c * TB + s2 * 128 + 128],
                            pb[bank][:, colo:colo + 128]),
                            reads=(f"pb{bank}",), writes=(f"B4:{c}",))
                if t == 0 and "qsm" in tap_aps:
                    bd.pool(lambda e: e.dma_start(out=tap_aps["qsm"], in_=B4[:]),
                            reads=tuple(f"B4:{c}" for c in range(NCH)), dma="dout")
                if t == 0 and "hB" in tap_aps:
                    bd.pool(lambda e: e.dma_start(out=tap_aps["hB"], in_=B2[:]),
                            reads=tuple(f"B2:{c}" for c in range(NCH)), dma="dout")
                attn_banks = [6, 6, 7, 7, 2, 2, 3, 3]
                for fj in range(8):
                    head, fc = fj // 2, fj % 2
                    bank = attn_banks[fj]
                    colo = (fj % 2) * 256
                    for ei in range(2):
                        j2 = head * 2 + ei
                        first = (ei == 0 and colo == 0)
                        last = (ei == 1 and colo == 256)
                        bd.pe(lambda e, j2=j2, fc=fc, bank=bank, colo=colo,
                              first=first, last=last: e.matmul(
                                  pb[bank][:, colo:colo + 256],
                                  ctx[:, j2 * 256 + fc * 128: j2 * 256 + fc * 128 + 128],
                                  B4[:, j2 * 256: j2 * 256 + 256],
                                  start=first, stop=last),
                              reads=(f"ctx:{j2}", f"B4:{j2}"),
                              writes=((f"pb{bank}",) if first else ()),
                              writes_ext=(() if first else (f"pb{bank}",)))
                for fj in range(8):
                    bank = attn_banks[fj]
                    colo = (fj % 2) * 256
                    split(pb[bank][0:128, colo:colo + 256],
                          P1h[:, fj * 256:(fj + 1) * 256],
                          P1l[:, fj * 256:(fj + 1) * 256],
                          (f"pb{bank}",), f"P1:{fj}")
                for o in range(NCH):
                    par = o % 2
                    bank = 4 + par
                    load_pair(f"dwo{par}", f"wo:{o}", wop[par], f"wop:{par}")
                    for f in range(NCH):
                        mm3(bank, 0, 256,
                            wop[par][:, f * 128:(f + 1) * 128],
                            wop[par][:, 1024 + f * 128: 1024 + f * 128 + 128],
                            P1h[:, f * 256:(f + 1) * 256], P1l[:, f * 256:(f + 1) * 256],
                            start=(f == 0), stop=(f == NCH - 1),
                            reads=(f"wop:{par}", f"P1:{f}h", f"P1:{f}l"))
                    bd.dve(lambda e, o=o, bank=bank: e.scalar_tensor_tensor(
                        B2[:, o * TB:(o + 1) * TB], pb[bank][:, 0:256], vcol("bo", o),
                        B3[:, o * TB:(o + 1) * TB], ALU.add, ALU.add),
                        reads=(f"pb{bank}", "vec", f"B3:{o}"), writes=(f"B2:{o}",))
                if t == 0 and "attn" in tap_aps:
                    bd.pool(lambda e: e.dma_start(out=tap_aps["attn"], in_=B1[:]),
                            reads=tuple(f"B1:{c}" for c in range(NCH)), dma="dout")
                if t == 0 and "X1" in tap_aps:
                    bd.pool(lambda e: e.dma_start(out=tap_aps["X1"], in_=B2[:]),
                            reads=tuple(f"B2:{c}" for c in range(NCH)), dma="dout")
                def x1src(c):
                    return B2[:, c * TB:(c + 1) * TB]

                x1keys = tuple(f"B2:{c}" for c in range(NCH))
                ln_stats(x1src, x1keys, B4, "B4", TB, want_r0=True)
                bcast(1, 0, TB, stat0[0:1, MU:MU + TB], "s_mu0", True, False)
                bcast(1, TB, TB, stat0[0:1, R0:R0 + TB], "s_r00", False, True)
                for c in range(NCH):
                    bd.dve(lambda e, c=c, x1src=x1src: e.tensor_sub(
                        B3[:, c * TB:(c + 1) * TB], x1src(c), pb[1][:, 0:TB]),
                        reads=(x1keys[c], "pb1"), writes=(f"B3:{c}",))
                for c in range(NCH):
                    bd.dve(lambda e, c=c: e.tensor_mul(
                        B3[:, c * TB:(c + 1) * TB], B3[:, c * TB:(c + 1) * TB],
                        pb[1][:, TB:2 * TB]),
                        reads=("pb1",), writes=(f"B3:{c}",))
                for c in range(NCH):
                    bd.dve(lambda e, c=c: e.tensor_scalar(
                        B3[:, c * TB:(c + 1) * TB], B3[:, c * TB:(c + 1) * TB],
                        vcol("ln2_g", c), vcol("ln2_b", c), ALU.mult, ALU.add),
                        reads=("vec",), writes=(f"B3:{c}",))
                if t == 0 and "h2" in tap_aps:
                    bd.pool(lambda e: e.dma_start(out=tap_aps["h2"], in_=B3[:]),
                            reads=tuple(f"B3:{c}" for c in range(NCH)), dma="dout")
                for c in range(NCH):
                    split(B3[:, c * TB:(c + 1) * TB], P1h[:, c * TB:(c + 1) * TB],
                          P1l[:, c * TB:(c + 1) * TB], (f"B3:{c}",), f"P1:{c}")
                ff2_banks = [2, 2, 3, 3, 6, 6, 7, 7]
                for o1 in range(33):
                    par = o1 % 2
                    bank = 4 + par
                    if o1 < 32:
                        load_pair(f"dg1{par}", f"ff1:{o1}", g1p[par], f"g1:{par}")
                        for c in range(NCH):
                            mm3(bank, 0, 256,
                                g1p[par][:, c * 128:(c + 1) * 128],
                                g1p[par][:, 1024 + c * 128: 1024 + c * 128 + 128],
                                P1h[:, c * TB:(c + 1) * TB], P1l[:, c * TB:(c + 1) * TB],
                                start=(c == 0), stop=(c == NCH - 1),
                                reads=(f"g1:{par}", f"P1:{c}h", f"P1:{c}l"))
                        bd.act(lambda e, o1=o1, par=par, bank=bank: e.activation(
                            out=B1[:, par * 256: par * 256 + 256], in_=pb[bank][:, 0:256],
                            func=AF.Gelu_apprx_tanh,
                            bias=vec[:, VI["b_ff1"] + o1: VI["b_ff1"] + o1 + 1]),
                            reads=(f"pb{bank}", "vec"), writes=(f"B1:{par}",))
                        split(B1[:, par * 256: par * 256 + 256],
                              gch[:, par * 256: par * 256 + 256],
                              gcl[:, par * 256: par * 256 + 256],
                              (f"B1:{par}",), f"gc:{par}")
                        load_pair(f"df2{par}", f"ff2:{o1}", f2p[par], f"f2:{par}")
                    if o1 > 0:
                        o1p = o1 - 1
                        parp = o1p % 2
                        for o2 in range(NCH):
                            bank2 = ff2_banks[o2]
                            colo2 = (o2 % 2) * 256
                            first = (o1p == 0 and colo2 == 0)
                            last = (o1p == 31 and colo2 == 256)
                            mm3(bank2, colo2, colo2 + 256,
                                f2p[parp][:, o2 * 128:(o2 + 1) * 128],
                                f2p[parp][:, 1024 + o2 * 128: 1024 + o2 * 128 + 128],
                                gch[:, parp * 256: parp * 256 + 256],
                                gcl[:, parp * 256: parp * 256 + 256],
                                start=first, stop=last,
                                reads=(f"f2:{parp}", f"gc:{parp}h", f"gc:{parp}l"))
                for o2 in range(NCH):
                    bank2 = ff2_banks[o2]
                    colo2 = (o2 % 2) * 256
                    bd.dve(lambda e, o2=o2, bank2=bank2, colo2=colo2: e.scalar_tensor_tensor(
                        B4[:, o2 * TB:(o2 + 1) * TB], pb[bank2][:, colo2:colo2 + 256],
                        vcol("b_ff2", o2), B2[:, o2 * TB:(o2 + 1) * TB], ALU.add, ALU.add),
                        reads=(f"pb{bank2}", "vec", f"B2:{o2}"), writes=(f"B4:{o2}",))
                if t == 0 and "X2" in tap_aps:
                    bd.pool(lambda e: e.dma_start(out=tap_aps["X2"], in_=B4[:]),
                            reads=tuple(f"B4:{c}" for c in range(NCH)), dma="dout")
                for c in range(NCH):
                    split(B4[:, c * TB:(c + 1) * TB], P1h[:, c * TB:(c + 1) * TB],
                          P1l[:, c * TB:(c + 1) * TB], (f"B4:{c}",), f"P1:{c}")
                wd_banks = [0, 0, 1, 1, 4, 4, 5, 5]
                for c in range(NCH):
                    par = c % 2
                    load_pair(f"dwd{par}", f"wd:{c}", wdp[par], f"wdp:{par}")
                    for o in range(NCH):
                        bank = wd_banks[o]
                        colo = (o % 2) * 256
                        first = (c == 0 and colo == 0)
                        last = (c == NCH - 1 and colo == 256)
                        mm3(bank, colo, colo + 256,
                            wdp[par][:, o * 128:(o + 1) * 128],
                            wdp[par][:, 1024 + o * 128: 1024 + o * 128 + 128],
                            P1h[:, c * TB:(c + 1) * TB], P1l[:, c * TB:(c + 1) * TB],
                            start=first, stop=last,
                            reads=(f"wdp:{par}", f"P1:{c}h", f"P1:{c}l"))
                for o in range(NCH):
                    bank = wd_banks[o]
                    colo = (o % 2) * 256
                    bd.dve(lambda e, o=o, bank=bank, colo=colo, T0=T0: e.scalar_tensor_tensor(
                        X[:, o * TOWN + T0: o * TOWN + T0 + TB],
                        pb[bank][:, colo:colo + 256], vcol("b_d", o),
                        X[:, o * TOWN + T0: o * TOWN + T0 + TB], ALU.add, ALU.add),
                        reads=(f"pb{bank}", "vec"), writes=(f"X:{o}",))

            if "Y" in tap_aps:
                for c in range(NCH):
                    bd.pool(lambda e, c=c: e.dma_start(
                        out=tap_aps["Y"][c * 128:(c + 1) * 128, :],
                        in_=X[:, c * TOWN:(c + 1) * TOWN]),
                        reads=(f"X:{c}",), dma="dout")

            bd.add_floor(
                [f"wq:{c}" for c in range(NCH)] + [f"wo:{c}" for c in range(NCH)]
                + [f"B1:{c}" for c in range(NCH)] + [f"B2:{c}" for c in range(NCH)]
                + [f"B3:{c}" for c in range(NCH)] + [f"B4:{c}" for c in range(NCH)]
                + [f"P1:{c}{s}" for c in range(NCH) for s in ("h", "l")]
                + [f"g1:{p}" for p in range(2)] + [f"f2:{p}" for p in range(2)]
                + ["wdp:0", "wdp:1", "wop:0", "wop:1"]
                + [f"gc:{p}{s}" for p in range(2) for s in ("h", "l")]
                + [f"qt:{h}{x}" for h in range(HEADS) for x in "abcd"])

        esd = ExitStack()
        with esd:
            h3h = sbo(esd, "h3h", 128, NCH * 512, BF16)
            h3l = sbo(esd, "h3l", 128, NCH * 512, BF16)
            routed = sbo(esd, "routed", 128, NCH * 512)
            comb = sbo(esd, "comb", 128, 32)
            prb = sbo(esd, "prb", 128, 8)
            mtmp = sbo(esd, "mtmp", 128, 8)
            rtmp = sbo(esd, "rtmp", 128, 8)
            routerh = sbo(esd, "routerh", 128, 64, BF16)
            routerl = sbo(esd, "routerl", 128, 64, BF16)
            sgwh = sbo(esd, "sgwh", 128, 8, BF16)
            sgwl = sbo(esd, "sgwl", 128, 8, BF16)
            ce_row = sbo(esd, "ce_row", 1, 512)
            s2row = ce_row
            s2b = sbo(esd, "s2b", 128, 512)
            ftmpD = sbo(esd, "ftmpD", 128, 512)
            fin = ftmpD
            heh = sbo(esd, "heh", 128, NCH * 512, BF16)
            hel = sbo(esd, "hel", 128, NCH * 512, BF16)
            w1p = [sbo(esd, f"w1p{i}", 128, 2048, BF16) for i in range(2)]
            w2p = [sbo(esd, f"w2p{i}", 128, 2048, BF16) for i in range(2)]
            sg = sbo(esd, "sg", 128, NCH * 512)
            mdh = sbo(esd, "mdh", 128, NCH * 512, BF16)
            mdl = sbo(esd, "mdl", 128, NCH * 512, BF16)
            gup = [sbo(esd, f"gup{i}", 128, 2048, BF16) for i in range(2)]
            split(router_sb[:], routerh[:], routerl[:], ("router_sb",), "router")
            split(vec[:, VI["sgw"]:VI["sgw"] + 8], sgwh[:], sgwl[:], ("vec",), "sgw")

            for hf in range(4):
                H0 = hf * 512
                bd.add_floor([f"routed:{o}" for o in range(NCH)]
                             + [f"xsqD:{c}" for c in range(NCH)]
                             + [f"d3:{c}" for c in range(NCH)])
                for tt in range(2):
                    T0 = H0 + tt * 256

                    def ysrc(c, T0=T0):
                        return X[:, c * TOWN + T0: c * TOWN + T0 + 256]

                    ykeys = tuple(f"X:{c}" for c in range(NCH))
                    sb_bank = 0 if tt == 0 else 2
                    bc_bank = 1 if tt == 0 else 3
                    soff = tt * 256
                    ln_stats(ysrc, ykeys, routed, "xsqD", 256, want_r0=True,
                             bank=sb_bank, soff=soff)
                    bcast(bc_bank, 0, 256, stat0[0:1, MU + soff:MU + soff + 256],
                          "s_mu" + str(soff), True, False)
                    bcast(bc_bank, 256, 256, stat0[0:1, R0 + soff:R0 + soff + 256],
                          "s_r0" + str(soff), False, True)
                    for c in range(NCH):
                        bd.dve(lambda e, c=c, tt=tt, ysrc=ysrc, bc_bank=bc_bank: e.tensor_sub(
                            routed[:, 2048 + c * 256: 2048 + c * 256 + 256],
                            ysrc(c), pb[bc_bank][:, 0:256]),
                            reads=(ykeys[c], f"pb{bc_bank}"), writes=(f"d3:{c}",))
                    for c in range(NCH):
                        bd.dve(lambda e, c=c, bc_bank=bc_bank: e.tensor_mul(
                            routed[:, 2048 + c * 256: 2048 + c * 256 + 256],
                            routed[:, 2048 + c * 256: 2048 + c * 256 + 256],
                            pb[bc_bank][:, 256:512]),
                            reads=(f"pb{bc_bank}",), writes=(f"d3:{c}",))
                    for c in range(NCH):
                        split(routed[:, 2048 + c * 256: 2048 + c * 256 + 256],
                              h3h[:, c * 512 + tt * 256: c * 512 + tt * 256 + 256],
                              h3l[:, c * 512 + tt * 256: c * 512 + tt * 256 + 256],
                              (f"d3:{c}",), f"h3:{c}")
                for sub in range(4):
                    rbank = 2 if sub % 2 == 0 else 0
                    for c in range(NCH):
                        mm3(rbank, 0, 8,
                            h3h[:, c * 512 + sub * 128: c * 512 + sub * 128 + 128],
                            h3l[:, c * 512 + sub * 128: c * 512 + sub * 128 + 128],
                            routerh[:, c * 8:(c + 1) * 8], routerl[:, c * 8:(c + 1) * 8],
                            start=(c == 0), stop=(c == NCH - 1),
                            reads=(f"h3:{c}h", f"h3:{c}l", "routerh", "routerl"))
                    bd.dve(lambda e, rbank=rbank: e.reduce_max(
                        rtmp[:, 0:1], pb[rbank][:, 0:8], axis=AX.X),
                        reads=(f"pb{rbank}",), writes=("rt0",))
                    bd.dve(lambda e: e.tensor_scalar_mul(rtmp[:, 1:2], rtmp[:, 0:1], -1.0),
                           reads=("rt0",), writes=("rt1",))
                    bd.act(lambda e, rbank=rbank: e.activation(
                        out=prb[:], in_=pb[rbank][:, 0:8],
                        func=AF.Exp, bias=rtmp[:, 1:2]),
                        reads=(f"pb{rbank}", "rt1"), writes=("prb",))
                    bd.dve(lambda e: e.reduce_sum(rtmp[:, 2:3], prb[:], axis=AX.X),
                           reads=("prb",), writes=("rt2",))
                    bd.dve(lambda e: e.reciprocal(rtmp[:, 3:4], rtmp[:, 2:3]),
                           reads=("rt2",), writes=("rt3",))
                    bd.dve(lambda e: e.tensor_scalar_mul(prb[:], prb[:], rtmp[:, 3:4]),
                           reads=("rt3",), writes=("prb",))
                    bd.dve(lambda e: e.reduce_max(rtmp[:, 4:5], prb[:], axis=AX.X),
                           reads=("prb",), writes=("rt4",))
                    bd.dve(lambda e: e.tensor_single_scalar(
                        mtmp[:], prb[:], rtmp[:, 4:5], ALU.is_ge),
                        reads=("prb", "rt4"), writes=("mtmp",))
                    bd.dve(lambda e: e.scalar_tensor_tensor(
                        mtmp[:], mtmp[:], -2.0, prb[:], ALU.mult, ALU.add),
                        reads=("prb",), writes=("mtmp",))
                    bd.dve(lambda e: e.reduce_max(rtmp[:, 5:6], mtmp[:], axis=AX.X),
                           reads=("mtmp",), writes=("rt5",))
                    bd.dve(lambda e, sub=sub: e.tensor_single_scalar(
                        comb[:, sub * 8:(sub + 1) * 8], prb[:], rtmp[:, 5:6], ALU.is_ge),
                        reads=("prb", "rt5"), writes=(f"comb:{sub}",))
                    bd.dve(lambda e, sub=sub: e.tensor_mul(
                        comb[:, sub * 8:(sub + 1) * 8], comb[:, sub * 8:(sub + 1) * 8],
                        prb[:]),
                        reads=("prb",), writes=(f"comb:{sub}",))
                xsq_drain = bd.drain([f"xsqD:{c}" for c in range(NCH)]
                                     + [f"d3:{c}" for c in range(NCH)])
                if True:
                    for en in range(NE):
                        for sub in range(4):
                            colo = sub * 128
                            bd.pe(lambda e, en=en, sub=sub, colo=colo: e.matmul(
                                pb[3][0:1, colo:colo + 128],
                                comb[:, sub * 8 + en: sub * 8 + en + 1], ident[:],
                                is_transpose=True,
                                start=(colo == 0), stop=(colo == 384)),
                                reads=(f"comb:{sub}", "ident"),
                                writes=(("pb3",) if colo == 0 else ()),
                                writes_ext=(() if colo == 0 else ("pb3",)))
                        bd.act(lambda e: e.activation(
                            out=ce_row[0:1, 0:512], in_=pb[3][0:1, 0:512], func=AF.Copy),
                            reads=("pb3",), writes=("ce_row",))
                        bd.pe(lambda e: e.matmul(
                            pb[6][:, 0:512], onesrow[:], ce_row[0:1, 0:512],
                            start=True, stop=True),
                            reads=("onesrow", "ce_row"), writes=("pb6",))
                        for o1 in range(NCH):
                            par = o1 % 2
                            load_pair(f"dw1{par}", f"w1:{en}:{o1}", w1p[par], f"w1p:{par}")
                            for c in range(NCH):
                                mm3(4 + par, 0, 512,
                                    w1p[par][:, c * 128:(c + 1) * 128],
                                    w1p[par][:, 1024 + c * 128: 1024 + c * 128 + 128],
                                    h3h[:, c * 512:(c + 1) * 512],
                                    h3l[:, c * 512:(c + 1) * 512],
                                    start=(c == 0), stop=(c == NCH - 1),
                                    reads=(f"w1p:{par}", f"h3:{c}h", f"h3:{c}l"))
                            bd.act(lambda e, par=par: e.activation(
                                out=ftmpD[:], in_=pb[4 + par][:, 0:512], func=AF.Silu),
                                reads=(f"pb{4 + par}",), writes=("ftmpD",))
                            bd.dve(lambda e: e.tensor_mul(ftmpD[:], ftmpD[:],
                                                          pb[6][:, 0:512]),
                                   reads=("pb6",), writes=("ftmpD",))
                            split(ftmpD[:], heh[:, o1 * 512:(o1 + 1) * 512],
                                  hel[:, o1 * 512:(o1 + 1) * 512], ("ftmpD",), f"he:{o1}")
                        for o2 in range(NCH):
                            par = o2 % 2
                            load_pair(f"dw2{par}", f"w2:{en}:{o2}", w2p[par], f"w2p:{par}")
                            for c in range(NCH):
                                mm3(2 + par, 0, 512,
                                    w2p[par][:, c * 128:(c + 1) * 128],
                                    w2p[par][:, 1024 + c * 128: 1024 + c * 128 + 128],
                                    heh[:, c * 512:(c + 1) * 512],
                                    hel[:, c * 512:(c + 1) * 512],
                                    start=(c == 0), stop=(c == NCH - 1),
                                    reads=(f"w2p:{par}", f"he:{c}h", f"he:{c}l"))
                            if en == 0:
                                bd.dve(lambda e, o2=o2, par=par: e.tensor_copy(
                                    routed[:, o2 * 512:(o2 + 1) * 512],
                                    pb[2 + par][:, 0:512]),
                                    deps=xsq_drain,
                                    reads=(f"pb{2 + par}",), writes=(f"routed:{o2}",))
                            else:
                                bd.dve(lambda e, o2=o2, par=par: e.tensor_add(
                                    routed[:, o2 * 512:(o2 + 1) * 512],
                                    routed[:, o2 * 512:(o2 + 1) * 512],
                                    pb[2 + par][:, 0:512]),
                                    reads=(f"pb{2 + par}",), writes=(f"routed:{o2}",))
                if True:
                    for c in range(NCH):
                        par = c % 2
                        load_pair(f"dgu{par}", f"sgug:{c}", gup[par], f"gup:{par}")
                        for og in range(NCH):
                            mm3(og, 0, 512,
                                gup[par][:, og * 128:(og + 1) * 128],
                                gup[par][:, 1024 + og * 128: 1024 + og * 128 + 128],
                                h3h[:, c * 512:(c + 1) * 512],
                                h3l[:, c * 512:(c + 1) * 512],
                                start=(c == 0), stop=(c == NCH - 1),
                                reads=(f"gup:{par}", f"h3:{c}h", f"h3:{c}l"))
                    for og in range(NCH):
                        bd.act(lambda e, og=og: e.activation(
                            out=sg[:, og * 512:(og + 1) * 512], in_=pb[og][:, 0:512],
                            func=AF.Silu),
                            reads=(f"pb{og}",), writes=(f"sg:{og}",))
                    for c in range(NCH):
                        par = c % 2
                        load_pair(f"dgu{par}", f"sguu:{c}", gup[par], f"gup:{par}")
                        for og in range(NCH):
                            mm3(og, 0, 512,
                                gup[par][:, og * 128:(og + 1) * 128],
                                gup[par][:, 1024 + og * 128: 1024 + og * 128 + 128],
                                h3h[:, c * 512:(c + 1) * 512],
                                h3l[:, c * 512:(c + 1) * 512],
                                start=(c == 0), stop=(c == NCH - 1),
                                reads=(f"gup:{par}", f"h3:{c}h", f"h3:{c}l"))
                    for og in range(NCH):
                        bd.dve(lambda e, og=og: e.tensor_mul(
                            sg[:, og * 512:(og + 1) * 512], sg[:, og * 512:(og + 1) * 512],
                            pb[og][:, 0:512]),
                            reads=(f"pb{og}",), writes=(f"sg:{og}",))
                    for og in range(NCH):
                        split(sg[:, og * 512:(og + 1) * 512],
                              mdh[:, og * 512:(og + 1) * 512],
                              mdl[:, og * 512:(og + 1) * 512], (f"sg:{og}",), f"md:{og}")
                    for c in range(NCH):
                        mm3(0, 0, 512,
                            sgwh[:, c:c + 1], sgwl[:, c:c + 1],
                            h3h[:, c * 512:(c + 1) * 512], h3l[:, c * 512:(c + 1) * 512],
                            start=(c == 0), stop=(c == NCH - 1),
                            reads=("sgwh", "sgwl", f"h3:{c}h", f"h3:{c}l"), prow=1)
                    bd.act(lambda e: e.activation(out=s2row[0:1, 0:512],
                                                  in_=pb[0][0:1, 0:512], func=AF.Sigmoid),
                           reads=("pb0",), writes=("ce_row",))
                    bd.pe(lambda e: e.matmul(pb[1][:, 0:512], onesrow[:],
                                             s2row[0:1, 0:512], start=True, stop=True),
                          reads=("onesrow", "ce_row"), writes=("pb1",))
                    bd.dve(lambda e: e.tensor_copy(s2b[:], pb[1][:, 0:512]),
                           reads=("pb1",), writes=("s2b",))
                    for og in range(NCH):
                        par = og % 2
                        load_pair(f"dsd{par}", f"sdown:{og}", gup[par], f"gup:{par}")
                        for o in range(NCH):
                            mm3(o, 0, 512,
                                gup[par][:, o * 128:(o + 1) * 128],
                                gup[par][:, 1024 + o * 128: 1024 + o * 128 + 128],
                                mdh[:, og * 512:(og + 1) * 512],
                                mdl[:, og * 512:(og + 1) * 512],
                                start=(og == 0), stop=(og == NCH - 1),
                                reads=(f"gup:{par}", f"md:{og}h", f"md:{og}l"))
                    for o in range(NCH):
                        bd.dve(lambda e, o=o: e.tensor_mul(
                            fin[:], pb[o][:, 0:512], s2b[:]),
                            reads=(f"pb{o}", "s2b"), writes=("ftmpD",))
                        bd.dve(lambda e, o=o: e.tensor_add(
                            fin[:], fin[:], routed[:, o * 512:(o + 1) * 512]),
                            reads=(f"routed:{o}",), writes=("ftmpD",))
                        bd.dve(lambda e, o=o, H0=H0: e.tensor_add(
                            fin[:], fin[:], X[:, o * TOWN + H0: o * TOWN + H0 + 512]),
                            reads=(f"X:{o}",), writes=("ftmpD",))
                        bd.pool(lambda e, o=o, H0=H0: e.dma_start(
                            out=out_h[o * 128:(o + 1) * 128, H0: H0 + 512],
                            in_=fin[:]),
                            reads=("ftmpD",), dma="dout")

        bd.op("pool", lambda e: e.wait_ge(sems["dout"], bd.dma_count.get("dout", 0)))

        bd.finalize()
        with nc.Block() as blk:
            @blk.tensor
            def _(eng):
                bd.replay("pe", eng, sems)

            @blk.vector
            def _(eng):
                bd.replay("dve", eng, sems)

            @blk.scalar
            def _(eng):
                bd.replay("act", eng, sems)

            @blk.gpsimd
            def _(eng):
                bd.replay("pool", eng, sems)

            @blk.sync
            def _(eng):
                bd.replay("sp", eng, sems)

    return nc


def kernel(**inputs):
    x = np.asarray(inputs["L0_attn_x"], dtype=np.float32)
    blob = make_wblob(inputs)
    nc = build_program()
    in_maps = []
    for core in range(8):
        b, half = core // 2, core % 2
        xt = np.ascontiguousarray(x[b].T)
        if half == 1:
            xt = np.ascontiguousarray(np.concatenate([xt[:, TOWN:], xt[:, :TOWN]], axis=1))
        in_maps.append({"wblob": blob, "xb": xt})
    res = run_bass_kernel_spmd(nc, in_maps, core_ids=list(range(8)))
    out = np.empty((4, 2 * TOWN, D), dtype=np.float32)
    for core in range(8):
        b, half = core // 2, core % 2
        out[b, half * TOWN:(half + 1) * TOWN, :] = res.results[core]["out"].T
    return out


# version 16 (final)
# speedup vs baseline: 1.0827x; 1.0386x over previous
import numpy as np
from contextlib import ExitStack

import concourse.bass as bass
import concourse.mybir as mybir
from concourse.bass_utils import run_bass_kernel_spmd

F32 = mybir.dt.float32
BF16 = mybir.dt.bfloat16
AF = mybir.ActivationFunctionType
ALU = mybir.AluOpType
AX = mybir.AxisListType

D = 1024
NCH = 8
TOWN = 2048
HEADS = 4
DH = 256
FF = 4096
NE = 8
EPS = 1e-5
SC = 0.25

ENGINES = ("pe", "dve", "act", "pool", "sp")
DMA_SEM = {"sp": "dsp", "act": "dact", "pool": "dpool"}

_W_ORDER = ["ident", "vec", "wk", "wv", "wq", "wo", "wd", "ff1cm", "ff2",
            "router", "w1cm", "w2cm", "sgu_gate", "sgu_up", "sdown"]
_W_SIZES = {
    "ident": 128 * 128, "vec": 128 * 96,
    "wk": D * D, "wv": D * D, "wq": D * D, "wo": D * D, "wd": D * D,
    "ff1cm": D * FF, "ff2": FF * D, "router": D * 8,
    "w1cm": NE * D * D, "w2cm": NE * D * D,
    "sgu_gate": D * D, "sgu_up": D * D, "sdown": D * D,
}
_W_OFF = {}
_o = 0
for _k in _W_ORDER:
    _W_OFF[_k] = _o
    _o += _W_SIZES[_k]
NBLOB = _o

VI = {"ln1_g": 0, "ln1_b": 8, "ln2_g": 16, "ln2_b": 24,
      "bo": 32, "b_d": 40, "b_ff2": 48, "sgw": 56, "b_ff1": 64}


class BD:

    def __init__(self):
        self.ev = {e: [] for e in ENGINES}
        self.buf = {}
        self.required = set()
        self.floor = []
        self.dma_count = {}
        self.dma_tok = {}
        self.dma_last = {}

    def op(self, eng, fn, deps=(), reads=(), writes=(), writes_ext=(), dma=None):
        dl = [d for d in deps if d is not None]
        dl.extend(self.floor)
        for k in reads:
            s = self.buf.get(k)
            if s and s["w"] is not None:
                dl.append(s["w"])
        for k in writes:
            s = self.buf.get(k)
            if s:
                if s["w"] is not None:
                    dl.append(s["w"])
                dl.extend(s["r"])
        seq = len(self.ev[eng])
        h = (eng, seq, dma is not None)
        if dma is not None:
            prev = self.dma_last.get(dma)
            if prev is not None:
                dl.append(prev)
            self.dma_count[dma] = self.dma_count.get(dma, 0) + 16
            self.dma_tok[(eng, seq)] = (dma, self.dma_count[dma])
            self.dma_last[dma] = h
        for d in set(dl):
            if (d[0], d[1]) != (eng, seq):
                self.required.add(d)
        self.ev[eng].append((fn, tuple(set(dl)), dma))
        for k in reads:
            self.buf.setdefault(k, {"w": None, "r": []})["r"].append(h)
        for k in writes:
            self.buf[k] = {"w": h, "r": []}
        for k in writes_ext:
            self.buf.setdefault(k, {"w": None, "r": []})["w"] = h
            self.buf[k]["r"] = []
        return h

    def pe(self, fn, **kw):
        return self.op("pe", fn, **kw)

    def dve(self, fn, **kw):
        return self.op("dve", fn, **kw)

    def act(self, fn, **kw):
        return self.op("act", fn, **kw)

    def pool(self, fn, **kw):
        return self.op("pool", fn, **kw)

    def sp(self, fn, **kw):
        return self.op("sp", fn, **kw)

    def drain(self, keys):
        deps = []
        for k in keys:
            s = self.buf.get(k)
            if s:
                if s["w"] is not None:
                    deps.append(s["w"])
                deps.extend(s["r"])
        return deps

    def add_floor(self, keys):
        deps = self.drain(keys)
        h = self.op("dve", lambda e: e.engine_nop(), deps=deps)
        self.floor = [h]

    def finalize(self):
        self.tokval = {}
        self.incflag = {}
        for eng in ENGINES:
            c_eng = 0
            for seq, (fn, deps, dma) in enumerate(self.ev[eng]):
                if dma is not None:
                    self.tokval[(eng, seq, True)] = self.dma_tok[(eng, seq)]
                    self.incflag[(eng, seq)] = dma
                elif (eng, seq, False) in self.required:
                    c_eng += 1
                    self.tokval[(eng, seq, False)] = (eng, c_eng)
                    self.incflag[(eng, seq)] = "eng"
                else:
                    self.incflag[(eng, seq)] = None

    def replay(self, eng, engine_obj, sems):
        wm = {}
        for seq, (fn, deps, dma) in enumerate(self.ev[eng]):
            for d in sorted(deps):
                if (d[0], d[1]) == (eng, seq):
                    continue
                sem_name, val = self.tokval.get(d, (None, None))
                if sem_name is None:
                    continue
                if wm.get(sem_name, 0) < val:
                    engine_obj.wait_ge(sems[sem_name], val)
                    wm[sem_name] = val
            ins = fn(engine_obj)
            kind = self.incflag[(eng, seq)]
            if kind == "eng":
                ins.then_inc(sems[eng], 1)
            elif kind is not None:
                ins.then_inc(sems[kind], 16)


def _pack_chunk_cols(v):
    return np.ascontiguousarray(np.asarray(v, dtype=np.float32).reshape(-1, 128).T)


def make_wblob(inp):
    vec_cols = [
        _pack_chunk_cols(inp["L0_attn_ln1_g"]), _pack_chunk_cols(inp["L0_attn_ln1_b"]),
        _pack_chunk_cols(inp["L0_attn_ln2_g"]), _pack_chunk_cols(inp["L0_attn_ln2_b"]),
        _pack_chunk_cols(inp["L0_attn_bo"]), _pack_chunk_cols(inp["L0_attn_b_d"]),
        _pack_chunk_cols(inp["L0_attn_b_ff2"]),
        _pack_chunk_cols(np.asarray(inp["L0_ffn_shared_gate_w"]).ravel()),
        _pack_chunk_cols(inp["L0_attn_b_ff1"]),
    ]
    ff1 = np.asarray(inp["L0_attn_w_ff1"], dtype=np.float32)
    w1 = np.asarray(inp["L0_ffn_w1"], dtype=np.float32)
    w2 = np.asarray(inp["L0_ffn_w2"], dtype=np.float32)
    sgu = np.asarray(inp["L0_ffn_shared_gate_up"], dtype=np.float32)
    pieces = {
        "ident": np.eye(128, dtype=np.float32),
        "vec": np.concatenate(vec_cols, axis=1),
        "wk": inp["L0_attn_wk"], "wv": inp["L0_attn_wv"],
        "wq": inp["L0_attn_wq"],
        "wo": np.asarray(inp["L0_attn_wo"], dtype=np.float32).reshape(8, 128, 8, 128).transpose(2, 1, 0, 3),
        "wd": inp["L0_attn_w_d"],
        "ff1cm": ff1.reshape(8, 128, 32, 128).transpose(2, 1, 0, 3),
        "ff2": inp["L0_attn_w_ff2"],
        "router": np.asarray(inp["L0_ffn_router_w"], dtype=np.float32).reshape(8, 128, 8).transpose(1, 0, 2),
        "w1cm": w1.reshape(NE, 8, 128, 8, 128).transpose(0, 3, 2, 1, 4),
        "w2cm": w2.reshape(NE, 8, 128, 8, 128).transpose(0, 3, 2, 1, 4),
        "sgu_gate": sgu[:, :D], "sgu_up": sgu[:, D:],
        "sdown": inp["L0_ffn_shared_down"],
    }
    blob = np.empty(NBLOB, dtype=np.float32)
    for k in _W_ORDER:
        a = np.ascontiguousarray(pieces[k], dtype=np.float32).ravel()
        assert a.size == _W_SIZES[k], (k, a.size, _W_SIZES[k])
        blob[_W_OFF[k]:_W_OFF[k] + a.size] = a
    return blob


def build_program(taps=()):
    nc = bass.Bass("TRN2", target_bir_lowering=False, debug=False)
    wblob = nc.dram_tensor("wblob", [NBLOB], F32, kind="ExternalInput").ap()
    XOWN = D * TOWN
    xhbm = nc.dram_tensor("xb", [XOWN + D * TOWN], F32, kind="ExternalInput").ap()
    out_h = nc.dram_tensor("out", [D, TOWN], F32, kind="ExternalOutput").ap()
    tap_aps = {}
    for tname, tshape in taps:
        tap_aps[tname] = nc.dram_tensor("tap_" + tname, list(tshape), F32,
                                        kind="ExternalOutput").ap()

    def wsl(name, elem_off, p, c):
        base = _W_OFF[name] + elem_off
        return wblob[base:base + p * c].rearrange("(p c) -> p c", c=c)

    GRAN = []
    GIDX = {}
    def _reg_gran(tag, wname, src_off):
        GIDX[tag] = len(GRAN)
        GRAN.append((wname, src_off))
    for _o in range(8):
        _reg_gran(f"wo:{_o}", "wo", _o * 8 * 128 * 128)
    for _o1 in range(32):
        _reg_gran(f"ff1:{_o1}", "ff1cm", _o1 * 8 * 128 * 128)
        _reg_gran(f"ff2:{_o1}", "ff2", _o1 * 128 * 1024)
    for _c in range(8):
        _reg_gran(f"wd:{_c}", "wd", _c * 128 * 1024)
    for _e in range(8):
        for _o1 in range(8):
            _reg_gran(f"w1:{_e}:{_o1}", "w1cm", _e * D * D + _o1 * 8 * 128 * 128)
    for _e in range(8):
        for _o1 in range(8):
            _reg_gran(f"w2:{_e}:{_o1}", "w2cm", _e * D * D + _o1 * 8 * 128 * 128)
    for _c in range(8):
        _reg_gran(f"sgug:{_c}", "sgu_gate", _c * 128 * 1024)
    for _c in range(8):
        _reg_gran(f"sguu:{_c}", "sgu_up", _c * 128 * 1024)
    for _c in range(8):
        _reg_gran(f"sdown:{_c}", "sdown", _c * 128 * 1024)
    NGRAN = len(GRAN)
    wpair = nc.dram_tensor("wpair", [NGRAN * 128 * 2048], BF16, kind="Internal").ap()

    bd = BD()
    eso = ExitStack()

    with eso:
        sems = {}
        for s in ("pe", "dve", "act", "pool", "sp", "dxp",
                  "dldc", "dldx", "dldk", "dldv", "dldq", "dldo",
                  "dg10", "dg11", "df20", "df21", "dwd0", "dwd1", "dwo0", "dwo1",
                  "dw10", "dw11", "dw20", "dw21", "dgu0", "dgu1",
                  "dsd0", "dsd1", "dsd2", "dsd3", "dpwi", "dpwo", "dout"):
            sems[s] = eso.enter_context(nc.semaphore(s))
        pb = [eso.enter_context(nc.psum_tensor(f"pb{i}", [128, 512], F32))
              for i in range(8)]

        def sbo(es, name, p, c, dt=F32):
            return es.enter_context(nc.sbuf_tensor(name, [p, c], dt))

        ident = sbo(eso, "ident", 128, 128)
        vec = sbo(eso, "vec", 128, 96)
        onescol = sbo(eso, "onescol", 128, 1)
        onesrow = sbo(eso, "onesrow", 1, 128)
        X = sbo(eso, "X", 128, NCH * TOWN)
        stat0 = sbo(eso, "stat0", 1, 1536)
        srow = sbo(eso, "srow", 1, 2048)
        router_sb = sbo(eso, "router_sb", 128, 64)
        ksum = sbo(eso, "ksum", 128, 8)
        rks = sbo(eso, "rks", 128, 8)
        ctx = sbo(eso, "ctx", 128, 2048)

        MU, RHO, R0 = 0, 512, 1024

        def vcol(name, c):
            return vec[:, VI[name] + c: VI[name] + c + 1]

        bd.op("sp", lambda e: e.dma_start(out=ident[:], in_=wsl("ident", 0, 128, 128)),
              writes=("ident",), dma="dldc")
        bd.op("sp", lambda e: e.dma_start(out=vec[:], in_=wsl("vec", 0, 128, 96)),
              writes=("vec",), dma="dldc")
        bd.op("sp", lambda e: e.dma_start(out=router_sb[:], in_=wsl("router", 0, 128, 64)),
              writes=("router_sb",), dma="dldc")
        for c in range(NCH):
            bd.op("sp", lambda e, c=c: e.dma_start(
                out=X[:, c * TOWN:(c + 1) * TOWN],
                in_=xhbm[c * 128 * TOWN:(c + 1) * 128 * TOWN].rearrange(
                    "(p t) -> p t", t=TOWN)),
                writes=(f"X:{c}",), dma="dldx")
        bd.dve(lambda e: e.memset(onescol[:], 1.0), writes=("onescol",))
        bd.dve(lambda e: e.memset(onesrow[:], 1.0), writes=("onesrow",))
        bd.dve(lambda e: e.memset(ksum[:], 0.0), writes=("ksum",))

        def ln_stats(src_fn, src_keys, xsq, xsq_key, tsz, want_r0, bank=0, soff=0):
            for c in range(NCH):
                bd.act(lambda e, c=c, src_fn=src_fn, xsq=xsq, tsz=tsz: e.activation(
                    out=xsq[:, c * tsz:(c + 1) * tsz], in_=src_fn(c), func=AF.Square),
                    reads=(src_keys[c],), writes=(f"{xsq_key}:{c}",))
            for c in range(NCH):
                bd.pe(lambda e, c=c, src_fn=src_fn, tsz=tsz, bank=bank: e.matmul(
                    pb[bank][0:1, 0:tsz], onescol[:], src_fn(c),
                    start=(c == 0), stop=(c == NCH - 1)),
                    reads=("onescol", src_keys[c]),
                    writes=((f"pb{bank}",) if c == 0 else ()),
                    writes_ext=(() if c == 0 else (f"pb{bank}",)))
            for c in range(NCH):
                bd.pe(lambda e, c=c, xsq=xsq, tsz=tsz, bank=bank: e.matmul(
                    pb[bank][32:33, 0:tsz], onescol[:], xsq[:, c * tsz:(c + 1) * tsz],
                    start=(c == 0), stop=(c == NCH - 1)),
                    reads=("onescol", f"{xsq_key}:{c}"), writes_ext=(f"pb{bank}",))
            mu = stat0[0:1, MU + soff:MU + soff + tsz]
            rho = stat0[0:1, RHO + soff:RHO + soff + tsz]
            sq = srow[0:1, 0:tsz]
            esq = srow[0:1, 256:256 + tsz]
            var = srow[0:1, 512:512 + tsz]
            av = srow[0:1, 768:768 + tsz]
            ra = srow[0:1, 1024:1024 + tsz]
            vr = srow[0:1, 1280:1280 + tsz]
            avr = srow[0:1, 1536:1536 + tsz]
            rar = srow[0:1, 1792:1792 + tsz]
            r1 = av
            r0ap = stat0[0:1, R0 + soff:R0 + soff + tsz] if want_r0 else sq
            bd.dve(lambda e, mu=mu, tsz=tsz, bank=bank: e.tensor_scalar_mul(
                mu, pb[bank][0:1, 0:tsz], 1.0 / D),
                reads=(f"pb{bank}",), writes=("s_mu" + str(soff),))
            bd.dve(lambda e, sq=sq, mu=mu: e.tensor_mul(sq, mu, mu),
                   reads=("s_mu" + str(soff),), writes=("s_sq",))
            bd.dve(lambda e, esq=esq, tsz=tsz, bank=bank: e.tensor_scalar_mul(
                esq, pb[bank][32:33, 0:tsz], 1.0 / D),
                reads=(f"pb{bank}",), writes=("s_esq",))
            bd.dve(lambda e, var=var, esq=esq, sq=sq: e.tensor_sub(var, esq, sq),
                   reads=("s_esq", "s_sq"), writes=("s_var",))
            if want_r0:
                bd.dve(lambda e, av=av, var=var: e.tensor_scalar_add(av, var, float(EPS)),
                       reads=("s_var",), writes=("s_a",))
                bd.dve(lambda e, ra=ra, av=av: e.reciprocal(ra, av),
                       reads=("s_a",), writes=("s_ra",))
                bd.act(lambda e, r0ap=r0ap, ra=ra: e.activation(
                    out=r0ap, in_=ra, func=AF.Sqrt),
                    reads=("s_ra", "s_sq"), writes=("s_r0" + str(soff),))
            bd.dve(lambda e, avr=avr, var=var: e.tensor_scalar(
                avr, var, float(1.0 + EPS), float(EPS * EPS), ALU.mult, ALU.add),
                reads=("s_var",), writes=("s_avr",))
            bd.dve(lambda e, rar=rar, avr=avr: e.reciprocal(rar, avr),
                   reads=("s_avr",), writes=("s_rar",))
            bd.act(lambda e, rho=rho, rar=rar: e.activation(
                out=rho, in_=rar, func=AF.Sqrt),
                reads=("s_rar",), writes=("s_rho" + str(soff),))

        def bcast(bank, colo, width, row_ap, row_key, start, stop):
            bd.pe(lambda e, bank=bank, colo=colo, width=width, row_ap=row_ap: e.matmul(
                pb[bank][:, colo:colo + width], onesrow[:], row_ap, start=start, stop=stop),
                reads=("onesrow", row_key),
                writes=((f"pb{bank}",) if start else ()),
                writes_ext=(() if start else (f"pb{bank}",)))

        def mm3(bank, cs, ce_, lh, ll, rh, rl, start, stop, reads, prow=128):
            key = f"pb{bank}"
            bd.pe(lambda e, lh=lh, rh=rh, bank=bank, cs=cs, ce_=ce_, start=start,
                  prow=prow: e.matmul(
                pb[bank][0:prow, cs:ce_], lh, rh, start=start, stop=False),
                reads=reads, writes=((key,) if start else ()),
                writes_ext=(() if start else (key,)))
            bd.pe(lambda e, lh=lh, rl=rl, bank=bank, cs=cs, ce_=ce_, prow=prow: e.matmul(
                pb[bank][0:prow, cs:ce_], lh, rl, start=False, stop=False),
                reads=reads, writes_ext=(key,))
            bd.pe(lambda e, ll=ll, rh=rh, bank=bank, cs=cs, ce_=ce_, stop=stop,
                  prow=prow: e.matmul(
                pb[bank][0:prow, cs:ce_], ll, rh, start=False, stop=stop),
                reads=reads, writes_ext=(key,))

        def split(src_ap, hi_ap, lo_ap, reads, wkey):
            bd.act(lambda e, src_ap=src_ap, hi_ap=hi_ap: e.activation(
                out=hi_ap, in_=src_ap, func=AF.Copy),
                reads=reads, writes=(wkey + "h",))
            bd.dve(lambda e, src_ap=src_ap, hi_ap=hi_ap, lo_ap=lo_ap: e.tensor_sub(
                lo_ap, src_ap, hi_ap),
                reads=reads + (wkey + "h",), writes=(wkey + "l",))

        def load_split(stream, wname, granule_off, cols, wtmp, wtmp_key, hi_t, lo_t,
                       dst_off, wkey, eng="sp", woff=0):
            wkeys = (wtmp_key,) if isinstance(wtmp_key, str) else tuple(wtmp_key)
            bd.op(eng, lambda e, granule_off=granule_off, cols=cols, woff=woff: e.dma_start(
                out=wtmp[:, woff:woff + cols], in_=wsl(wname, granule_off, 128, cols)),
                writes=wkeys, dma=stream)
            bd.act(lambda e, cols=cols, dst_off=dst_off, woff=woff: e.activation(
                out=hi_t[:, dst_off:dst_off + cols], in_=wtmp[:, woff:woff + cols],
                func=AF.Copy),
                reads=wkeys, writes=(wkey + "h",))
            bd.dve(lambda e, cols=cols, dst_off=dst_off, woff=woff: e.tensor_sub(
                lo_t[:, dst_off:dst_off + cols], wtmp[:, woff:woff + cols],
                hi_t[:, dst_off:dst_off + cols]),
                reads=wkeys + (wkey + "h",), writes=(wkey + "l",))

        def load_pair(stream, tag, slot, slot_key, eng="sp"):
            gi = GIDX[tag]
            base = gi * 128 * 2048
            src2 = wpair[base:base + 128 * 2048].rearrange("(h p c) -> p h c", h=2, c=1024)
            bd.op(eng, lambda e, src2=src2: e.dma_start(out=slot[:], in_=src2),
                  reads=(f"pw:{gi}",), writes=(slot_key,), dma=stream)

        def emit_pair_build(lo_g, hi_g, wtmps, pslots):
            for gi in range(lo_g, hi_g):
                wtmp, wtmp_key = wtmps[gi % 2]
                pslot, pslot_key = pslots[gi % 2]
                wname, src_off = GRAN[gi]
                base = gi * 128 * 2048
                hi_hbm = wpair[base:base + 128 * 1024].rearrange("(p c) -> p c", c=1024)
                lo_hbm = wpair[base + 128 * 1024:base + 128 * 2048].rearrange(
                    "(p c) -> p c", c=1024)
                bd.op("sp", lambda e, wname=wname, src_off=src_off, wtmp=wtmp: e.dma_start(
                    out=wtmp[:], in_=wsl(wname, src_off, 128, 1024)),
                    writes=(wtmp_key,), dma="dpwi")
                bd.act(lambda e, wtmp=wtmp, pslot=pslot: e.activation(
                    out=pslot[:, 0:1024], in_=wtmp[:], func=AF.Copy),
                    reads=(wtmp_key,), writes=(pslot_key + "h",))
                bd.dve(lambda e, wtmp=wtmp, pslot=pslot: e.tensor_sub(
                    pslot[:, 1024:2048], wtmp[:], pslot[:, 0:1024]),
                    reads=(wtmp_key, pslot_key + "h"), writes=(pslot_key + "l",))
                bd.op("pool", lambda e, hi_hbm=hi_hbm, pslot=pslot: e.dma_start(
                    out=hi_hbm, in_=pslot[:, 0:1024]),
                    reads=(pslot_key + "h",), writes=(f"pwh:{gi}",), dma="dpwo")
                bd.op("pool", lambda e, lo_hbm=lo_hbm, pslot=pslot: e.dma_start(
                    out=lo_hbm, in_=pslot[:, 1024:2048]),
                    reads=(pslot_key + "l",), writes=(f"pw:{gi}",), dma="dpwo")

        esa = ExitStack()
        with esa:
            wkh = sbo(esa, "wkh", 128, NCH * 1024, BF16)
            wkl = sbo(esa, "wkl", 128, NCH * 1024, BF16)
            wvh = sbo(esa, "wvh", 128, NCH * 1024, BF16)
            wvl = sbo(esa, "wvl", 128, NCH * 1024, BF16)
            wtmpA = sbo(esa, "wtmpA", 128, 1024)
            wtmpA2 = sbo(esa, "wtmpA2", 128, 1024)
            xp = sbo(esa, "xp", 128, 2048)
            hAh = sbo(esa, "hAh", 128, 2 * NCH * 128, BF16)
            hAl = sbo(esa, "hAl", 128, 2 * NCH * 128, BF16)
            xsqA = sbo(esa, "xsqA", 128, NCH * 128)
            diffA = sbo(esa, "diffA", 128, NCH * 128)
            kexp = sbo(esa, "kexp", 128, 1024)
            keh = sbo(esa, "keh", 128, 1024, BF16)
            kel = sbo(esa, "kel", 128, 1024, BF16)
            vth = sbo(esa, "vth", 128, 1024, BF16)
            vtl = sbo(esa, "vtl", 128, 1024, BF16)
            onescol_bf = sbo(esa, "onescol_bf", 128, 1, BF16)
            pslotA = sbo(esa, "pslotA", 128, 2048, BF16)
            pslotB = sbo(esa, "pslotB", 128, 2048, BF16)
            bd.dve(lambda e: e.memset(onescol_bf[:], 1.0), writes=("onescol_bf",))

            for c in range(NCH):
                load_split("dldk", "wk", c * 128 * 1024, 1024, wtmpA, "wtmpA",
                           wkh, wkl, c * 1024, f"wk:{c}")
                load_split("dldv", "wv", c * 128 * 1024, 1024, wtmpA2, "wtmpA2",
                           wvh, wvl, c * 1024, f"wv:{c}")

            def a_env(s):
                g, s4 = s // 4, s % 4
                own = g < 4
                po2 = (s % 2) * 1024

                def src(c, g=g, s4=s4, own=own, po2=po2):
                    if own:
                        o = g * 512 + s4 * 128
                        return X[:, c * TOWN + o: c * TOWN + o + 128]
                    return xp[:, po2 + c * 128: po2 + c * 128 + 128]

                skeys = tuple((f"X:{c}" if own else f"xp:{s % 2}:{c}")
                              for c in range(NCH))
                return g, s4, own, po2, src, skeys

            def ln_a(s):
                g, s4, own, po2, src, skeys = a_env(s)
                bank = s % 2
                if not own:
                    sub = (g - 4) * 4 + s4
                    base = XOWN + sub * 128 * 1024
                    bd.op("sp", lambda e, base=base, po2=po2: e.dma_start(
                        out=xp[:, po2:po2 + 1024],
                        in_=xhbm[base:base + 128 * 1024].rearrange(
                            "(p k) -> p k", k=1024)),
                        writes=tuple(f"xp:{s % 2}:{c}" for c in range(NCH)), dma="dxp")
                for c in range(NCH):
                    bd.act(lambda e, c=c, src=src: e.activation(
                        out=xsqA[:, c * 128:(c + 1) * 128], in_=src(c), func=AF.Square),
                        reads=(skeys[c],), writes=(f"xsqA:{c}",))
                for c in range(NCH):
                    bd.pe(lambda e, c=c, src=src, bank=bank: e.matmul(
                        pb[bank][0:1, 384:512], onescol[:], src(c),
                        start=(c == 0), stop=(c == NCH - 1)),
                        reads=("onescol", skeys[c]),
                        writes=((f"pb{bank}",) if c == 0 else ()),
                        writes_ext=(() if c == 0 else (f"pb{bank}",)))
                for c in range(NCH):
                    bd.pe(lambda e, c=c, bank=bank: e.matmul(
                        pb[bank][32:33, 384:512], onescol[:],
                        xsqA[:, c * 128:(c + 1) * 128],
                        start=(c == 0), stop=(c == NCH - 1)),
                        reads=("onescol", f"xsqA:{c}"), writes_ext=(f"pb{bank}",))
                po = 128 * (s % 2)
                mu = stat0[0:1, MU + po:MU + po + 128]
                rho = stat0[0:1, RHO + po:RHO + po + 128]
                sq = srow[0:1, 0:128]
                esq = srow[0:1, 256:256 + 128]
                var = srow[0:1, 512:512 + 128]
                avr = srow[0:1, 1536:1536 + 128]
                rar = srow[0:1, 1792:1792 + 128]
                bd.dve(lambda e, mu=mu, bank=bank: e.tensor_scalar_mul(
                    mu, pb[bank][0:1, 384:512], 1.0 / D),
                    reads=(f"pb{bank}",), writes=("s_mu" + str(s % 2),))
                bd.dve(lambda e, sq=sq, mu=mu: e.tensor_mul(sq, mu, mu),
                       reads=("s_mu" + str(s % 2),), writes=("s_sq",))
                bd.dve(lambda e, esq=esq, bank=bank: e.tensor_scalar_mul(
                    esq, pb[bank][32:33, 384:512], 1.0 / D),
                    reads=(f"pb{bank}",), writes=("s_esq",))
                bd.dve(lambda e, var=var, esq=esq, sq=sq: e.tensor_sub(var, esq, sq),
                       reads=("s_esq", "s_sq"), writes=("s_var",))
                bd.dve(lambda e, avr=avr, var=var: e.tensor_scalar(
                    avr, var, float(1.0 + EPS), float(EPS * EPS), ALU.mult, ALU.add),
                    reads=("s_var",), writes=("s_avr",))
                bd.dve(lambda e, rar=rar, avr=avr: e.reciprocal(rar, avr),
                       reads=("s_avr",), writes=("s_rar",))
                bd.act(lambda e, rho=rho, rar=rar: e.activation(
                    out=rho, in_=rar, func=AF.Sqrt),
                    reads=("s_rar",), writes=("s_rho" + str(s % 2),))

            def ln_b(s):
                g, s4, own, po2, src, skeys = a_env(s)
                bank = s % 2
                po = 128 * (s % 2)
                bcast(bank, 128, 128, stat0[0:1, MU + po:MU + po + 128],
                      "s_mu" + str(s % 2), True, False)
                bcast(bank, 256, 128, stat0[0:1, RHO + po:RHO + po + 128],
                      "s_rho" + str(s % 2), False, True)
                for c in range(NCH):
                    bd.dve(lambda e, c=c, src=src, bank=bank: e.tensor_sub(
                        diffA[:, c * 128:(c + 1) * 128], src(c), pb[bank][:, 128:256]),
                        reads=(skeys[c], f"pb{bank}"), writes=(f"diffA:{c}",))
                for c in range(NCH):
                    bd.dve(lambda e, c=c, bank=bank: e.tensor_mul(
                        diffA[:, c * 128:(c + 1) * 128], diffA[:, c * 128:(c + 1) * 128],
                        pb[bank][:, 256:384]),
                        reads=(f"pb{bank}",), writes=(f"diffA:{c}",))
                for c in range(NCH):
                    bd.dve(lambda e, c=c: e.tensor_scalar(
                        diffA[:, c * 128:(c + 1) * 128], diffA[:, c * 128:(c + 1) * 128],
                        vcol("ln1_g", c), vcol("ln1_b", c), ALU.mult, ALU.add),
                        reads=("vec",), writes=(f"diffA:{c}",))
                for c in range(NCH):
                    split(diffA[:, c * 128:(c + 1) * 128],
                          hAh[:, po2 + c * 128: po2 + c * 128 + 128],
                          hAl[:, po2 + c * 128: po2 + c * 128 + 128],
                          (f"diffA:{c}",), f"hA:{s % 2}:{c}")

            def kv_stage(s):
                g, s4, own, po2, src, skeys = a_env(s)
                bank = s % 2
                for c in range(NCH):
                    for hc in range(2):
                        mm3(2 + hc, 0, 512,
                            hAh[:, po2 + c * 128: po2 + c * 128 + 128],
                            hAl[:, po2 + c * 128: po2 + c * 128 + 128],
                            wkh[:, c * 1024 + hc * 512: c * 1024 + hc * 512 + 512],
                            wkl[:, c * 1024 + hc * 512: c * 1024 + hc * 512 + 512],
                            start=(c == 0), stop=(c == NCH - 1),
                            reads=(f"hA:{s % 2}:{c}h", f"hA:{s % 2}:{c}l",
                                   f"wk:{c}h", f"wk:{c}l"))
                emit_pair_build(s * 8, min(s * 8 + 2, NGRAN),
                                ((wtmpA, "wtmpA"), (wtmpA2, "wtmpA2")),
                                ((pslotA, "psA"), (pslotB, "psB")))
                for hc in range(2):
                    bd.act(lambda e, hc=hc: e.activation(
                        out=kexp[:, hc * 512:(hc + 1) * 512], in_=pb[2 + hc][:, 0:512],
                        func=AF.Exp, scale=float(SC)),
                        reads=(f"pb{2 + hc}",), writes=(f"kexp:{hc}",))
                for hc in range(2):
                    split(kexp[:, hc * 512:(hc + 1) * 512],
                          keh[:, hc * 512:(hc + 1) * 512],
                          kel[:, hc * 512:(hc + 1) * 512],
                          (f"kexp:{hc}",), f"kexp:{hc}")
                for c in range(NCH):
                    for hc in range(2):
                        mm3(2 + hc, 0, 512,
                            hAh[:, po2 + c * 128: po2 + c * 128 + 128],
                            hAl[:, po2 + c * 128: po2 + c * 128 + 128],
                            wvh[:, c * 1024 + hc * 512: c * 1024 + hc * 512 + 512],
                            wvl[:, c * 1024 + hc * 512: c * 1024 + hc * 512 + 512],
                            start=(c == 0), stop=(c == NCH - 1),
                            reads=(f"hA:{s % 2}:{c}h", f"hA:{s % 2}:{c}l",
                                   f"wv:{c}h", f"wv:{c}l"))
                emit_pair_build(s * 8 + 2, min(s * 8 + 4, NGRAN),
                                ((wtmpA, "wtmpA"), (wtmpA2, "wtmpA2")),
                                ((pslotA, "psA"), (pslotB, "psB")))
                for hc in range(2):
                    split(pb[2 + hc][:, 0:512], vth[:, hc * 512:(hc + 1) * 512],
                          vtl[:, hc * 512:(hc + 1) * 512],
                          (f"pb{2 + hc}",), f"vtm:{hc}")
                emit_pair_build(s * 8 + 4, min(s * 8 + 6, NGRAN),
                                ((wtmpA, "wtmpA"), (wtmpA2, "wtmpA2")),
                                ((pslotA, "psA"), (pslotB, "psB")))
                for j in range(8):
                    head, ec = j // 2, j % 2
                    cbank, colo = 4 + j // 2, (j % 2) * 256
                    first = (s4 == 0 and colo == 0)
                    last = (s4 == 3 and colo == 256)
                    ks = head * 256 + ec * 128
                    mm3(cbank, colo, colo + 256,
                        keh[:, ks:ks + 128], kel[:, ks:ks + 128],
                        vth[:, head * 256: head * 256 + 256],
                        vtl[:, head * 256: head * 256 + 256],
                        start=first, stop=last,
                        reads=(f"kexp:{ks // 512}h", f"kexp:{ks // 512}l",
                               f"vtm:{(head * 256) // 512}h",
                               f"vtm:{(head * 256) // 512}l"))
                    bd.pe(lambda e, j=j, ks=ks, bank=bank: e.matmul(
                        pb[bank][:, j:j + 1], keh[:, ks:ks + 128], onescol_bf[:],
                        start=(j == 0), stop=False),
                        reads=(f"kexp:{ks // 512}h", "onescol_bf"),
                        writes=((f"pb{bank}",) if j == 0 else ()),
                        writes_ext=(() if j == 0 else (f"pb{bank}",)))
                    bd.pe(lambda e, j=j, ks=ks, bank=bank: e.matmul(
                        pb[bank][:, j:j + 1], kel[:, ks:ks + 128], onescol_bf[:],
                        start=False, stop=(j == 7)),
                        reads=(f"kexp:{ks // 512}l", "onescol_bf"),
                        writes_ext=(f"pb{bank}",))
                bd.dve(lambda e, bank=bank: e.tensor_add(
                    ksum[:, 0:8], ksum[:, 0:8], pb[bank][:, 0:8]),
                    reads=(f"pb{bank}",), writes=("ksum",))
                emit_pair_build(s * 8 + 6, min(s * 8 + 8, NGRAN),
                                ((wtmpA, "wtmpA"), (wtmpA2, "wtmpA2")),
                                ((pslotA, "psA"), (pslotB, "psB")))
                if s4 == 3:
                    for j in range(8):
                        cbank, colo = 4 + j // 2, (j % 2) * 256
                        if g == 0:
                            bd.dve(lambda e, j=j, cbank=cbank, colo=colo: e.tensor_copy(
                                ctx[:, j * 256:(j + 1) * 256],
                                pb[cbank][:, colo:colo + 256]),
                                reads=(f"pb{cbank}",), writes=(f"ctx:{j}",))
                        else:
                            bd.dve(lambda e, j=j, cbank=cbank, colo=colo: e.tensor_add(
                                ctx[:, j * 256:(j + 1) * 256],
                                ctx[:, j * 256:(j + 1) * 256],
                                pb[cbank][:, colo:colo + 256]),
                                reads=(f"pb{cbank}",), writes=(f"ctx:{j}",))

            for i in range(34):
                if i < 32:
                    ln_a(i)
                if 1 <= i <= 32:
                    ln_b(i - 1)
                if i >= 2:
                    kv_stage(i - 2)

            bd.dve(lambda e: e.reciprocal(rks[:], ksum[:]), reads=("ksum",), writes=("rks",))
            for j in range(8):
                bd.dve(lambda e, j=j: e.tensor_scalar_mul(
                    ctx[:, j * 256:(j + 1) * 256], ctx[:, j * 256:(j + 1) * 256],
                    rks[:, j:j + 1]),
                    reads=("rks",), writes=(f"ctx:{j}",))
            if "ctx" in tap_aps:
                bd.pool(lambda e: e.dma_start(out=tap_aps["ctx"], in_=ctx[:]),
                        reads=tuple(f"ctx:{j}" for j in range(8)), dma="dout")

            bd.add_floor(
                [f"wk:{c}{s}" for c in range(NCH) for s in ("h", "l")]
                + [f"wv:{c}{s}" for c in range(NCH) for s in ("h", "l")]
                + [f"xp:{p}:{c}" for p in range(2) for c in range(NCH)]
                + [f"hA:{p}:{c}{s}" for p in range(2) for c in range(NCH) for s in ("h", "l")]
                + [f"xsqA:{c}" for c in range(NCH)] + [f"diffA:{c}" for c in range(NCH)]
                + ["wtmpA", "wtmpA2", "onescol_bf", "psAh", "psAl", "psBh", "psBl"]
                + [f"kexp:{h}{s}" for h in range(2) for s in ("", "h", "l")]
                + [f"vtm:{h}{s}" for h in range(2) for s in ("h", "l")])

        esb = ExitStack()
        with esb:
            wq_sb = sbo(esb, "wq_sb", 128, NCH * 1024)
            B1 = sbo(esb, "B1", 128, 2048)
            B2 = sbo(esb, "B2", 128, 2048)
            B3 = sbo(esb, "B3", 128, 2048)
            B4 = sbo(esb, "B4", 128, 2048)
            qt = sbo(esb, "qt", 128, 16)
            P1h = sbo(esb, "P1h", 128, 2048, BF16)
            P1l = sbo(esb, "P1l", 128, 2048, BF16)
            wop = [sbo(esb, f"wop{i}", 128, 2048, BF16) for i in range(2)]
            g1p = [sbo(esb, f"g1p{i}", 128, 2048, BF16) for i in range(2)]
            f2p = [sbo(esb, f"f2p{i}", 128, 2048, BF16) for i in range(2)]
            wdp = [sbo(esb, f"wdp{i}", 128, 2048, BF16) for i in range(2)]
            gch = sbo(esb, "gch", 128, 512, BF16)
            gcl = sbo(esb, "gcl", 128, 512, BF16)

            for c in range(NCH):
                bd.op("sp", lambda e, c=c: e.dma_start(
                    out=wq_sb[:, c * 1024:(c + 1) * 1024],
                    in_=wsl("wq", c * 128 * 1024, 128, 1024)),
                    writes=(f"wq:{c}",), dma="dldq")

            TB = 256

            for t in range(TOWN // TB):
                T0 = t * TB

                def xsrc(c, T0=T0):
                    return X[:, c * TOWN + T0: c * TOWN + T0 + TB]

                xkeys = tuple(f"X:{c}" for c in range(NCH))
                ln_stats(xsrc, xkeys, B4, "B4", TB, want_r0=True)
                bcast(1, 0, TB, stat0[0:1, MU:MU + TB], "s_mu0", True, False)
                bcast(1, TB, TB, stat0[0:1, RHO:RHO + TB], "s_rho0", False, True)
                bcast(0, 0, TB, stat0[0:1, R0:R0 + TB], "s_r00", True, True)
                for c in range(NCH):
                    bd.dve(lambda e, c=c, xsrc=xsrc: e.tensor_sub(
                        B1[:, c * TB:(c + 1) * TB], xsrc(c), pb[1][:, 0:TB]),
                        reads=(xkeys[c], "pb1"), writes=(f"B1:{c}",))
                for c in range(NCH):
                    bd.dve(lambda e, c=c: e.tensor_mul(
                        B2[:, c * TB:(c + 1) * TB], B1[:, c * TB:(c + 1) * TB],
                        pb[1][:, TB:2 * TB]),
                        reads=("pb1", f"B1:{c}"), writes=(f"B2:{c}",))
                for c in range(NCH):
                    bd.dve(lambda e, c=c: e.tensor_scalar(
                        B2[:, c * TB:(c + 1) * TB], B2[:, c * TB:(c + 1) * TB],
                        vcol("ln1_g", c), vcol("ln1_b", c), ALU.mult, ALU.add),
                        reads=("vec",), writes=(f"B2:{c}",))
                for c in range(NCH):
                    bd.dve(lambda e, c=c: e.tensor_mul(
                        B3[:, c * TB:(c + 1) * TB], B1[:, c * TB:(c + 1) * TB],
                        pb[0][:, 0:TB]),
                        reads=("pb0", f"B1:{c}"), writes=(f"B3:{c}",))
                for s2 in range(2):
                    for c in range(NCH):
                        for hc in range(2):
                            bd.pe(lambda e, c=c, hc=hc, s2=s2: e.matmul(
                                pb[2 + hc][:, 0:512],
                                B2[:, c * TB + s2 * 128: c * TB + s2 * 128 + 128],
                                wq_sb[:, c * 1024 + hc * 512: c * 1024 + hc * 512 + 512],
                                start=(c == 0), stop=(c == NCH - 1)),
                                reads=(f"B2:{c}", f"wq:{c}"),
                                writes=((f"pb{2 + hc}",) if c == 0 else ()),
                                writes_ext=(() if c == 0 else (f"pb{2 + hc}",)))
                    for head in range(HEADS):
                        bank = 2 + head // 2
                        colo = (head % 2) * 256
                        qcell = qt[:, head * 4: head * 4 + 1]
                        qbias = qt[:, head * 4 + 1: head * 4 + 2]
                        qsum = qt[:, head * 4 + 2: head * 4 + 3]
                        qrec = qt[:, head * 4 + 3: head * 4 + 4]
                        qekey = f"B1:{s2 * 4 + head}"
                        qe_ap = B1[:, s2 * 1024 + head * 256: s2 * 1024 + head * 256 + 256]
                        bd.act(lambda e, bank=bank, colo=colo, qe_ap=qe_ap:
                               e.activation(out=qe_ap, in_=pb[bank][:, colo:colo + 256],
                                            func=AF.Exp, scale=float(SC)),
                               reads=(f"pb{bank}",), writes=(qekey,))
                        bd.dve(lambda e, qsum=qsum, qe_ap=qe_ap: e.reduce_sum(
                            qsum, qe_ap, axis=AX.X),
                            reads=(qekey,), writes=(f"qt:{head}c",))
                        bd.dve(lambda e, qrec=qrec, qsum=qsum: e.reciprocal(qrec, qsum),
                               reads=(f"qt:{head}c",), writes=(f"qt:{head}d",))
                        bd.dve(lambda e, qe_ap=qe_ap, qrec=qrec: e.tensor_scalar_mul(
                            qe_ap, qe_ap, qrec),
                            reads=(f"qt:{head}d",), writes=(qekey,))
                    for c in range(NCH):
                        bank = 4 + c // 4
                        colo = (c % 4) * 128
                        bd.pe(lambda e, c=c, s2=s2, bank=bank, colo=colo: e.matmul(
                            pb[bank][:, colo:colo + 128],
                            B1[:, s2 * 1024 + c * 128: s2 * 1024 + c * 128 + 128],
                            ident[:], is_transpose=True,
                            start=(colo == 0), stop=(colo == 384)),
                            reads=(f"B1:{s2 * 4 + c // 2}", "ident"),
                            writes=((f"pb{bank}",) if colo == 0 else ()),
                            writes_ext=(() if colo == 0 else (f"pb{bank}",)))
                    for c in range(NCH):
                        bank = 4 + c // 4
                        colo = (c % 4) * 128
                        bd.dve(lambda e, c=c, s2=s2, bank=bank, colo=colo: e.tensor_copy(
                            B4[:, c * TB + s2 * 128: c * TB + s2 * 128 + 128],
                            pb[bank][:, colo:colo + 128]),
                            reads=(f"pb{bank}",), writes=(f"B4:{c}",))
                if t == 0 and "qsm" in tap_aps:
                    bd.pool(lambda e: e.dma_start(out=tap_aps["qsm"], in_=B4[:]),
                            reads=tuple(f"B4:{c}" for c in range(NCH)), dma="dout")
                if t == 0 and "hB" in tap_aps:
                    bd.pool(lambda e: e.dma_start(out=tap_aps["hB"], in_=B2[:]),
                            reads=tuple(f"B2:{c}" for c in range(NCH)), dma="dout")
                attn_banks = [6, 6, 7, 7, 2, 2, 3, 3]
                for fj in range(8):
                    head, fc = fj // 2, fj % 2
                    bank = attn_banks[fj]
                    colo = (fj % 2) * 256
                    for ei in range(2):
                        j2 = head * 2 + ei
                        first = (ei == 0 and colo == 0)
                        last = (ei == 1 and colo == 256)
                        bd.pe(lambda e, j2=j2, fc=fc, bank=bank, colo=colo,
                              first=first, last=last: e.matmul(
                                  pb[bank][:, colo:colo + 256],
                                  ctx[:, j2 * 256 + fc * 128: j2 * 256 + fc * 128 + 128],
                                  B4[:, j2 * 256: j2 * 256 + 256],
                                  start=first, stop=last),
                              reads=(f"ctx:{j2}", f"B4:{j2}"),
                              writes=((f"pb{bank}",) if first else ()),
                              writes_ext=(() if first else (f"pb{bank}",)))
                for fj in range(8):
                    bank = attn_banks[fj]
                    colo = (fj % 2) * 256
                    split(pb[bank][0:128, colo:colo + 256],
                          P1h[:, fj * 256:(fj + 1) * 256],
                          P1l[:, fj * 256:(fj + 1) * 256],
                          (f"pb{bank}",), f"P1:{fj}")
                for o in range(NCH):
                    par = o % 2
                    bank = 4 + par
                    load_pair(f"dwo{par}", f"wo:{o}", wop[par], f"wop:{par}")
                    for f in range(NCH):
                        mm3(bank, 0, 256,
                            wop[par][:, f * 128:(f + 1) * 128],
                            wop[par][:, 1024 + f * 128: 1024 + f * 128 + 128],
                            P1h[:, f * 256:(f + 1) * 256], P1l[:, f * 256:(f + 1) * 256],
                            start=(f == 0), stop=(f == NCH - 1),
                            reads=(f"wop:{par}", f"P1:{f}h", f"P1:{f}l"))
                    bd.dve(lambda e, o=o, bank=bank: e.scalar_tensor_tensor(
                        B2[:, o * TB:(o + 1) * TB], pb[bank][:, 0:256], vcol("bo", o),
                        B3[:, o * TB:(o + 1) * TB], ALU.add, ALU.add),
                        reads=(f"pb{bank}", "vec", f"B3:{o}"), writes=(f"B2:{o}",))
                if t == 0 and "attn" in tap_aps:
                    bd.pool(lambda e: e.dma_start(out=tap_aps["attn"], in_=B1[:]),
                            reads=tuple(f"B1:{c}" for c in range(NCH)), dma="dout")
                if t == 0 and "X1" in tap_aps:
                    bd.pool(lambda e: e.dma_start(out=tap_aps["X1"], in_=B2[:]),
                            reads=tuple(f"B2:{c}" for c in range(NCH)), dma="dout")
                def x1src(c):
                    return B2[:, c * TB:(c + 1) * TB]

                x1keys = tuple(f"B2:{c}" for c in range(NCH))
                ln_stats(x1src, x1keys, B4, "B4", TB, want_r0=True)
                bcast(1, 0, TB, stat0[0:1, MU:MU + TB], "s_mu0", True, False)
                bcast(1, TB, TB, stat0[0:1, R0:R0 + TB], "s_r00", False, True)
                for c in range(NCH):
                    bd.dve(lambda e, c=c, x1src=x1src: e.tensor_sub(
                        B3[:, c * TB:(c + 1) * TB], x1src(c), pb[1][:, 0:TB]),
                        reads=(x1keys[c], "pb1"), writes=(f"B3:{c}",))
                for c in range(NCH):
                    bd.dve(lambda e, c=c: e.tensor_mul(
                        B3[:, c * TB:(c + 1) * TB], B3[:, c * TB:(c + 1) * TB],
                        pb[1][:, TB:2 * TB]),
                        reads=("pb1",), writes=(f"B3:{c}",))
                for c in range(NCH):
                    bd.dve(lambda e, c=c: e.tensor_scalar(
                        B3[:, c * TB:(c + 1) * TB], B3[:, c * TB:(c + 1) * TB],
                        vcol("ln2_g", c), vcol("ln2_b", c), ALU.mult, ALU.add),
                        reads=("vec",), writes=(f"B3:{c}",))
                if t == 0 and "h2" in tap_aps:
                    bd.pool(lambda e: e.dma_start(out=tap_aps["h2"], in_=B3[:]),
                            reads=tuple(f"B3:{c}" for c in range(NCH)), dma="dout")
                for c in range(NCH):
                    split(B3[:, c * TB:(c + 1) * TB], P1h[:, c * TB:(c + 1) * TB],
                          P1l[:, c * TB:(c + 1) * TB], (f"B3:{c}",), f"P1:{c}")
                ff2_banks = [2, 2, 3, 3, 6, 6, 7, 7]
                for o1 in range(33):
                    par = o1 % 2
                    bank = 4 + par
                    if o1 < 32:
                        load_pair(f"dg1{par}", f"ff1:{o1}", g1p[par], f"g1:{par}")
                        for c in range(NCH):
                            mm3(bank, 0, 256,
                                g1p[par][:, c * 128:(c + 1) * 128],
                                g1p[par][:, 1024 + c * 128: 1024 + c * 128 + 128],
                                P1h[:, c * TB:(c + 1) * TB], P1l[:, c * TB:(c + 1) * TB],
                                start=(c == 0), stop=(c == NCH - 1),
                                reads=(f"g1:{par}", f"P1:{c}h", f"P1:{c}l"))
                        bd.act(lambda e, o1=o1, par=par, bank=bank: e.activation(
                            out=B1[:, par * 256: par * 256 + 256], in_=pb[bank][:, 0:256],
                            func=AF.Gelu_apprx_tanh,
                            bias=vec[:, VI["b_ff1"] + o1: VI["b_ff1"] + o1 + 1]),
                            reads=(f"pb{bank}", "vec"), writes=(f"B1:{par}",))
                        split(B1[:, par * 256: par * 256 + 256],
                              gch[:, par * 256: par * 256 + 256],
                              gcl[:, par * 256: par * 256 + 256],
                              (f"B1:{par}",), f"gc:{par}")
                        load_pair(f"df2{par}", f"ff2:{o1}", f2p[par], f"f2:{par}")
                    if o1 > 0:
                        o1p = o1 - 1
                        parp = o1p % 2
                        for o2 in range(NCH):
                            bank2 = ff2_banks[o2]
                            colo2 = (o2 % 2) * 256
                            first = (o1p == 0 and colo2 == 0)
                            last = (o1p == 31 and colo2 == 256)
                            mm3(bank2, colo2, colo2 + 256,
                                f2p[parp][:, o2 * 128:(o2 + 1) * 128],
                                f2p[parp][:, 1024 + o2 * 128: 1024 + o2 * 128 + 128],
                                gch[:, parp * 256: parp * 256 + 256],
                                gcl[:, parp * 256: parp * 256 + 256],
                                start=first, stop=last,
                                reads=(f"f2:{parp}", f"gc:{parp}h", f"gc:{parp}l"))
                for o2 in range(NCH):
                    bank2 = ff2_banks[o2]
                    colo2 = (o2 % 2) * 256
                    bd.dve(lambda e, o2=o2, bank2=bank2, colo2=colo2: e.scalar_tensor_tensor(
                        B4[:, o2 * TB:(o2 + 1) * TB], pb[bank2][:, colo2:colo2 + 256],
                        vcol("b_ff2", o2), B2[:, o2 * TB:(o2 + 1) * TB], ALU.add, ALU.add),
                        reads=(f"pb{bank2}", "vec", f"B2:{o2}"), writes=(f"B4:{o2}",))
                if t == 0 and "X2" in tap_aps:
                    bd.pool(lambda e: e.dma_start(out=tap_aps["X2"], in_=B4[:]),
                            reads=tuple(f"B4:{c}" for c in range(NCH)), dma="dout")
                for c in range(NCH):
                    split(B4[:, c * TB:(c + 1) * TB], P1h[:, c * TB:(c + 1) * TB],
                          P1l[:, c * TB:(c + 1) * TB], (f"B4:{c}",), f"P1:{c}")
                wd_banks = [0, 0, 1, 1, 4, 4, 5, 5]
                for c in range(NCH):
                    par = c % 2
                    load_pair(f"dwd{par}", f"wd:{c}", wdp[par], f"wdp:{par}")
                    for o in range(NCH):
                        bank = wd_banks[o]
                        colo = (o % 2) * 256
                        first = (c == 0 and colo == 0)
                        last = (c == NCH - 1 and colo == 256)
                        mm3(bank, colo, colo + 256,
                            wdp[par][:, o * 128:(o + 1) * 128],
                            wdp[par][:, 1024 + o * 128: 1024 + o * 128 + 128],
                            P1h[:, c * TB:(c + 1) * TB], P1l[:, c * TB:(c + 1) * TB],
                            start=first, stop=last,
                            reads=(f"wdp:{par}", f"P1:{c}h", f"P1:{c}l"))
                for o in range(NCH):
                    bank = wd_banks[o]
                    colo = (o % 2) * 256
                    bd.dve(lambda e, o=o, bank=bank, colo=colo, T0=T0: e.scalar_tensor_tensor(
                        X[:, o * TOWN + T0: o * TOWN + T0 + TB],
                        pb[bank][:, colo:colo + 256], vcol("b_d", o),
                        X[:, o * TOWN + T0: o * TOWN + T0 + TB], ALU.add, ALU.add),
                        reads=(f"pb{bank}", "vec"), writes=(f"X:{o}",))

            if "Y" in tap_aps:
                for c in range(NCH):
                    bd.pool(lambda e, c=c: e.dma_start(
                        out=tap_aps["Y"][c * 128:(c + 1) * 128, :],
                        in_=X[:, c * TOWN:(c + 1) * TOWN]),
                        reads=(f"X:{c}",), dma="dout")

            bd.add_floor(
                [f"wq:{c}" for c in range(NCH)] + [f"wo:{c}" for c in range(NCH)]
                + [f"B1:{c}" for c in range(NCH)] + [f"B2:{c}" for c in range(NCH)]
                + [f"B3:{c}" for c in range(NCH)] + [f"B4:{c}" for c in range(NCH)]
                + [f"P1:{c}{s}" for c in range(NCH) for s in ("h", "l")]
                + [f"g1:{p}" for p in range(2)] + [f"f2:{p}" for p in range(2)]
                + ["wdp:0", "wdp:1", "wop:0", "wop:1"]
                + [f"gc:{p}{s}" for p in range(2) for s in ("h", "l")]
                + [f"qt:{h}{x}" for h in range(HEADS) for x in "abcd"])

        esd = ExitStack()
        with esd:
            h3h = sbo(esd, "h3h", 128, NCH * 512, BF16)
            h3l = sbo(esd, "h3l", 128, NCH * 512, BF16)
            routed = sbo(esd, "routed", 128, NCH * 512)
            comb = sbo(esd, "comb", 128, 32)
            prb = sbo(esd, "prb", 128, 8)
            mtmp = sbo(esd, "mtmp", 128, 8)
            rtmp = sbo(esd, "rtmp", 128, 8)
            routerh = sbo(esd, "routerh", 128, 64, BF16)
            routerl = sbo(esd, "routerl", 128, 64, BF16)
            sgwh = sbo(esd, "sgwh", 128, 8, BF16)
            sgwl = sbo(esd, "sgwl", 128, 8, BF16)
            ce_row = sbo(esd, "ce_row", 1, 512)
            s2row = ce_row
            s2b = sbo(esd, "s2b", 128, 512)
            ftmpD = sbo(esd, "ftmpD", 128, 512)
            fin = ftmpD
            heh = sbo(esd, "heh", 128, NCH * 512, BF16)
            hel = sbo(esd, "hel", 128, NCH * 512, BF16)
            w1p = [sbo(esd, f"w1p{i}", 128, 2048, BF16) for i in range(2)]
            w2p = [sbo(esd, f"w2p{i}", 128, 2048, BF16) for i in range(2)]
            sg = sbo(esd, "sg", 128, NCH * 512)
            mdh = sbo(esd, "mdh", 128, NCH * 512, BF16)
            mdl = sbo(esd, "mdl", 128, NCH * 512, BF16)
            gup = [sbo(esd, f"gup{i}", 128, 2048, BF16) for i in range(2)]
            split(router_sb[:], routerh[:], routerl[:], ("router_sb",), "router")
            split(vec[:, VI["sgw"]:VI["sgw"] + 8], sgwh[:], sgwl[:], ("vec",), "sgw")

            for hf in range(4):
                H0 = hf * 512
                bd.add_floor([f"routed:{o}" for o in range(NCH)]
                             + [f"xsqD:{c}" for c in range(NCH)]
                             + [f"d3:{c}" for c in range(NCH)])
                for tt in range(2):
                    T0 = H0 + tt * 256

                    def ysrc(c, T0=T0):
                        return X[:, c * TOWN + T0: c * TOWN + T0 + 256]

                    ykeys = tuple(f"X:{c}" for c in range(NCH))
                    sb_bank = 0 if tt == 0 else 2
                    bc_bank = 1 if tt == 0 else 3
                    soff = tt * 256
                    ln_stats(ysrc, ykeys, routed, "xsqD", 256, want_r0=True,
                             bank=sb_bank, soff=soff)
                    bcast(bc_bank, 0, 256, stat0[0:1, MU + soff:MU + soff + 256],
                          "s_mu" + str(soff), True, False)
                    bcast(bc_bank, 256, 256, stat0[0:1, R0 + soff:R0 + soff + 256],
                          "s_r0" + str(soff), False, True)
                    for c in range(NCH):
                        bd.dve(lambda e, c=c, tt=tt, ysrc=ysrc, bc_bank=bc_bank: e.tensor_sub(
                            routed[:, 2048 + c * 256: 2048 + c * 256 + 256],
                            ysrc(c), pb[bc_bank][:, 0:256]),
                            reads=(ykeys[c], f"pb{bc_bank}"), writes=(f"d3:{c}",))
                    for c in range(NCH):
                        bd.dve(lambda e, c=c, bc_bank=bc_bank: e.tensor_mul(
                            routed[:, 2048 + c * 256: 2048 + c * 256 + 256],
                            routed[:, 2048 + c * 256: 2048 + c * 256 + 256],
                            pb[bc_bank][:, 256:512]),
                            reads=(f"pb{bc_bank}",), writes=(f"d3:{c}",))
                    for c in range(NCH):
                        split(routed[:, 2048 + c * 256: 2048 + c * 256 + 256],
                              h3h[:, c * 512 + tt * 256: c * 512 + tt * 256 + 256],
                              h3l[:, c * 512 + tt * 256: c * 512 + tt * 256 + 256],
                              (f"d3:{c}",), f"h3:{c}")
                for sub in range(4):
                    rbank = 2 if sub % 2 == 0 else 0
                    for c in range(NCH):
                        mm3(rbank, 0, 8,
                            h3h[:, c * 512 + sub * 128: c * 512 + sub * 128 + 128],
                            h3l[:, c * 512 + sub * 128: c * 512 + sub * 128 + 128],
                            routerh[:, c * 8:(c + 1) * 8], routerl[:, c * 8:(c + 1) * 8],
                            start=(c == 0), stop=(c == NCH - 1),
                            reads=(f"h3:{c}h", f"h3:{c}l", "routerh", "routerl"))
                    bd.dve(lambda e, rbank=rbank: e.reduce_max(
                        rtmp[:, 0:1], pb[rbank][:, 0:8], axis=AX.X),
                        reads=(f"pb{rbank}",), writes=("rt0",))
                    bd.dve(lambda e: e.tensor_scalar_mul(rtmp[:, 1:2], rtmp[:, 0:1], -1.0),
                           reads=("rt0",), writes=("rt1",))
                    bd.act(lambda e, rbank=rbank: e.activation(
                        out=prb[:], in_=pb[rbank][:, 0:8],
                        func=AF.Exp, bias=rtmp[:, 1:2]),
                        reads=(f"pb{rbank}", "rt1"), writes=("prb",))
                    bd.dve(lambda e: e.reduce_sum(rtmp[:, 2:3], prb[:], axis=AX.X),
                           reads=("prb",), writes=("rt2",))
                    bd.dve(lambda e: e.reciprocal(rtmp[:, 3:4], rtmp[:, 2:3]),
                           reads=("rt2",), writes=("rt3",))
                    bd.dve(lambda e: e.tensor_scalar_mul(prb[:], prb[:], rtmp[:, 3:4]),
                           reads=("rt3",), writes=("prb",))
                    bd.dve(lambda e: e.reduce_max(rtmp[:, 4:5], prb[:], axis=AX.X),
                           reads=("prb",), writes=("rt4",))
                    bd.dve(lambda e: e.tensor_single_scalar(
                        mtmp[:], prb[:], rtmp[:, 4:5], ALU.is_ge),
                        reads=("prb", "rt4"), writes=("mtmp",))
                    bd.dve(lambda e: e.scalar_tensor_tensor(
                        mtmp[:], mtmp[:], -2.0, prb[:], ALU.mult, ALU.add),
                        reads=("prb",), writes=("mtmp",))
                    bd.dve(lambda e: e.reduce_max(rtmp[:, 5:6], mtmp[:], axis=AX.X),
                           reads=("mtmp",), writes=("rt5",))
                    bd.dve(lambda e, sub=sub: e.tensor_single_scalar(
                        comb[:, sub * 8:(sub + 1) * 8], prb[:], rtmp[:, 5:6], ALU.is_ge),
                        reads=("prb", "rt5"), writes=(f"comb:{sub}",))
                    bd.dve(lambda e, sub=sub: e.tensor_mul(
                        comb[:, sub * 8:(sub + 1) * 8], comb[:, sub * 8:(sub + 1) * 8],
                        prb[:]),
                        reads=("prb",), writes=(f"comb:{sub}",))
                xsq_drain = bd.drain([f"xsqD:{c}" for c in range(NCH)]
                                     + [f"d3:{c}" for c in range(NCH)])
                if True:
                    for en in range(NE):
                        for sub in range(4):
                            colo = sub * 128
                            bd.pe(lambda e, en=en, sub=sub, colo=colo: e.matmul(
                                pb[3][0:1, colo:colo + 128],
                                comb[:, sub * 8 + en: sub * 8 + en + 1], ident[:],
                                is_transpose=True,
                                start=(colo == 0), stop=(colo == 384)),
                                reads=(f"comb:{sub}", "ident"),
                                writes=(("pb3",) if colo == 0 else ()),
                                writes_ext=(() if colo == 0 else ("pb3",)))
                        bd.act(lambda e: e.activation(
                            out=ce_row[0:1, 0:512], in_=pb[3][0:1, 0:512], func=AF.Copy),
                            reads=("pb3",), writes=("ce_row",))
                        bd.pe(lambda e: e.matmul(
                            pb[6][:, 0:512], onesrow[:], ce_row[0:1, 0:512],
                            start=True, stop=True),
                            reads=("onesrow", "ce_row"), writes=("pb6",))
                        for o1 in range(NCH):
                            par = o1 % 2
                            load_pair(f"dw1{par}", f"w1:{en}:{o1}", w1p[par], f"w1p:{par}")
                            for c in range(NCH):
                                mm3(4 + par, 0, 512,
                                    w1p[par][:, c * 128:(c + 1) * 128],
                                    w1p[par][:, 1024 + c * 128: 1024 + c * 128 + 128],
                                    h3h[:, c * 512:(c + 1) * 512],
                                    h3l[:, c * 512:(c + 1) * 512],
                                    start=(c == 0), stop=(c == NCH - 1),
                                    reads=(f"w1p:{par}", f"h3:{c}h", f"h3:{c}l"))
                            bd.act(lambda e, par=par: e.activation(
                                out=ftmpD[:], in_=pb[4 + par][:, 0:512], func=AF.Silu),
                                reads=(f"pb{4 + par}",), writes=("ftmpD",))
                            bd.dve(lambda e: e.tensor_mul(ftmpD[:], ftmpD[:],
                                                          pb[6][:, 0:512]),
                                   reads=("pb6",), writes=("ftmpD",))
                            split(ftmpD[:], heh[:, o1 * 512:(o1 + 1) * 512],
                                  hel[:, o1 * 512:(o1 + 1) * 512], ("ftmpD",), f"he:{o1}")
                        for o2 in range(NCH):
                            par = o2 % 2
                            load_pair(f"dw2{par}", f"w2:{en}:{o2}", w2p[par], f"w2p:{par}")
                            for c in range(NCH):
                                mm3(2 + par, 0, 512,
                                    w2p[par][:, c * 128:(c + 1) * 128],
                                    w2p[par][:, 1024 + c * 128: 1024 + c * 128 + 128],
                                    heh[:, c * 512:(c + 1) * 512],
                                    hel[:, c * 512:(c + 1) * 512],
                                    start=(c == 0), stop=(c == NCH - 1),
                                    reads=(f"w2p:{par}", f"he:{c}h", f"he:{c}l"))
                            if en == 0:
                                bd.dve(lambda e, o2=o2, par=par: e.tensor_copy(
                                    routed[:, o2 * 512:(o2 + 1) * 512],
                                    pb[2 + par][:, 0:512]),
                                    deps=xsq_drain,
                                    reads=(f"pb{2 + par}",), writes=(f"routed:{o2}",))
                            else:
                                bd.dve(lambda e, o2=o2, par=par: e.tensor_add(
                                    routed[:, o2 * 512:(o2 + 1) * 512],
                                    routed[:, o2 * 512:(o2 + 1) * 512],
                                    pb[2 + par][:, 0:512]),
                                    reads=(f"pb{2 + par}",), writes=(f"routed:{o2}",))
                if True:
                    for c in range(NCH):
                        par = c % 2
                        load_pair(f"dgu{par}", f"sgug:{c}", gup[par], f"gup:{par}")
                        for og in range(NCH):
                            mm3(og, 0, 512,
                                gup[par][:, og * 128:(og + 1) * 128],
                                gup[par][:, 1024 + og * 128: 1024 + og * 128 + 128],
                                h3h[:, c * 512:(c + 1) * 512],
                                h3l[:, c * 512:(c + 1) * 512],
                                start=(c == 0), stop=(c == NCH - 1),
                                reads=(f"gup:{par}", f"h3:{c}h", f"h3:{c}l"))
                    for og in range(NCH):
                        bd.act(lambda e, og=og: e.activation(
                            out=sg[:, og * 512:(og + 1) * 512], in_=pb[og][:, 0:512],
                            func=AF.Silu),
                            reads=(f"pb{og}",), writes=(f"sg:{og}",))
                    for c in range(NCH):
                        par = c % 2
                        load_pair(f"dgu{par}", f"sguu:{c}", gup[par], f"gup:{par}")
                        for og in range(NCH):
                            mm3(og, 0, 512,
                                gup[par][:, og * 128:(og + 1) * 128],
                                gup[par][:, 1024 + og * 128: 1024 + og * 128 + 128],
                                h3h[:, c * 512:(c + 1) * 512],
                                h3l[:, c * 512:(c + 1) * 512],
                                start=(c == 0), stop=(c == NCH - 1),
                                reads=(f"gup:{par}", f"h3:{c}h", f"h3:{c}l"))
                    for og in range(NCH):
                        bd.dve(lambda e, og=og: e.tensor_mul(
                            sg[:, og * 512:(og + 1) * 512], sg[:, og * 512:(og + 1) * 512],
                            pb[og][:, 0:512]),
                            reads=(f"pb{og}",), writes=(f"sg:{og}",))
                    for og in range(NCH):
                        split(sg[:, og * 512:(og + 1) * 512],
                              mdh[:, og * 512:(og + 1) * 512],
                              mdl[:, og * 512:(og + 1) * 512], (f"sg:{og}",), f"md:{og}")
                    for c in range(NCH):
                        mm3(0, 0, 512,
                            sgwh[:, c:c + 1], sgwl[:, c:c + 1],
                            h3h[:, c * 512:(c + 1) * 512], h3l[:, c * 512:(c + 1) * 512],
                            start=(c == 0), stop=(c == NCH - 1),
                            reads=("sgwh", "sgwl", f"h3:{c}h", f"h3:{c}l"), prow=1)
                    bd.act(lambda e: e.activation(out=s2row[0:1, 0:512],
                                                  in_=pb[0][0:1, 0:512], func=AF.Sigmoid),
                           reads=("pb0",), writes=("ce_row",))
                    bd.pe(lambda e: e.matmul(pb[1][:, 0:512], onesrow[:],
                                             s2row[0:1, 0:512], start=True, stop=True),
                          reads=("onesrow", "ce_row"), writes=("pb1",))
                    bd.dve(lambda e: e.tensor_copy(s2b[:], pb[1][:, 0:512]),
                           reads=("pb1",), writes=("s2b",))
                    for og in range(NCH):
                        par = og % 2
                        load_pair(f"dsd{par}", f"sdown:{og}", gup[par], f"gup:{par}")
                        for o in range(NCH):
                            mm3(o, 0, 512,
                                gup[par][:, o * 128:(o + 1) * 128],
                                gup[par][:, 1024 + o * 128: 1024 + o * 128 + 128],
                                mdh[:, og * 512:(og + 1) * 512],
                                mdl[:, og * 512:(og + 1) * 512],
                                start=(og == 0), stop=(og == NCH - 1),
                                reads=(f"gup:{par}", f"md:{og}h", f"md:{og}l"))
                    for o in range(NCH):
                        bd.dve(lambda e, o=o: e.tensor_mul(
                            fin[:], pb[o][:, 0:512], s2b[:]),
                            reads=(f"pb{o}", "s2b"), writes=("ftmpD",))
                        bd.dve(lambda e, o=o: e.tensor_add(
                            fin[:], fin[:], routed[:, o * 512:(o + 1) * 512]),
                            reads=(f"routed:{o}",), writes=("ftmpD",))
                        bd.dve(lambda e, o=o, H0=H0: e.tensor_add(
                            fin[:], fin[:], X[:, o * TOWN + H0: o * TOWN + H0 + 512]),
                            reads=(f"X:{o}",), writes=("ftmpD",))
                        bd.pool(lambda e, o=o, H0=H0: e.dma_start(
                            out=out_h[o * 128:(o + 1) * 128, H0: H0 + 512],
                            in_=fin[:]),
                            reads=("ftmpD",), dma="dout")

        bd.op("pool", lambda e: e.wait_ge(sems["dout"], bd.dma_count.get("dout", 0)))

        bd.finalize()
        with nc.Block() as blk:
            @blk.tensor
            def _(eng):
                bd.replay("pe", eng, sems)

            @blk.vector
            def _(eng):
                bd.replay("dve", eng, sems)

            @blk.scalar
            def _(eng):
                bd.replay("act", eng, sems)

            @blk.gpsimd
            def _(eng):
                bd.replay("pool", eng, sems)

            @blk.sync
            def _(eng):
                bd.replay("sp", eng, sems)

    return nc


def kernel(**inputs):
    x = np.asarray(inputs["L0_attn_x"], dtype=np.float32)
    blob = make_wblob(inputs)
    nc = build_program()
    in_maps = []
    for core in range(8):
        b, half = core // 2, core % 2
        xt = x[b].T
        own = xt[:, half * TOWN:(half + 1) * TOWN]
        part = xt[:, (1 - half) * TOWN:(2 - half) * TOWN]
        pp = part.reshape(8, 128, 16, 128).transpose(2, 1, 0, 3)
        xb = np.concatenate([np.ascontiguousarray(own).ravel(),
                             np.ascontiguousarray(pp).ravel()])
        in_maps.append({"wblob": blob, "xb": xb})
    res = run_bass_kernel_spmd(nc, in_maps, core_ids=list(range(8)))
    out = np.empty((4, 2 * TOWN, D), dtype=np.float32)
    for core in range(8):
        b, half = core // 2, core % 2
        out[b, half * TOWN:(half + 1) * TOWN, :] = res.results[core]["out"].T
    return out
